# version 04
# baseline (speedup 1.0000x reference)
import sys
import numpy as np

sys.path.insert(0, "/opt/trn_rl_repo")
import concourse.bass as bass
import concourse.mybir as mybir
from concourse.bass_utils import run_bass_kernel_spmd

F32 = mybir.dt.float32
BF16 = mybir.dt.bfloat16
AF = mybir.ActivationFunctionType
ALU = mybir.AluOpType
AX = mybir.AxisListType

N_CORES = 8


class Sched:
    STREAMS = ("T", "V", "A", "G", "S")
    NROT_BY = {"S": 3, "A": 3, "G": 2, "T": 1, "V": 1}

    def __init__(self):
        self.ops = {s: [] for s in self.STREAMS}
        self.cnt = {}
        self.known = {s: {} for s in self.STREAMS}
        self.dma_idx = {s: 0 for s in self.STREAMS}
        for s in self.STREAMS:
            self.cnt["c" + s] = 0

    @staticmethod
    def _flat(deps, out):
        for t in deps:
            if t is None:
                continue
            if isinstance(t, list):
                Sched._flat(t, out)
            else:
                out.append(t)

    def op(self, stream, fn, deps=(), inc=False, dma=False):
        waits = []
        k = self.known[stream]
        tok = None
        incspec = None
        if stream in ("V", "A", "G") and not dma:
            sem = "c" + stream
            prev = self.cnt[sem]
            if prev > 0:
                waits.append((sem, prev))
            inc = True
        if dma:
            i = self.dma_idx[stream]
            self.dma_idx[stream] += 1
            nrot = self.NROT_BY[stream]
            sem = "d%s%d" % (stream, i % nrot)
            base = 16 * (i // nrot)
            if base > 0 and k.get(sem, 0) < base:
                k[sem] = base
                waits.append((sem, base))
            tok = (sem, base + 16)
            incspec = (sem, 16)
        elif inc:
            sem = "c" + stream
            self.cnt[sem] += 1
            tok = (sem, self.cnt[sem])
            incspec = (sem, 1)
        flat = []
        self._flat(deps, flat)
        best = {}
        for sem, val in flat:
            if sem == "c" + stream and not dma:
                continue
            if val > best.get(sem, -1):
                best[sem] = val
        for sem in sorted(best):
            val = best[sem]
            if k.get(sem, 0) >= val:
                continue
            k[sem] = val
            waits.append((sem, val))
        self.ops[stream].append((waits, fn, incspec))
        return tok

    def build(self, nc):
        from contextlib import ExitStack
        es = ExitStack()
        sems = {}
        for s in self.STREAMS:
            sems["c" + s] = es.enter_context(nc.semaphore("sem_c" + s))
            for r in range(self.NROT_BY[s]):
                sems["d%s%d" % (s, r)] = es.enter_context(
                    nc.semaphore("sem_d%s%d" % (s, r)))
        engmap = {"T": "tensor", "V": "vector", "A": "scalar", "G": "gpsimd", "S": "sync"}
        block = es.enter_context(nc.Block())
        for s in self.STREAMS:
            def body(e, _s=s):
                for waits, fn, incspec in self.ops[_s]:
                    for sem, val in waits:
                        e.wait_ge(sems[sem], val)
                    inst = fn(e)
                    if incspec is not None:
                        inst.then_inc(sems[incspec[0]], incspec[1])
            getattr(block, engmap[s])(body)
        es.close()


class Cfg:
    def __init__(self, S=2048, NSEQ=2, HFF=4096):
        self.S, self.NSEQ, self.HFF = S, NSEQ, HFF
        self.D = 1024
        self.NH = 16
        self.T = S * NSEQ
        self.HD = 64
        self.NC = self.T // 128
        self.CPS = S // 128
        self.DC = 8
        self.HC = HFF // 128
        self.GT = 512
        self.CPG = 4
        self.NG = self.T // 512
        self.GPS = S // 512
        assert S % 512 == 0 and HFF % 1024 == 0 and self.NG >= 1


class Carver:
    def __init__(self, ap, size):
        self.ap, self.size, self.off = ap, size, 0

    def take(self, els):
        assert self.off + els <= self.size, (self.off, els, self.size)
        s = self.ap[:, self.off:self.off + els]
        self.off += els
        return s

    def row(self, els):
        assert self.off + els <= self.size
        s = self.ap[0:1, self.off:self.off + els]
        self.off += els
        return s


def build_program(cfg: Cfg, upto=None):
    nc = bass.Bass("TRN2", target_bir_lowering=False)
    S, D, NH, HFF, T = cfg.S, cfg.D, cfg.NH, cfg.HFF, cfg.T
    NC, CPS, DC, HC = cfg.NC, cfg.CPS, cfg.DC, cfg.HC
    GT, CPG, NG, GPS = cfg.GT, cfg.CPG, cfg.NG, cfg.GPS
    NPAIR = NH // 2
    RT = HFF // 128
    sch = Sched()

    _fin = {}
    def finish(deps):
        sch.op("S", lambda e: e.nop(), deps=deps)
        sch.build(nc)
        _fin["done"] = True

    def din(name, shape, dt=F32):
        return nc.dram_tensor(name, list(shape), dt, kind="ExternalInput").ap()

    x_h = din("x", (T, D))
    W_attn = {w: din(w, (D, D)) for w in ("Wq", "Wk", "Wv", "Wd")}
    b_attn = {b: din(b, (1, D)) for b in ("bq", "bk", "bv", "bd")}
    gam_h = din("gamma", (1, D)); bet_h = din("beta", (1, D))
    Wfc_h = din("Wfc", (HFF, D))
    lnw_h = din("lnw", (1, HFF)); lnb_h = din("lnb", (1, HFF))
    Wpr_h = din("Wproj", (D, HFF))
    out_h = nc.dram_tensor("out", [T, D], F32, kind="ExternalOutput").ap()
    xbf_h = nc.dram_tensor("xbf_s", [T, D], BF16, kind="Internal").ap()
    import os as _os
    x1_h = nc.dram_tensor("x1_s", [T, D], BF16,
                          kind="ExternalOutput" if _os.environ.get("KDBG_X1") else "Internal").ap()
    KDUMP = bool(_os.environ.get("KDBG_DUMPS"))
    if KDUMP:
        dkv_h = nc.dram_tensor("d_kv", [128, cfg.NSEQ * 8 * 130], BF16, kind="ExternalOutput").ap()
        dqe_h = nc.dram_tensor("d_qe", [128, 8 * 512], BF16, kind="ExternalOutput").ap()
        dkb_h = nc.dram_tensor("d_kb", [128, 2 * D], BF16, kind="ExternalOutput").ap()
        dvg_h = nc.dram_tensor("d_vg", [128, 2 * NH * 65], BF16, kind="ExternalOutput").ap()
        dcx_h = nc.dram_tensor("d_cx", [128, D], BF16, kind="ExternalOutput").ap()
        dxf_h = nc.dram_tensor("d_xf", [128, 8 * 512], BF16, kind="ExternalOutput").ap()
        dwk_h = nc.dram_tensor("d_wk", [128, DC * D], BF16, kind="ExternalOutput").ap()
        dqh_h = nc.dram_tensor("d_qh", [HFF, D], BF16, kind="ExternalOutput").ap()
        dqp_h = nc.dram_tensor("d_qp", [D, HFF], BF16, kind="ExternalOutput").ap()
        dwf_h = nc.dram_tensor("d_wf", [128, DC * HFF], BF16, kind="ExternalOutput").ap()
        dhh_h = nc.dram_tensor("d_hh", [128, HC * T], BF16, kind="ExternalOutput").ap()
    wfcq_h = nc.dram_tensor("wfcq_s", [HFF, D], BF16, kind="Internal").ap()
    wprq_h = nc.dram_tensor("wprq_s", [D, HFF], BF16, kind="Internal").ap()
    wbf_h = {w: nc.dram_tensor(w + "bf_s", [D, D], BF16, kind="Internal").ap()
             for w in ("Wq", "Wk", "Wv", "Wd")}
    h_sh = nc.dram_tensor("h_s", [128, HC, T], BF16, kind="Internal").ap()
    xfm_h2 = nc.dram_tensor("xfm_s", [128, (T // 512) * 4096], BF16, kind="Internal").ap()

    from contextlib import ExitStack
    es = ExitStack()

    def sb(name, p, f, dt):
        return es.enter_context(nc.sbuf_tensor(name, [p, f], dt))

    def ps(name, f):
        return es.enter_context(nc.psum_tensor(name, [128, f], F32))

    ones_col = sb("ones_col", 128, 1, BF16)
    ones_row = sb("ones_row", 1, 512, BF16)
    onesf_row = sb("onesf_row", 1, 128, F32)
    onesf_col = sb("onesf_col", 128, 1, F32)
    onef = sb("onef", 1, 1, F32)
    brows = sb("brows", 1, 4 * D, BF16)
    gb_bc = sb("gb_bc", 128, 2 * D, BF16)
    c0_bc = sb("c0_bc", 128, D, BF16)
    s2_row = sb("s2_row", 1, D, BF16)
    lnw_cols = sb("lnw_cols", 128, HC, F32)
    lnb_cols = sb("lnb_cols", 128, HC, BF16)
    rff_cols = sb("rff_cols", 128, NC, F32)
    thr_cols = sb("thr_cols", 128, 4, F32)
    accq = sb("accq", 128, 2, F32)
    redq = sb("redq", 128, RT + DC, F32)
    smallf = sb("smallf", 1, 16, F32)
    mv = sb("mv", 128, 8, F32)
    stat = sb("stat", 128, 24, F32)
    rden = sb("rden", 128, 2 * NH, F32)
    epsc = sb("epsc", 128, 2, F32)
    idmat = sb("idmat", 128, 128, BF16)
    idr = sb("idr", 1, 128, F32)
    idc = sb("idc", 128, 1, F32)
    wstg = sb("wstg", 128, 4096, BF16)

    B1 = 8192 + 4096 + 2048 + 2 * NH * 65 + 2048 + 2048 + 2048 + 2048 + cfg.NSEQ * NPAIR * 130 + 2048
    B2 = T + max(2 * DC * GT + HC * GT + 1024, 2 * HC * 256 + 2 * D)
    AR_B = sb("AR_B", 128, max(B1, B2) + 64, BF16)
    cb = Carver(AR_B, max(B1, B2) + 64)
    xfm = cb.take(2 * DC * GT)
    Qe = cb.take(DC * GT)
    kbf = cb.take(2 * D)
    vaug = cb.take(2 * NH * 65)
    ctxbf = cb.take(2 * D)
    ctxfm = cb.take(2 * D)
    wq_bf = cb.take(2 * D)
    wtile = cb.take(2 * 1024)
    kvaug = cb.take(cfg.NSEQ * NPAIR * 130)
    x1t = cb.take(2 * D)
    cb2 = Carver(AR_B, max(B1, B2) + 64)
    negmu_r = cb2.row(T)
    x1fm = cb2.take(2 * DC * GT)
    hgrp = cb2.take(HC * GT)
    hqt = cb2.take(2 * 512)
    cb3 = Carver(AR_B, max(B1, B2) + 64)
    cb3.off = T
    hld = cb3.take(2 * HC * 256)
    x1t_b = cb3.take(2 * D)

    F1 = 1024 + 2048 + 2048 + 1024 + 1024 + 1024 + 1024 + 2048 + 1024 + 1024 + NPAIR * 65
    F2 = T + T + 1024 + 1024 + 2048 + 1024
    AR_F = sb("AR_F", 128, max(F1, F2, 2 * HFF + 2048 + 2 * D) + 32, F32)
    cf = Carver(AR_F, max(F1, F2, 2 * HFF + 2048 + 2 * D) + 32)
    eluu = cf.take(1024)
    elue = cf.take(2048)
    xf32 = cf.take(2 * D)
    xbeta = cf.take(D)
    v0t = cf.take(D)
    nt = cf.take(D)
    v1t = cf.take(D)
    wnat = cf.take(2048)
    qs1 = cf.take(1024)
    qs2 = cf.take(1024)
    raug_f = cf.take(NPAIR * 65)
    lnwrow = AR_F[0:1, 0:HFF]
    lnbrow = AR_F[0:1, HFF:2 * HFF]
    gbrow = AR_F[0:1, 2 * HFF:2 * HFF + 2 * D]
    cf2 = Carver(AR_F, max(F1, F2, 2 * HFF + 2048 + 2 * D) + 32)
    sumh_r = cf2.row(T)
    sumq_r = cf2.row(T)
    x1c = cf2.take(D)
    v2t = cf2.take(D)
    outt = cf2.take(2 * D)
    c0row = cf2.row(D)


    AR_W = sb("AR_W", 128, DC * max(HFF, 4 * D), BF16)
    WkT = AR_W[:, 0 * DC * D:1 * DC * D]
    WvT = AR_W[:, 1 * DC * D:2 * DC * D]
    WqT = AR_W[:, 2 * DC * D:3 * DC * D]
    WdT = AR_W[:, 3 * DC * D:4 * DC * D]
    WfcT = AR_W[:, 0:DC * HFF]
    wT_map = {"Wk": WkT, "Wv": WvT, "Wq": WqT, "Wd": WdT}

    pA = ps("pA", 512)
    pB = ps("pB", 512)
    pC = ps("pC", 512)
    pD = ps("pD", 512)
    pR1 = ps("pR1", 512)
    pR2 = ps("pR2", 512)
    pS1 = ps("pS1", 512)
    pS2 = ps("pS2", 512)

    def t_ms(fn, deps=()):
        return sch.op("T", fn, deps, inc=True)

    def chain_mm(mms, deps=()):
        tok = None
        for i, f in enumerate(mms):
            last = i == len(mms) - 1
            if i == 0:
                tok = sch.op("T", f, deps, inc=last)
            else:
                tok = sch.op("T", f, inc=last)
        return tok

    v_ones = sch.op("V", lambda e: e.memset(ones_col[:, :], 1.0), inc=True)
    sch.op("V", lambda e: e.memset(ones_row[:, :], 1.0))
    sch.op("V", lambda e: e.memset(onesf_row[:, :], 1.0))
    sch.op("V", lambda e: e.memset(onesf_col[:, :], 1.0))
    sch.op("V", lambda e: e.memset(onef[:, :], 1.0))
    sch.op("V", lambda e: e.memset(kvaug[:, :], 0.0))
    sch.op("V", lambda e: e.memset(vaug[:, :], 1.0))
    sch.op("V", lambda e: e.memset(epsc[:, 0:1], 1e-12))
    v_eps = sch.op("V", lambda e: e.memset(epsc[:, 1:2], 1e-5), inc=True)
    v_init = sch.op("V", lambda e: e.memset(accq[:, :], 0.0), inc=True)

    d_b = {}
    for i, b in enumerate(("bq", "bk", "bv", "bd")):
        d_b[b] = sch.op("G", lambda e, i=i, b=b: e.dma_start(
            brows[0:1, D * i:D * (i + 1)], b_attn[b][:, :]), dma=True)
    d_gam = sch.op("S", lambda e: e.dma_start(gbrow[0:1, 0:D], gam_h[:, :]), dma=True)
    d_bet = sch.op("S", lambda e: e.dma_start(gbrow[0:1, D:2 * D], bet_h[:, :]), dma=True)
    d_lnw = sch.op("S", lambda e: e.dma_start(lnwrow, lnw_h[:, :]), dma=True)
    d_lnb = sch.op("S", lambda e: e.dma_start(lnbrow, lnb_h[:, :]), dma=True)

    gb_last = None
    for j in range(2 * D // 512):
        deps = [d_gam, d_bet, v_init] + ([gb_last] if gb_last else [])
        tk = t_ms(lambda e, j=j: e.matmul(pS1[:, 0:512], onesf_row[0:1, :],
                                          gbrow[0:1, 512 * j:512 * (j + 1)],
                                          start=True, stop=True), deps=deps)
        gb_last = sch.op("V", lambda e, j=j: e.tensor_copy(
            gb_bc[:, 512 * j:512 * (j + 1)], pS1[:, 0:512]), deps=[tk], inc=True)
    v_gb = gb_last
    v_sc = sch.op("V", lambda e: e.tensor_tensor_scan(
        idr[0:1, :], onesf_row[0:1, :], onesf_row[0:1, :], 1.0,
        ALU.add, ALU.mult), inc=True)
    t_idc = t_ms(lambda e: e.transpose(pS2[:, 2:3], idr[0:1, :], onef[0:1, 0:1]),
                 deps=[v_sc])
    v_idc = sch.op("V", lambda e: e.tensor_copy(idc[:, 0:1], pS2[:, 2:3]),
                   deps=[t_idc], inc=True)
    t_idb = t_ms(lambda e: e.matmul(pS1[:, 0:128], onesf_row[0:1, :], idr[0:1, :],
                                    start=True, stop=True), deps=[v_sc, v_gb])
    v_id = sch.op("V", lambda e: e.tensor_scalar(
        idmat[:, :], pS1[:, 0:128], idc[:, 0:1], None, ALU.is_equal),
        deps=[t_idb, v_idc], inc=True)

    col_last = None
    for k in range(HC):
        deps = [d_lnw, v_init] + ([col_last] if col_last else [])
        t1 = t_ms(lambda e, k=k: e.transpose(pS2[:, 0:1], lnwrow[0:1, 128 * k:128 * (k + 1)],
                                             onef[0:1, 0:1]), deps=deps)
        c1 = sch.op("V", lambda e, k=k: e.tensor_copy(lnw_cols[:, k:k + 1], pS2[:, 0:1]),
                    deps=[t1], inc=True)
        t2 = t_ms(lambda e, k=k: e.transpose(pS2[:, 1:2], lnbrow[0:1, 128 * k:128 * (k + 1)],
                                             onef[0:1, 0:1]), deps=[d_lnb, c1])
        col_last = sch.op("V", lambda e, k=k: e.tensor_copy(lnb_cols[:, k:k + 1], pS2[:, 1:2]),
                          deps=[t2], inc=True)
    v_cols = col_last

    d_xbf = []
    for g in range(NG):
        d_xbf.append(sch.op("G", lambda e, g=g: e.dma_start(
            xbf_h[GT * g:GT * (g + 1), :], x_h[GT * g:GT * (g + 1), :]), dma=True))

    wq_slot_free = [[], []]
    w_done = {}
    sbt_chain = [None]
    guard_pS = [None]


    w_staged = {}

    def emit_weight_oc(wname, oc, unit_idx):
        slot = unit_idx % 2
        d = sch.op("G", lambda e, wname=wname, oc=oc, slot=slot: e.dma_start(
            wq_bf[:, 1024 * slot:1024 * (slot + 1)],
            W_attn[wname][128 * oc:128 * (oc + 1), :]), dma=True,
            deps=wq_slot_free[slot])
        st = sch.op("G", lambda e, wname=wname, oc=oc, slot=slot: e.dma_start(
            wbf_h[wname][128 * oc:128 * (oc + 1), :],
            wq_bf[:, 1024 * slot:1024 * (slot + 1)]), dma=True, deps=[d])
        wq_slot_free[slot] = [st]
        w_staged.setdefault(wname, []).append(st)

    def emit_weight_loads(wname):
        WT = wT_map[wname]
        toks = []
        for k in range(DC):
            for oc in range(DC):
                q = "S"
                toks.append(sch.op(q, lambda e, WT=WT, k=k, oc=oc, wname=wname: e.dma_start(
                    WT[:, D * k + 128 * oc: D * k + 128 * oc + 128],
                    wbf_h[wname][128 * oc:128 * (oc + 1), 128 * k:128 * (k + 1)],
                    transpose=True), deps=w_staged[wname], dma=True))
        w_done[wname] = toks

    ui = 0
    for oc in range(DC):
        emit_weight_oc("Wk", oc, ui); ui += 1
        emit_weight_oc("Wv", oc, ui); ui += 1
    emit_weight_loads("Wk")
    emit_weight_loads("Wv")

    from collections import deque
    bg = deque()
    for oc in range(DC):
        bg.append(("wq", oc))
    bg.append(("wqload", 0))
    for oc in range(DC):
        bg.append(("wd", oc))
    bg.append(("wdload", 0))

    bgstate = dict(ui=ui, am_last=v_cols, qf_last=[[], []], qp_last=[[], []],
                   wfcq_done=None, wprq_done=None, am_fc_tok=None, am_pr_tok=None,
                   v_thr=None)

    def emit_absmean(i):
        slot = bgstate["ui"] % 2
        bgstate["ui"] += 1
        if i < RT:
            d = sch.op("G", lambda e, i=i, slot=slot: e.dma_start(
                wnat[:, 1024 * slot:1024 * (slot + 1)], Wfc_h[128 * i:128 * (i + 1), :]),
                dma=True, deps=[bgstate["am_last"]])
            v = sch.op("V", lambda e, i=i, slot=slot: e.reduce_sum(
                redq[:, i:i + 1], wnat[:, 1024 * slot:1024 * (slot + 1)], AX.X,
                apply_absolute_value=True), deps=[d], inc=True)
            bgstate["am_last"] = v
        else:
            ii = i - RT
            nps = HFF // 1024
            vs = []
            for pcs in range(nps):
                slot2 = bgstate["ui"] % 2
                bgstate["ui"] += 1
                d = sch.op("G", lambda e, ii=ii, pcs=pcs, slot2=slot2: e.dma_start(
                    wnat[:, 1024 * slot2:1024 * (slot2 + 1)],
                    Wpr_h[128 * ii:128 * (ii + 1), 1024 * pcs:1024 * (pcs + 1)]),
                    dma=True, deps=[bgstate["am_last"]])
                v = sch.op("V", lambda e, ii=ii, pcs=pcs, slot2=slot2: e.reduce_sum(
                    qs1[:, ii * nps + pcs:ii * nps + pcs + 1],
                    wnat[:, 1024 * slot2:1024 * (slot2 + 1)], AX.X,
                    apply_absolute_value=True), deps=[d], inc=True)
                bgstate["am_last"] = v
                vs.append(v)
            v = sch.op("V", lambda e, ii=ii, nps=nps: e.reduce_sum(
                redq[:, RT + ii:RT + ii + 1], qs1[:, ii * nps:(ii + 1) * nps], AX.X),
                deps=vs, inc=True)
            bgstate["am_last"] = v
        if i == RT - 1:
            bgstate["am_fc_tok"] = sch.op("V", lambda e: e.reduce_sum(
                accq[:, 0:1], redq[:, 0:RT], AX.X), deps=[bgstate["am_last"]], inc=True)
        if i == RT + DC - 1:
            bgstate["am_pr_tok"] = sch.op("V", lambda e: e.reduce_sum(
                accq[:, 1:2], redq[:, RT:RT + DC], AX.X), deps=[bgstate["am_last"]], inc=True)

    def emit_thr():
        tg1 = t_ms(lambda e: e.matmul(pS2[0:1, 4:5], accq[:, 0:1], onesf_col[:, 0:1],
                                      start=True, stop=True),
                   deps=[bgstate["am_fc_tok"], v_cols, guard_pS[0]])
        tg2 = t_ms(lambda e: e.matmul(pS2[0:1, 5:6], accq[:, 1:2], onesf_col[:, 0:1],
                                      start=True, stop=True), deps=[bgstate["am_pr_tok"]])
        v1 = sch.op("V", lambda e: e.tensor_scalar(
            smallf[0:1, 0:1], pS2[0:1, 4:5], 0.5 / (HFF * D), 0.5e-6, ALU.mult, ALU.add),
            deps=[tg1], inc=True)
        v2 = sch.op("V", lambda e: e.tensor_scalar(
            smallf[0:1, 1:2], pS2[0:1, 5:6], 0.5 / (HFF * D), 0.5e-6, ALU.mult, ALU.add),
            deps=[tg2], inc=True)
        tb = t_ms(lambda e: e.matmul(pS2[:, 8:10], onesf_row[0:1, :], smallf[0:1, 0:2],
                                     start=True, stop=True), deps=[v1, v2])
        v3 = sch.op("V", lambda e: e.tensor_copy(thr_cols[:, 0:2], pS2[:, 8:10]),
                    deps=[tb], inc=True)
        v4 = sch.op("V", lambda e: e.tensor_scalar(thr_cols[:, 2:4], thr_cols[:, 0:2],
                                                   -1.0, None, ALU.mult), inc=True)
        bgstate["v_thr"] = v4

    def emit_quant(kind, idx):
        slot = idx % 2
        if kind == "qf":
            W_src = Wfc_h[128 * idx:128 * (idx + 1), :]
            dst = wfcq_h[128 * idx:128 * (idx + 1), :]
            thr_i = 0
            lastlist = bgstate["qf_last"]
        else:
            ii, pcs = divmod(idx, HFF // 1024)
            W_src = Wpr_h[128 * ii:128 * (ii + 1), 1024 * pcs:1024 * (pcs + 1)]
            dst = wprq_h[128 * ii:128 * (ii + 1), 1024 * pcs:1024 * (pcs + 1)]
            thr_i = 1
            lastlist = bgstate["qp_last"]
        dep = [lastlist[slot], bgstate["am_last"]]
        d = sch.op("G", lambda e, W_src=W_src, slot=slot: e.dma_start(
            wnat[:, 1024 * slot:1024 * (slot + 1)], W_src), dma=True, deps=dep)
        bgstate.setdefault(kind + "_loads", []).append(d)
        sch.op("V", lambda e, slot=slot: e.tensor_scalar(
            qs1[:, 0:1024], wnat[:, 1024 * slot:1024 * (slot + 1)],
            thr_cols[:, thr_i:thr_i + 1], None, ALU.is_ge),
            deps=[d, bgstate["v_thr"]])
        sch.op("V", lambda e, slot=slot: e.tensor_scalar(
            qs2[:, 0:1024], wnat[:, 1024 * slot:1024 * (slot + 1)],
            thr_cols[:, 2 + thr_i:3 + thr_i], None, ALU.is_le))
        v3 = sch.op("V", lambda e, slot=slot: e.tensor_tensor(
            wq_bf[:, 1024 * slot:1024 * (slot + 1)], qs1[:, 0:1024], qs2[:, 0:1024],
            ALU.subtract), deps=[wq_slot_free[slot]], inc=True)
        st = sch.op("G", lambda e, dst=dst, slot=slot: e.dma_start(
            dst, wq_bf[:, 1024 * slot:1024 * (slot + 1)]), deps=[v3], dma=True)
        lastlist[slot] = [st]
        wq_slot_free[slot] = [st]
        bgstate.setdefault(kind + "_stores", []).append(st)
        return st

    nps_pr = HFF // 1024
    for i in range(RT + DC):
        bg.append(("am", i))
    bg.append(("thr", 0))
    for i in range(RT):
        bg.append(("qf", i))
    for i in range(DC * nps_pr):
        bg.append(("qp", i))

    def drain_bg(n):
        for _ in range(n):
            if not bg:
                return
            kind, i = bg.popleft()
            if kind == "wq":
                emit_weight_oc("Wq", i, bgstate["ui"]); bgstate["ui"] += 1
            elif kind == "wqload":
                emit_weight_loads("Wq")
            elif kind == "wd":
                emit_weight_oc("Wd", i, bgstate["ui"]); bgstate["ui"] += 1
            elif kind == "wdload":
                emit_weight_loads("Wd")
            elif kind == "am":
                emit_absmean(i)
            elif kind == "thr":
                emit_thr()
            elif kind == "qf":
                emit_quant("qf", i)
                if i == RT - 1:
                    bgstate["wfcq_done"] = bgstate["qf_stores"]
            elif kind == "qp":
                emit_quant("qp", i)
                if i == DC * nps_pr - 1:
                    bgstate["wprq_done"] = bgstate["qp_stores"]

    def drain_all_bg():
        while bg:
            drain_bg(1)

    def win_loads(dst, src_h, g, slot, deps):
        toks = []
        for tc in range(CPG):
            row0 = GT * g + 128 * tc
            for k in range(DC):
                q = "S"
                toks.append(sch.op(q, lambda e, dst=dst, src_h=src_h, row0=row0, k=k, tc=tc,
                                   slot=slot: e.dma_start(
                                       dst[:, slot * DC * GT + GT * k + 128 * tc:
                                           slot * DC * GT + GT * k + 128 * tc + 128],
                                       src_h[row0:row0 + 128, 128 * k:128 * (k + 1)],
                                       transpose=True), deps=deps, dma=True))
        return toks

    xfm_st = {}
    xfm_free = [None, None]
    k_ps_free = None
    v_ps_free = None
    elue_free = [None, None]
    kbf_free = [None, None]
    vaug_free = [None, None]
    kv_seq_tok = {}
    last_kv_mm = None
    kv_acc = None

    for g in range(NG):
        seq = g // GPS
        slot = g % 2
        wl = win_loads(xfm, xbf_h, g, slot,
                       [d_xbf[g], v_cols] + ([xfm_free[slot]] if xfm_free[slot] else []))
        st_xfm = sch.op("G", lambda e, g=g, slot=slot: e.dma_start(
            xfm_h2[:, 4096 * g:4096 * (g + 1)],
            xfm[:, slot * DC * GT:slot * DC * GT + 4096]), deps=wl, dma=True)
        xfm_st[g] = st_xfm
        wl = wl + [st_xfm]
        if KDUMP and g == 0:
            dxf_t = sch.op("G", lambda e: e.dma_start(dxf_h[:, :], xfm[:, 0:4096]),
                           deps=wl, dma=True)
            wl = wl + [dxf_t]
        xoff = slot * DC * GT
        for tc in range(CPG):
            c = g * CPG + tc
            cs = c % CPS
            pp = c % 2
            mms = []
            for k in range(DC):
                for half, bank in ((0, pA), (1, pB)):
                    def mk(e, k=k, bank=bank, half=half, tc=tc, xoff=xoff):
                        return e.matmul(bank[:, 0:512],
                                        xfm[:, xoff + GT * k + 128 * tc: xoff + GT * k + 128 * tc + 128],
                                        WkT[:, D * k + 512 * half: D * k + 512 * (half + 1)],
                                        start=(k == 0), stop=False)
                    mms.append(mk)
            for half, bank in ((0, pA), (1, pB)):
                def mkb(e, bank=bank, half=half):
                    return e.matmul(bank[:, 0:512], ones_row[0:1, 0:128],
                                    brows[0:1, D + 512 * half: D + 512 * (half + 1)],
                                    start=False, stop=True)
                mms.append(mkb)
            tok_kmm = chain_mm(mms, deps=[wl, w_done["Wk"],
                                          d_b["bk"], v_ones, k_ps_free])
            mms = []
            for k in range(DC):
                for half, bank in ((0, pC), (1, pD)):
                    def mv_(e, k=k, bank=bank, half=half, tc=tc, xoff=xoff):
                        return e.matmul(bank[:, 0:512],
                                        xfm[:, xoff + GT * k + 128 * tc: xoff + GT * k + 128 * tc + 128],
                                        WvT[:, D * k + 512 * half: D * k + 512 * (half + 1)],
                                        start=(k == 0), stop=False)
                    mms.append(mv_)
            for half, bank in ((0, pC), (1, pD)):
                def mvb(e, bank=bank, half=half):
                    return e.matmul(bank[:, 0:512], ones_row[0:1, 0:128],
                                    brows[0:1, 2 * D + 512 * half: 2 * D + 512 * (half + 1)],
                                    start=False, stop=True)
                mms.append(mvb)
            tok_vmm = chain_mm(mms, deps=[w_done["Wv"], d_b["bv"],
                                          v_ps_free])

            a_u1 = sch.op("A", lambda e: e.activation(
                eluu[:, 0:512], pA[:, 0:512], AF.Relu, scale=-1.0), deps=[tok_kmm])
            a_u2 = sch.op("A", lambda e: e.activation(
                eluu[:, 512:1024], pB[:, 0:512], AF.Relu, scale=-1.0), inc=True)
            sch.op("A", lambda e, pp=pp: e.activation(
                elue[:, D * pp:D * pp + 512], eluu[:, 0:512], AF.Exp, scale=-1.0),
                deps=[elue_free[pp]])
            a_e2 = sch.op("A", lambda e, pp=pp: e.activation(
                elue[:, D * pp + 512:D * pp + 1024], eluu[:, 512:1024], AF.Exp, scale=-1.0),
                inc=True)
            p1_last_a = a_e2
            sch.op("V", lambda e, pp=pp: e.scalar_tensor_tensor(
                kbf[:, D * pp:D * pp + 512], pA[:, 0:512], 0.0,
                elue[:, D * pp:D * pp + 512], ALU.max, ALU.add),
                deps=[a_e2, kbf_free[pp]])
            v_k2 = sch.op("V", lambda e, pp=pp: e.scalar_tensor_tensor(
                kbf[:, D * pp + 512:D * pp + 1024], pB[:, 0:512], 0.0,
                elue[:, D * pp + 512:D * pp + 1024], ALU.max, ALU.add), inc=True)
            k_ps_free = v_k2
            elue_free[pp] = v_k2
            def vev1(e, pp=pp):
                dst = vaug[:, pp * NH * 65:(pp + 1) * NH * 65].rearrange(
                    "p (h c) -> p h c", h=NH)[:, 0:8, 0:64]
                return e.tensor_copy(dst, pC[:, 0:512].rearrange("p (h c) -> p h c", h=8))
            sch.op("V", vev1, deps=[tok_vmm, vaug_free[pp]])
            def vev2(e, pp=pp):
                dst = vaug[:, pp * NH * 65:(pp + 1) * NH * 65].rearrange(
                    "p (h c) -> p h c", h=NH)[:, 8:16, 0:64]
                return e.tensor_copy(dst, pD[:, 0:512].rearrange("p (h c) -> p h c", h=8))
            v_v2 = sch.op("V", vev2, inc=True)
            v_ps_free = v_v2
            if cs == 0:
                kv_acc = sch.op("V", lambda e: e.memset(raug_f[:, :], 0.0),
                                deps=[kv_seq_tok.get(seq - 1)])
            mms = []
            for hh in range(NH):
                j = hh // 2
                half = hh % 2
                bank = pR1 if hh < 8 else pR2
                off = 65 * (j % 4)
                def mkv(e, hh=hh, half=half, bank=bank, off=off, pp=pp):
                    kw = dict(tile_position=(0, 64)) if half == 1 else {}
                    return e.matmul(
                        bank[64 * half:64 * (half + 1), off:off + 65],
                        kbf[:, D * pp + 64 * hh: D * pp + 64 * (hh + 1)],
                        vaug[:, pp * NH * 65 + 65 * hh: pp * NH * 65 + 65 * (hh + 1)],
                        start=True, stop=True, **kw)
                mms.append(mkv)
            kv_deps = [v_k2, v_v2, kv_acc]
            tok_kv = chain_mm(mms, deps=kv_deps)
            sch.op("V", lambda e: e.tensor_tensor(
                raug_f[:, 0:260], raug_f[:, 0:260], pR1[:, 0:260], ALU.add),
                deps=[tok_kv])
            kv_acc = sch.op("V", lambda e: e.tensor_tensor(
                raug_f[:, 260:520], raug_f[:, 260:520], pR2[:, 0:260], ALU.add))
            kbf_free[pp] = tok_kv
            vaug_free[pp] = tok_kv
            last_kv_mm = kv_acc
            if KDUMP and g == 0 and tc == 0:
                dt1 = sch.op("G", lambda e: e.dma_start(dkb_h[:, :], kbf[:, :]),
                             deps=[v_k2], dma=True)
                dt2 = sch.op("G", lambda e: e.dma_start(dvg_h[:, :], vaug[:, :]),
                             deps=[v_v2], dma=True)
                kbf_free[pp] = [tok_kv, dt1]
                vaug_free[pp] = [tok_kv, dt2]
            if tc == CPG - 1:
                xfm_free[slot] = [tok_kv, xfm_st[g]]
        if (g + 1) % GPS == 0:
            def veps(e):
                ap = raug_f[:, :].rearrange("p (j c) -> p j c", j=NPAIR)[:, :, 64:65]
                return e.tensor_scalar(ap, ap, 1e-6, None, ALU.add)
            v_eps_r = sch.op("V", veps, inc=True)
            base = seq * NPAIR * 130
            for j in range(NPAIR):
                sch.op("V", lambda e, j=j, base=base: e.tensor_copy(
                    kvaug[0:64, base + 130 * j: base + 130 * j + 65],
                    raug_f[0:64, 65 * j:65 * (j + 1)]), deps=[v_eps_r])
                tok_cp = sch.op("V", lambda e, j=j, base=base: e.tensor_copy(
                    kvaug[64:128, base + 130 * j + 65: base + 130 * (j + 1)],
                    raug_f[64:128, 65 * j:65 * (j + 1)]), inc=(j == NPAIR - 1))
            kv_seq_tok[seq] = tok_cp
        drain_bg(9)

    if KDUMP:
        sch.op("G", lambda e: e.dma_start(dkv_h[:, :], kvaug[:, :]),
               deps=[kv_seq_tok[cfg.NSEQ - 1]], dma=True)
        sch.op("G", lambda e: e.dma_start(dwk_h[:, :], WkT[:, :]),
               deps=[w_done["Wk"]], dma=True)
    p1_kv_done = kv_seq_tok[cfg.NSEQ - 1]
    p1_T_done = last_kv_mm
    p1_A_done = p1_last_a
    if upto == "p1":
        finish([p1_kv_done, p1_T_done, p1_A_done])
        es.close()
        return nc
    if upto == "setup":
        finish([v_cols, v_gb, p1_kv_done])
        es.close()
        return nc

    xfm_free2 = dict(enumerate(xfm_free))
    qe_free = None
    qpsum_free = v_id
    ctx_ps_free = p1_kv_done
    ctx_ps_free_a = p1_A_done
    xf32_free = [[v_gb, v_cols], [v_gb, v_cols]]
    x1t_free = [None, None]
    nt_free = None
    v0_free = None
    v1_free = None
    ctxbf_free = [[], []]
    ctxfm_free = [None, None]
    rden_free = [None, None]
    mv_free = None
    qelu_free = [None, None]
    x1_store_toks = []

    p2a_stage = 99
    if isinstance(upto, tuple) and upto[0] == "p2a":
        p2a_stage = upto[1]
        upto = "p2a"
    stage_toks = []
    for g in range(NG):
        seq = g // GPS
        slot = g % 2
        wl = [sch.op("A", lambda e, g=g, slot=slot: e.dma_start(
            xfm[:, slot * DC * GT:slot * DC * GT + 4096],
            xfm_h2[:, 4096 * g:4096 * (g + 1)]),
            deps=[xfm_st[g]] + ([xfm_free2.get(slot)] if xfm_free2.get(slot) else []),
            dma=True)]
        xoff = slot * DC * GT
        q_v_toks = []
        for oc in range(DC):
            bank = pS1 if oc % 2 == 0 else pS2
            mms = []
            for k in range(DC):
                def mq(e, oc=oc, k=k, bank=bank, xoff=xoff):
                    return e.matmul(bank[:, 0:512],
                                    WqT[:, D * k + 128 * oc: D * k + 128 * oc + 128],
                                    xfm[:, xoff + GT * k: xoff + GT * (k + 1)],
                                    start=(k == 0), stop=False)
                mms.append(mq)
            def mqb(e, oc=oc, bank=bank):
                return e.matmul(bank[:, 0:512], brows[0:1, 128 * oc:128 * (oc + 1)],
                                ones_row[0:1, 0:512], start=False, stop=True)
            mms.append(mqb)
            qdeps = [wl, w_done["Wq"], d_b["bq"], v_ones,
                     v_gb, v_cols, qpsum_free]
            if len(q_v_toks) >= 2:
                qdeps.append(q_v_toks[-2])
            tok_q = chain_mm(mms, deps=qdeps)
            if p2a_stage < 2:
                stage_toks.append(tok_q)
                qpsum_free = tok_q
                continue
            op_ = oc % 2
            sch.op("A", lambda e, bank=bank, op_=op_: e.activation(
                eluu[:, 512 * op_:512 * (op_ + 1)], bank[:, 0:512], AF.Relu, scale=-1.0),
                deps=[tok_q, p1_kv_done, qelu_free[op_]])
            a2 = sch.op("A", lambda e, op_=op_: e.activation(
                elue[:, 512 * op_:512 * (op_ + 1)], eluu[:, 512 * op_:512 * (op_ + 1)],
                AF.Exp, scale=-1.0), deps=[qelu_free[op_]], inc=True)
            vq = sch.op("V", lambda e, oc=oc, bank=bank, op_=op_: e.scalar_tensor_tensor(
                Qe[:, GT * oc:GT * oc + 512], bank[:, 0:512], 0.0,
                elue[:, 512 * op_:512 * (op_ + 1)], ALU.max, ALU.add),
                deps=[a2, qe_free], inc=True)
            q_v_toks.append(vq)
            qelu_free[op_] = vq
        xfm_free2[slot] = tok_q
        if KDUMP and g == 0:
            dqe_t = sch.op("G", lambda e: e.dma_start(dqe_h[:, :], Qe[:, 0:4096]),
                           deps=q_v_toks, dma=True)
            qe_free = [dqe_t, qe_free]

        if p2a_stage < 3:
            continue
        for tc in range(CPG):
            c = g * CPG + tc
            pp = c % 2
            d_x32 = sch.op("A", lambda e, c=c, pp=pp: e.dma_start(
                xf32[:, D * pp:D * (pp + 1)], x_h[128 * c:128 * (c + 1), :]),
                dma=True, deps=[xf32_free[pp]])
            v_xb = sch.op("V", lambda e, pp=pp: e.tensor_tensor(
                xbeta[:, 0:D], xf32[:, D * pp:D * (pp + 1)],
                gb_bc[:, D:2 * D], ALU.add), deps=[d_x32, v_gb], inc=True)
            mms = []
            for j in range(NPAIR):
                bank = (pA, pB, pC)[j // 3]
                off = 130 * (j % 3)
                base = seq * NPAIR * 130
                def mc(e, j=j, bank=bank, off=off, base=base, tc=tc):
                    return e.matmul(bank[:, off:off + 130],
                                    Qe[:, GT * j + 128 * tc: GT * j + 128 * tc + 128],
                                    kvaug[:, base + 130 * j: base + 130 * (j + 1)],
                                    start=True, stop=True)
                mms.append(mc)
            cdeps = list(q_v_toks) + [p1_kv_done, ctx_ps_free, ctx_ps_free_a]
            tok_ctx = chain_mm(mms, deps=cdeps)
            if tc == CPG - 1:
                qe_free = tok_ctx
            if p2a_stage < 4:
                stage_toks.append(tok_ctx)
                ctx_ps_free = tok_ctx
                ctx_ps_free_a = None
                continue
            for j in range(NPAIR):
                bank = (pA, pB, pC)[j // 3]
                off = 130 * (j % 3)
                def vr(e, j=j, bank=bank, off=off, pp=pp):
                    src = bank[:, off:off + 130].rearrange(
                        "p (a b) -> p a b", a=2)[:, :, 64:65].squeeze(2)
                    return e.reciprocal(rden[:, pp * NH + 2 * j: pp * NH + 2 * j + 2], src)
                v_rc = sch.op("V", vr, deps=[tok_ctx, rden_free[pp]],
                              inc=(j == NPAIR - 1))
            if p2a_stage < 5:
                stage_toks.append(v_rc)
                ctx_ps_free = v_rc
                ctx_ps_free_a = None
                rden_free[pp] = v_rc
                continue
            for j in range(NPAIR):
                bank = (pA, pB, pC)[j // 3]
                off = 130 * (j % 3)
                for half in range(2):
                    hh = 2 * j + half
                    v_ctxe = sch.op("V", lambda e, bank=bank, off=off, half=half,
                                    hh=hh, pp=pp: e.tensor_scalar(
                                        ctxbf[:, D * pp + 64 * hh: D * pp + 64 * (hh + 1)],
                                        bank[:, off + 65 * half: off + 65 * half + 64],
                                        rden[:, pp * NH + hh: pp * NH + hh + 1],
                                        None, ALU.mult),
                                    deps=[v_rc] + ctxbf_free[pp], inc=True)
            a_ctxe = v_ctxe
            if KDUMP and c == 0:
                dcx_t = sch.op("G", lambda e: e.dma_start(dcx_h[:, :], ctxbf[:, 0:D]),
                               deps=[v_ctxe], dma=True)
                v_ctxe = [v_ctxe, dcx_t]
                a_ctxe = v_ctxe
            ctx_ps_free, ctx_ps_free_a = v_ctxe, a_ctxe
            rden_free[pp] = v_ctxe
            if p2a_stage < 6:
                stage_toks.append(v_ctxe)
                stage_toks.append(a_ctxe)
                ctxbf_free[pp] = [v_ctxe, a_ctxe]
                continue
            ct_toks = []
            for k in range(DC):
                t = sch.op("S", lambda e, k=k, pp=pp: e.dma_start(
                    ctxfm[:, D * pp + 128 * k: D * pp + 128 * (k + 1)],
                    ctxbf[:, D * pp + 128 * k: D * pp + 128 * (k + 1)], transpose=True),
                    deps=[v_ctxe, a_ctxe, ctxfm_free[pp]], dma=True)
                ct_toks.append(t)
            ctxbf_free[pp] = ct_toks
            if p2a_stage < 7:
                stage_toks.extend(ct_toks)
                continue
            mms = []
            for k in range(DC):
                for half, bank in ((0, pS1), (1, pS2)):
                    def mh(e, k=k, half=half, bank=bank, pp=pp):
                        return e.matmul(bank[:, 0:512],
                                        ctxfm[:, D * pp + 128 * k: D * pp + 128 * (k + 1)],
                                        WdT[:, D * k + 512 * half: D * k + 512 * (half + 1)],
                                        start=(k == 0), stop=False)
                    mms.append(mh)
            for half, bank in ((0, pS1), (1, pS2)):
                def mhb(e, half=half, bank=bank):
                    return e.matmul(bank[:, 0:512], ones_row[0:1, 0:128],
                                    brows[0:1, 3 * D + 512 * half: 3 * D + 512 * (half + 1)],
                                    start=False, stop=True)
                mms.append(mhb)
            hdeps = [ct_toks, w_done["Wd"], d_b["bd"],
                     qpsum_free] + q_v_toks
            tok_hs = chain_mm(mms, deps=hdeps)
            ctxfm_free[pp] = tok_hs
            if p2a_stage < 8:
                stage_toks.append(tok_hs)
                qpsum_free = tok_hs
                guard_pS[0] = tok_hs
                continue
            v_v0a = sch.op("V", lambda e, pp=pp: e.tensor_tensor(
                v0t[:, 0:512], pS1[:, 0:512], xf32[:, D * pp:D * pp + 512], ALU.add),
                deps=[tok_hs, d_x32, v0_free])
            v_v0b = sch.op("V", lambda e, pp=pp: e.tensor_tensor(
                v0t[:, 512:1024], pS2[:, 0:512], xf32[:, D * pp + 512:D * pp + 1024],
                ALU.add), inc=True)
            qpsum_free = v_v0b
            guard_pS[0] = v_v0b
            sch.op("V", lambda e: e.bn_stats(stat[:, 0:6], v0t[:, 0:512]), deps=[mv_free])
            sch.op("V", lambda e: e.bn_stats(stat[:, 6:12], v0t[:, 512:1024]))
            v_ag = sch.op("V", lambda e: e.bn_aggr(mv[:, 0:2], stat[:, 0:12]), inc=True)
            sch.op("A", lambda e: e.activation(mv[:, 2:3], mv[:, 1:2], AF.Ln, bias=epsc[:, 0:1]),
                   deps=[v_ag, v_eps])
            a_r = sch.op("A", lambda e: e.activation(mv[:, 3:4], mv[:, 2:3], AF.Exp,
                                                     scale=-0.5), inc=True)
            a_n = sch.op("V", lambda e: e.tensor_scalar(
                nt[:, 0:D], v0t[:, 0:D], mv[:, 0:1], mv[:, 3:4], ALU.subtract, ALU.mult),
                deps=[a_r, nt_free], inc=True)
            v0_free = a_n
            sch.op("V", lambda e: e.tensor_tensor(nt[:, 0:D], nt[:, 0:D], gb_bc[:, 0:D],
                                                  ALU.mult), deps=[a_n])
            v_v1b = sch.op("V", lambda e: e.tensor_tensor(
                v1t[:, 0:D], nt[:, 0:D], xbeta[:, 0:D], ALU.add),
                deps=[v_xb, v1_free], inc=True)
            nt_free = v_v1b
            xf32_free[pp] = v_v1b
            sch.op("V", lambda e: e.bn_stats(stat[:, 12:18], v1t[:, 0:512]))
            sch.op("V", lambda e: e.bn_stats(stat[:, 18:24], v1t[:, 512:1024]))
            v_ag1 = sch.op("V", lambda e: e.bn_aggr(mv[:, 6:8], stat[:, 12:24]), inc=True)
            sch.op("A", lambda e: e.activation(mv[:, 2:3], mv[:, 7:8], AF.Ln, bias=epsc[:, 1:2]),
                   deps=[v_ag1, v_eps])
            a_r1 = sch.op("A", lambda e: e.activation(mv[:, 3:4], mv[:, 2:3], AF.Exp,
                                                      scale=-0.5), inc=True)
            a_x1 = sch.op("V", lambda e, pp=pp: e.tensor_scalar(
                x1t[:, D * pp:D * (pp + 1)], v1t[:, 0:D], mv[:, 6:7], mv[:, 3:4],
                ALU.subtract, ALU.mult), deps=[a_r1, x1t_free[pp]], inc=True)
            mv_free = a_x1
            v1_free = a_x1
            st = sch.op("G", lambda e, c=c, pp=pp: e.dma_start(
                x1_h[128 * c:128 * (c + 1), :], x1t[:, D * pp:D * (pp + 1)]),
                deps=[a_x1], dma=True)
            x1t_free[pp] = st
            x1_store_toks.append(st)
        drain_bg(9)

    drain_all_bg()
    if p2a_stage < 99:
        p2a_T_done = p2a_V_done = p2a_A_done = None
    else:
        p2a_T_done = tok_hs
        p2a_V_done = a_x1
        p2a_A_done = a_r1
    if upto == "p2a":
        if p2a_stage < 99:
            finish([t for t in stage_toks[-8:] if t] +
                   [bgstate["wfcq_done"], bgstate["wprq_done"]])
            es.close()
            return nc
        finish([p2a_T_done, p2a_V_done, p2a_A_done, x1_store_toks[-1],
                bgstate["wfcq_done"], bgstate["wprq_done"]])
        es.close()
        return nc

    stg_free = [None, None, None, None]
    ps_tr = {0: [qpsum_free], 1: [qpsum_free]}
    ps_par = [0]

    def pe_transpose(dst_slice, src_slice, deps_mm, deps_evac):
        p = ps_par[0]
        bank = pS1 if p == 0 else pS2
        t = sch.op("T", lambda e, bank=bank, src_slice=src_slice: e.matmul(
            bank[:, 0:128], src_slice, idmat[:, :], start=True, stop=True),
            deps=[v_id] + deps_mm + ps_tr[p], inc=True)
        v = sch.op("V", lambda e, bank=bank, dst_slice=dst_slice: e.tensor_copy(
            dst_slice, bank[:, 0:128]), deps=[t] + deps_evac, inc=True)
        ps_tr[p] = [v]
        ps_par[0] ^= 1
        return t, v

    wfcT_by_hc = {}
    wfcT_loads = []
    for hc in range(HC):
        s = hc % 4
        ld = sch.op("A", lambda e, hc=hc, s=s: e.dma_start(
            wstg[:, 1024 * s:1024 * (s + 1)], wfcq_h[128 * hc:128 * (hc + 1), :]),
            deps=[bgstate["wfcq_done"], stg_free[s]], dma=True)
        toks = []
        t_last = None
        for k in range(DC):
            if k < 4:
                rdep = [p1_T_done]
            elif k < 6:
                rdep = [qpsum_free]
            else:
                rdep = [p2a_T_done]
            t_last, v = pe_transpose(
                WfcT[:, HFF * k + 128 * hc:HFF * k + 128 * hc + 128],
                wstg[:, 1024 * s + 128 * k:1024 * s + 128 * k + 128],
                [ld], rdep)
            toks.append(v)
        stg_free[s] = t_last
        wfcT_by_hc[hc] = toks
        wfcT_loads.extend(toks)

    if KDUMP:
        sch.op("G", lambda e: e.dma_start(dqh_h[:, :], wfcq_h[:, :]),
               deps=[bgstate["wfcq_done"]], dma=True)
        sch.op("G", lambda e: e.dma_start(dqp_h[:, :], wprq_h[:, :]),
               deps=[bgstate["wprq_done"]], dma=True)
        sch.op("G", lambda e: e.dma_start(dwf_h[:, :], WfcT[:, 0:DC * HFF]),
               deps=wfcT_loads, dma=True)

    x1fm_free = [None, None]
    fc_ps_free = [[ctx_ps_free, ctx_ps_free_a], [ctx_ps_free, ctx_ps_free_a]]
    hqt_free = [None, None]
    stat_ps_free = p1_kv_done
    hgrp_free = None
    hst_toks = []
    lastrow = {}

    for g in range(NG):
        slot = g % 2
        wl = []
        for tc in range(CPG):
            c = g * CPG + tc
            s = tc
            ld = sch.op("A", lambda e, c=c, s=s: e.dma_start(
                wstg[:, 1024 * s:1024 * (s + 1)], x1_h[128 * c:128 * (c + 1), :]),
                deps=[x1_store_toks[c], stg_free[s]], dma=True)
            t_last = None
            for k in range(DC):
                edep = [x1fm_free[slot]] if x1fm_free[slot] else []
                t_last, v = pe_transpose(
                    x1fm[:, slot * DC * GT + GT * k + 128 * tc:
                         slot * DC * GT + GT * k + 128 * tc + 128],
                    wstg[:, 1024 * s + 128 * k:1024 * s + 128 * k + 128],
                    [ld], edep)
                wl.append(v)
            stg_free[s] = t_last
        xoff = slot * DC * GT
        lastst = None
        a_gelu_last = None
        first_extra = [hgrp_free] if hgrp_free else []
        for hc in range(HC):
            bank = pA if hc % 2 == 0 else pB
            hcp = hc % 2
            mms = []
            for k in range(DC):
                def mf(e, hc=hc, k=k, bank=bank, xoff=xoff):
                    return e.matmul(bank[:, 0:512],
                                    WfcT[:, HFF * k + 128 * hc: HFF * k + 128 * hc + 128],
                                    x1fm[:, xoff + GT * k: xoff + GT * (k + 1)],
                                    start=(k == 0), stop=(k == DC - 1))
                mms.append(mf)
            fdeps = [wl, wfcT_by_hc[hc], fc_ps_free[hcp]]
            tok_fc = chain_mm(mms, deps=fdeps)
            a_g = sch.op("A", lambda e, hc=hc, bank=bank: e.activation(
                hgrp[:, GT * hc:GT * (hc + 1)], bank[:, 0:512], AF.Gelu),
                deps=[tok_fc] + (first_extra if hc == 0 else []), inc=True)
            first_extra = []
            fc_ps_free[hcp] = [a_g]
            a_q = sch.op("A", lambda e, hc=hc, hcp=hcp: e.activation(
                hqt[:, 512 * hcp:512 * (hcp + 1)], hgrp[:, GT * hc:GT * (hc + 1)],
                AF.Square), deps=[hqt_free[hcp]], inc=True)
            a_gelu_last = a_q
            def ms1(e, hc=hc):
                return e.matmul(pR1[0:1, 0:512], ones_col[:, 0:1],
                                hgrp[:, GT * hc:GT * (hc + 1)],
                                start=(hc == 0), stop=(hc == HC - 1))
            def ms2(e, hc=hc, hcp=hcp):
                return e.matmul(pR2[0:1, 0:512], ones_col[:, 0:1],
                                hqt[:, 512 * hcp:512 * (hcp + 1)],
                                start=(hc == 0), stop=(hc == HC - 1))
            sdeps = [a_g, a_q, v_ones]
            if hc == 0:
                sdeps.append(stat_ps_free)
            sch.op("T", ms1, deps=sdeps)
            t2 = sch.op("T", ms2, inc=(hc == HC - 1))
            hqt_free[hcp] = t2
            if hc == HC - 1:
                lastst = t2
        sch.op("V", lambda e, g=g: e.tensor_copy(
            sumh_r[0:1, GT * g:GT * (g + 1)], pR1[0:1, 0:512]), deps=[lastst])
        v_sr2 = sch.op("V", lambda e, g=g: e.tensor_copy(
            sumq_r[0:1, GT * g:GT * (g + 1)], pR2[0:1, 0:512]), inc=True)
        lastrow[g] = v_sr2
        stat_ps_free = v_sr2
        sth = sch.op("G", lambda e, g=g: e.dma_start(
            h_sh[:, :, GT * g:GT * (g + 1)],
            hgrp[:, :].rearrange("p (h t) -> p h t", h=HC)),
            deps=[lastst, a_gelu_last], dma=True)
        hst_toks.append(sth)
        hgrp_free = sth
        x1fm_free[slot] = tok_fc

    if KDUMP:
        sch.op("G", lambda e: e.dma_start(
            dhh_h[:, :], h_sh[:, :, :].rearrange("p h t -> p (h t)")),
            deps=hst_toks, dma=True)

    if upto == "p2bi":
        finish([lastrow[NG - 1], hst_toks[-1]] + fc_ps_free[0] + fc_ps_free[1])
        es.close()
        return nc

    W2T = WfcT
    c0_last = None
    s2_last = None
    w2_scale_last = None
    w2dep = [bgstate["wprq_done"], hst_toks[-1]] + fc_ps_free[0] + fc_ps_free[1]
    w2_by_k = {k: [] for k in range(HC)}
    for o in range(DC):
        for qtr in range(HFF // 1024):
            s = (o * (HFF // 1024) + qtr) % 4
            ld = sch.op("A", lambda e, o=o, qtr=qtr, s=s: e.dma_start(
                wstg[:, 1024 * s:1024 * (s + 1)],
                wprq_h[128 * o:128 * (o + 1), 1024 * qtr:1024 * (qtr + 1)]),
                deps=[bgstate["wprq_done"], stg_free[s]], dma=True)
            t_last = None
            for kk in range(8):
                k = 8 * qtr + kk
                t_last, v = pe_transpose(
                    W2T[:, D * k + 128 * o:D * k + 128 * (o + 1)],
                    wstg[:, 1024 * s + 128 * kk:1024 * s + 128 * kk + 128],
                    [ld], w2dep)
                w2_by_k[k].append(v)
            stg_free[s] = t_last
    for k in range(HC):
        d = w2_by_k[k]
        sch.op("T", lambda e, k=k: e.matmul(
            pR1[0:1, 0:512], lnb_cols[:, k:k + 1], W2T[:, D * k:D * k + 512],
            start=(k == 0), stop=(k == HC - 1)),
            deps=[d, v_cols, lastrow[NG - 1]])
        c0_last = sch.op("T", lambda e, k=k: e.matmul(
            pR2[0:1, 0:512], lnb_cols[:, k:k + 1], W2T[:, D * k + 512:D * (k + 1)],
            start=(k == 0), stop=(k == HC - 1)), inc=True)
        vsc = sch.op("V", lambda e, k=k: e.tensor_scalar(
            W2T[:, D * k:D * (k + 1)], W2T[:, D * k:D * (k + 1)],
            lnw_cols[:, k:k + 1], None, ALU.mult), deps=[c0_last], inc=True)
        w2_scale_last = vsc
        sch.op("T", lambda e, k=k: e.matmul(
            pC[0:1, 0:512], ones_col[:, 0:1], W2T[:, D * k:D * k + 512],
            start=(k == 0), stop=(k == HC - 1)), deps=[vsc])
        s2_last = sch.op("T", lambda e, k=k: e.matmul(
            pD[0:1, 0:512], ones_col[:, 0:1], W2T[:, D * k + 512:D * (k + 1)],
            start=(k == 0), stop=(k == HC - 1)), inc=True)
    w2_ready = w2_scale_last

    v_c0a = sch.op("V", lambda e: e.tensor_copy(c0row[0:1, 0:512], pR1[0:1, 0:512]),
                   deps=[c0_last, s2_last])
    v_c0b = sch.op("V", lambda e: e.tensor_copy(c0row[0:1, 512:1024], pR2[0:1, 0:512]),
                   inc=True)
    tb1 = t_ms(lambda e: e.matmul(pS1[:, 0:512], onesf_row[0:1, :], c0row[0:1, 0:512],
                                  start=True, stop=True), deps=[v_c0b, qpsum_free])
    v_cb1 = sch.op("V", lambda e: e.tensor_copy(c0_bc[:, 0:512], pS1[:, 0:512]),
                   deps=[tb1], inc=True)
    tb2 = t_ms(lambda e: e.matmul(pS1[:, 0:512], onesf_row[0:1, :], c0row[0:1, 512:1024],
                                  start=True, stop=True), deps=[v_cb1])
    v_cb2 = sch.op("V", lambda e: e.tensor_copy(c0_bc[:, 512:1024], pS1[:, 0:512]),
                   deps=[tb2], inc=True)
    v_s2a = sch.op("V", lambda e: e.tensor_copy(s2_row[0:1, 0:512], pC[0:1, 0:512]),
                   deps=[s2_last])
    v_s2b = sch.op("V", lambda e: e.tensor_copy(s2_row[0:1, 512:1024], pD[0:1, 0:512]),
                   inc=True)

    row_v = []
    row_a = []
    for pcs in range((T + 1023) // 1024):
        lo, hi = 1024 * pcs, min(T, 1024 * (pcs + 1))
        sch.op("V", lambda e, lo=lo, hi=hi: e.tensor_scalar(
            sumh_r[0:1, lo:hi], sumh_r[0:1, lo:hi], 1.0 / HFF, None, ALU.mult),
            deps=[lastrow[NG - 1]])
        v2 = sch.op("V", lambda e, lo=lo, hi=hi: e.tensor_scalar(
            negmu_r[0:1, lo:hi], sumh_r[0:1, lo:hi], -1.0, None, ALU.mult), inc=True)
        sch.op("V", lambda e, lo=lo, hi=hi: e.tensor_scalar(
            sumq_r[0:1, lo:hi], sumq_r[0:1, lo:hi], 1.0 / HFF, None, ALU.mult))
        sch.op("V", lambda e, lo=lo, hi=hi: e.tensor_tensor(
            c0row[0:1, 0:hi - lo], sumh_r[0:1, lo:hi], sumh_r[0:1, lo:hi], ALU.mult),
            deps=[v_cb2])
        v4 = sch.op("V", lambda e, lo=lo, hi=hi: e.tensor_tensor(
            sumq_r[0:1, lo:hi], sumq_r[0:1, lo:hi], c0row[0:1, 0:hi - lo], ALU.subtract),
            inc=True)
        row_v.append((v2, v4))

    rcol_last = None
    rows_v_done = row_v[-1][1]
    for pcs in range((T + 1023) // 1024):
        lo, hi = 1024 * pcs, min(T, 1024 * (pcs + 1))
        a1 = sch.op("A", lambda e, lo=lo, hi=hi: e.activation(
            sumq_r[0:1, lo:hi], sumq_r[0:1, lo:hi], AF.Ln, bias=epsc[0:1, 1:2]), deps=[rows_v_done, v_eps])
        a2 = sch.op("A", lambda e, lo=lo, hi=hi: e.activation(
            sumq_r[0:1, lo:hi], sumq_r[0:1, lo:hi], AF.Exp, scale=-0.5), inc=True)
        row_a.append(a2)
    rows_a_done = row_a[-1]

    for ci in range(NC):
        dep = [rows_a_done] + ([rcol_last] if rcol_last else [])
        tkt = t_ms(lambda e, ci=ci: e.transpose(pS2[:, 2:3],
                                                sumq_r[0:1, 128 * ci:128 * (ci + 1)],
                                                onef[0:1, 0:1]), deps=dep)
        rcol_last = sch.op("V", lambda e, ci=ci: e.tensor_copy(
            rff_cols[:, ci:ci + 1], pS2[:, 2:3]), deps=[tkt], inc=True)
    v_rcols = rcol_last

    NHG = T // 256
    hld_free = [None, None]
    x1tb_free = [None, None]
    x1c_free = None
    v2_free = None
    out_free = [None, None]
    proj_ps_free = [s2_last, s2_last]
    mv2_free = None
    out_toks = []

    for hgi in range(NHG):
        slot = hgi % 2
        qh = "A" if hgi % 2 == 0 else "G"
        d_h = sch.op(qh, lambda e, hgi=hgi, slot=slot: e.dma_start(
            hld[:, slot * HC * 256:(slot + 1) * HC * 256].rearrange("p (h t) -> p h t", h=HC),
            h_sh[:, :, 256 * hgi:256 * (hgi + 1)]),
            deps=[hst_toks[hgi // 2], hld_free[slot]], dma=True)
        for t2_ in range(2):
            c = 2 * hgi + t2_
            pp = c % 2
            bank1, bank2 = (pA, pB) if pp == 0 else (pC, pD)
            d_x1 = sch.op("A", lambda e, c=c, pp=pp: e.dma_start(
                x1t_b[:, D * pp:D * (pp + 1)], x1_h[128 * c:128 * (c + 1), :]),
                deps=[x1_store_toks[c], x1tb_free[pp]], dma=True)
            v_x1c = sch.op("V", lambda e, pp=pp: e.tensor_tensor(
                x1c[:, 0:D], x1t_b[:, D * pp:D * (pp + 1)], c0_bc[:, :], ALU.add),
                deps=[d_x1, v_cb2, x1c_free], inc=True)
            x1tb_free[pp] = v_x1c
            mms = []
            for k in range(HC):
                for half, bank in ((0, bank1), (1, bank2)):
                    def mp(e, k=k, half=half, bank=bank, slot=slot, t2_=t2_):
                        return e.matmul(
                            bank[:, 0:512],
                            hld[:, slot * HC * 256 + 256 * k + 128 * t2_:
                                 slot * HC * 256 + 256 * k + 128 * t2_ + 128],
                            W2T[:, D * k + 512 * half:D * k + 512 * (half + 1)],
                            start=(k == 0), stop=False)
                    mms.append(mp)
            for half, bank in ((0, bank1), (1, bank2)):
                def mpf(e, half=half, bank=bank, c=c):
                    return e.matmul(bank[:, 0:512],
                                    negmu_r[0:1, 128 * c:128 * (c + 1)],
                                    s2_row[0:1, 512 * half:512 * (half + 1)],
                                    start=False, stop=True)
                mms.append(mpf)
            pdeps = [d_h, v_s2b, rows_v_done, w2_ready, proj_ps_free[pp]]
            tok_p = chain_mm(mms, deps=pdeps)
            if t2_ == 1:
                hld_free[slot] = tok_p
            v_ya = sch.op("V", lambda e, c=c, bank1=bank1: e.scalar_tensor_tensor(
                v2t[:, 0:512], bank1[:, 0:512], rff_cols[:, c:c + 1],
                x1c[:, 0:512], ALU.mult, ALU.add),
                deps=[tok_p, v_rcols, v_x1c, v2_free])
            v_yb = sch.op("V", lambda e, c=c, bank2=bank2: e.scalar_tensor_tensor(
                v2t[:, 512:1024], bank2[:, 0:512], rff_cols[:, c:c + 1],
                x1c[:, 512:1024], ALU.mult, ALU.add), inc=True)
            proj_ps_free[pp] = v_yb
            x1c_free = v_yb
            sch.op("V", lambda e: e.bn_stats(stat[:, 0:6], v2t[:, 0:512]), deps=[mv2_free])
            sch.op("V", lambda e: e.bn_stats(stat[:, 6:12], v2t[:, 512:1024]))
            v_ag2 = sch.op("V", lambda e: e.bn_aggr(mv[:, 0:2], stat[:, 0:12]), inc=True)
            sch.op("A", lambda e: e.activation(mv[:, 2:3], mv[:, 1:2], AF.Ln, bias=epsc[:, 1:2]),
                   deps=[v_ag2, v_eps])
            a_r2 = sch.op("A", lambda e: e.activation(mv[:, 3:4], mv[:, 2:3], AF.Exp,
                                                      scale=-0.5), inc=True)
            a_o = sch.op("V", lambda e, pp=pp: e.tensor_scalar(
                outt[:, D * pp:D * (pp + 1)], v2t[:, 0:D], mv[:, 0:1], mv[:, 3:4],
                ALU.subtract, ALU.mult), deps=[a_r2, out_free[pp]], inc=True)
            mv2_free = a_o
            v2_free = a_o
            sto = sch.op("G", lambda e, c=c, pp=pp: e.dma_start(
                out_h[128 * c:128 * (c + 1), :], outt[:, D * pp:D * (pp + 1)]),
                deps=[a_o], dma=True)
            out_free[pp] = sto
            out_toks.append(sto)

    sch.op("S", lambda e: e.nop(), deps=out_toks)
    sch.build(nc)
    es.close()
    return nc


_CACHE = {}


def _get_program(cfg=None):
    key = "full" if cfg is None else (cfg.S, cfg.NSEQ, cfg.HFF)
    if key not in _CACHE:
        _CACHE[key] = build_program(cfg or Cfg())
    return _CACHE[key]


def kernel(L0_attn_input_tensor, L0_attn_attention_mask, L0_attn_Wq, L0_attn_bq,
           L0_attn_Wk, L0_attn_bk, L0_attn_Wv, L0_attn_bv, L0_attn_Wd, L0_attn_bd,
           L0_attn_gamma, L0_attn_beta, L0_ffn_W_fc, L0_ffn_ln_w, L0_ffn_ln_b,
           L0_ffn_W_proj):
    x = np.ascontiguousarray(np.asarray(L0_attn_input_tensor, dtype=np.float32))
    B, S, D = x.shape
    nc = _get_program()
    w = dict(
        Wq=np.ascontiguousarray(L0_attn_Wq, dtype=np.float32),
        Wk=np.ascontiguousarray(L0_attn_Wk, dtype=np.float32),
        Wv=np.ascontiguousarray(L0_attn_Wv, dtype=np.float32),
        Wd=np.ascontiguousarray(L0_attn_Wd, dtype=np.float32),
        bq=np.ascontiguousarray(np.asarray(L0_attn_bq, dtype=np.float32).reshape(1, D)),
        bk=np.ascontiguousarray(np.asarray(L0_attn_bk, dtype=np.float32).reshape(1, D)),
        bv=np.ascontiguousarray(np.asarray(L0_attn_bv, dtype=np.float32).reshape(1, D)),
        bd=np.ascontiguousarray(np.asarray(L0_attn_bd, dtype=np.float32).reshape(1, D)),
        gamma=np.ascontiguousarray(np.asarray(L0_attn_gamma, dtype=np.float32).reshape(1, D)),
        beta=np.ascontiguousarray(np.asarray(L0_attn_beta, dtype=np.float32).reshape(1, D)),
        Wfc=np.ascontiguousarray(L0_ffn_W_fc, dtype=np.float32),
        lnw=np.ascontiguousarray(np.asarray(L0_ffn_ln_w, dtype=np.float32).reshape(1, -1)),
        lnb=np.ascontiguousarray(np.asarray(L0_ffn_ln_b, dtype=np.float32).reshape(1, -1)),
        Wproj=np.ascontiguousarray(L0_ffn_W_proj, dtype=np.float32),
    )
    per = B // N_CORES
    in_maps = []
    for c in range(N_CORES):
        m = dict(w)
        m["x"] = np.ascontiguousarray(x[per * c:per * (c + 1)].reshape(per * S, D))
        in_maps.append(m)
    res = run_bass_kernel_spmd(nc, in_maps, core_ids=list(range(N_CORES)))
    outs = [np.asarray(r["out"]).reshape(per, S, D) for r in res.results]
    return np.concatenate(outs, axis=0).astype(np.float32)


# version 06
# speedup vs baseline: 1.0729x; 1.0729x over previous
import sys
import numpy as np

sys.path.insert(0, "/opt/trn_rl_repo")
import concourse.bass as bass
import concourse.mybir as mybir
from concourse.bass_utils import run_bass_kernel_spmd

F32 = mybir.dt.float32
BF16 = mybir.dt.bfloat16
AF = mybir.ActivationFunctionType
ALU = mybir.AluOpType
AX = mybir.AxisListType

N_CORES = 8


class Sched:
    STREAMS = ("T", "V", "A", "G", "S")
    NROT_BY = {"S": 3, "A": 3, "G": 2, "T": 1, "V": 1}

    def __init__(self):
        self.ops = {s: [] for s in self.STREAMS}
        self.cnt = {}
        self.known = {s: {} for s in self.STREAMS}
        self.dma_idx = {s: 0 for s in self.STREAMS}
        for s in self.STREAMS:
            self.cnt["c" + s] = 0

    @staticmethod
    def _flat(deps, out):
        for t in deps:
            if t is None:
                continue
            if isinstance(t, list):
                Sched._flat(t, out)
            else:
                out.append(t)

    def op(self, stream, fn, deps=(), inc=False, dma=False):
        waits = []
        k = self.known[stream]
        tok = None
        incspec = None
        if stream in ("V", "A", "G") and not dma:
            sem = "c" + stream
            prev = self.cnt[sem]
            if prev > 0:
                waits.append((sem, prev))
            inc = True
        if dma:
            i = self.dma_idx[stream]
            self.dma_idx[stream] += 1
            nrot = self.NROT_BY[stream]
            sem = "d%s%d" % (stream, i % nrot)
            base = 16 * (i // nrot)
            if base > 0 and k.get(sem, 0) < base:
                k[sem] = base
                waits.append((sem, base))
            tok = (sem, base + 16)
            incspec = (sem, 16)
        elif inc:
            sem = "c" + stream
            self.cnt[sem] += 1
            tok = (sem, self.cnt[sem])
            incspec = (sem, 1)
        flat = []
        self._flat(deps, flat)
        best = {}
        for sem, val in flat:
            if sem == "c" + stream and not dma:
                continue
            if val > best.get(sem, -1):
                best[sem] = val
        for sem in sorted(best):
            val = best[sem]
            if k.get(sem, 0) >= val:
                continue
            k[sem] = val
            waits.append((sem, val))
        self.ops[stream].append((waits, fn, incspec))
        return tok

    def build(self, nc):
        from contextlib import ExitStack
        es = ExitStack()
        sems = {}
        for s in self.STREAMS:
            sems["c" + s] = es.enter_context(nc.semaphore("sem_c" + s))
            for r in range(self.NROT_BY[s]):
                sems["d%s%d" % (s, r)] = es.enter_context(
                    nc.semaphore("sem_d%s%d" % (s, r)))
        engmap = {"T": "tensor", "V": "vector", "A": "scalar", "G": "gpsimd", "S": "sync"}
        block = es.enter_context(nc.Block())
        for s in self.STREAMS:
            def body(e, _s=s):
                for waits, fn, incspec in self.ops[_s]:
                    for sem, val in waits:
                        e.wait_ge(sems[sem], val)
                    inst = fn(e)
                    if incspec is not None:
                        inst.then_inc(sems[incspec[0]], incspec[1])
            getattr(block, engmap[s])(body)
        es.close()


class Cfg:
    def __init__(self, S=2048, NSEQ=2, HFF=4096):
        self.S, self.NSEQ, self.HFF = S, NSEQ, HFF
        self.D = 1024
        self.NH = 16
        self.T = S * NSEQ
        self.HD = 64
        self.NC = self.T // 128
        self.CPS = S // 128
        self.DC = 8
        self.HC = HFF // 128
        self.GT = 512
        self.CPG = 4
        self.NG = self.T // 512
        self.GPS = S // 512
        assert S % 512 == 0 and HFF % 1024 == 0 and self.NG >= 1


class Carver:
    def __init__(self, ap, size):
        self.ap, self.size, self.off = ap, size, 0

    def take(self, els):
        assert self.off + els <= self.size, (self.off, els, self.size)
        s = self.ap[:, self.off:self.off + els]
        self.off += els
        return s

    def row(self, els):
        assert self.off + els <= self.size
        s = self.ap[0:1, self.off:self.off + els]
        self.off += els
        return s


def build_program(cfg: Cfg, upto=None):
    nc = bass.Bass("TRN2", target_bir_lowering=False)
    S, D, NH, HFF, T = cfg.S, cfg.D, cfg.NH, cfg.HFF, cfg.T
    NC, CPS, DC, HC = cfg.NC, cfg.CPS, cfg.DC, cfg.HC
    GT, CPG, NG, GPS = cfg.GT, cfg.CPG, cfg.NG, cfg.GPS
    NPAIR = NH // 2
    RT = HFF // 128
    sch = Sched()

    _fin = {}
    def finish(deps):
        sch.op("S", lambda e: e.nop(), deps=deps)
        sch.build(nc)
        _fin["done"] = True

    def din(name, shape, dt=F32):
        return nc.dram_tensor(name, list(shape), dt, kind="ExternalInput").ap()

    x_h = din("x", (T, D))
    W_attn = {w: din(w, (D, D)) for w in ("Wq", "Wk", "Wv", "Wd")}
    b_attn = {b: din(b, (1, D)) for b in ("bq", "bk", "bv", "bd")}
    gam_h = din("gamma", (1, D)); bet_h = din("beta", (1, D))
    Wfc_h = din("Wfc", (HFF, D))
    lnw_h = din("lnw", (1, HFF)); lnb_h = din("lnb", (1, HFF))
    Wpr_h = din("Wproj", (D, HFF))
    out_h = nc.dram_tensor("out", [T, D], F32, kind="ExternalOutput").ap()
    xbf_h = nc.dram_tensor("xbf_s", [T, D], BF16, kind="Internal").ap()
    import os as _os
    x1_h = nc.dram_tensor("x1_s", [T, D], BF16,
                          kind="ExternalOutput" if _os.environ.get("KDBG_X1") else "Internal").ap()
    KDUMP = bool(_os.environ.get("KDBG_DUMPS"))
    if KDUMP:
        dkv_h = nc.dram_tensor("d_kv", [128, cfg.NSEQ * 8 * 130], BF16, kind="ExternalOutput").ap()
        dqe_h = nc.dram_tensor("d_qe", [128, 8 * 512], BF16, kind="ExternalOutput").ap()
        dkb_h = nc.dram_tensor("d_kb", [128, 2 * D], BF16, kind="ExternalOutput").ap()
        dvg_h = nc.dram_tensor("d_vg", [128, 2 * NH * 65], BF16, kind="ExternalOutput").ap()
        dcx_h = nc.dram_tensor("d_cx", [128, D], BF16, kind="ExternalOutput").ap()
        dxf_h = nc.dram_tensor("d_xf", [128, 8 * 512], BF16, kind="ExternalOutput").ap()
        dwk_h = nc.dram_tensor("d_wk", [128, DC * D], BF16, kind="ExternalOutput").ap()
        dqh_h = nc.dram_tensor("d_qh", [HFF, D], BF16, kind="ExternalOutput").ap()
        dqp_h = nc.dram_tensor("d_qp", [D, HFF], BF16, kind="ExternalOutput").ap()
        dwf_h = nc.dram_tensor("d_wf", [128, DC * HFF], BF16, kind="ExternalOutput").ap()
        dhh_h = nc.dram_tensor("d_hh", [128, HC * T], BF16, kind="ExternalOutput").ap()
    wfcq_h = nc.dram_tensor("wfcq_s", [HFF, D], BF16, kind="Internal").ap()
    wprq_h = nc.dram_tensor("wprq_s", [D, HFF], BF16, kind="Internal").ap()
    wbf_h = {w: nc.dram_tensor(w + "bf_s", [D, D], BF16, kind="Internal").ap()
             for w in ("Wq", "Wk", "Wv", "Wd")}
    h_sh = nc.dram_tensor("h_s", [128, HC, T], BF16, kind="Internal").ap()
    xfm_h2 = nc.dram_tensor("xfm_s", [128, (T // 512) * 4096], BF16, kind="Internal").ap()

    from contextlib import ExitStack
    es = ExitStack()

    def sb(name, p, f, dt):
        return es.enter_context(nc.sbuf_tensor(name, [p, f], dt))

    def ps(name, f):
        return es.enter_context(nc.psum_tensor(name, [128, f], F32))

    ones_col = sb("ones_col", 128, 1, BF16)
    ones_row = sb("ones_row", 1, 512, BF16)
    onesf_row = sb("onesf_row", 1, 128, F32)
    onesf_col = sb("onesf_col", 128, 1, F32)
    onef = sb("onef", 1, 1, F32)
    brows = sb("brows", 1, 4 * D, BF16)
    gb_bc = sb("gb_bc", 128, 2 * D, BF16)
    c0_bc = sb("c0_bc", 128, D, BF16)
    s2_row = sb("s2_row", 1, D, BF16)
    lnw_cols = sb("lnw_cols", 128, HC, F32)
    lnb_cols = sb("lnb_cols", 128, HC, BF16)
    rff_cols = sb("rff_cols", 128, NC, F32)
    thr_cols = sb("thr_cols", 128, 4, F32)
    accq = sb("accq", 128, 2, F32)
    redq = sb("redq", 128, RT + DC, F32)
    smallf = sb("smallf", 1, 16, F32)
    mv = sb("mv", 128, 8, F32)
    stat = sb("stat", 128, 24, F32)
    rden = sb("rden", 128, 2 * NH, F32)
    epsc = sb("epsc", 128, 2, F32)
    idmat = sb("idmat", 128, 128, BF16)
    idr = sb("idr", 1, 128, F32)
    idc = sb("idc", 128, 1, F32)
    wstg = sb("wstg", 128, 4096, BF16)

    B1 = 8192 + 4096 + 2048 + 2 * NH * 65 + 2048 + 2048 + 2048 + 2048 + cfg.NSEQ * NPAIR * 130 + 2048
    B2 = T + max(2 * DC * GT + HC * GT + 1024, 2 * HC * 256 + 2 * D)
    AR_B = sb("AR_B", 128, max(B1, B2) + 64, BF16)
    cb = Carver(AR_B, max(B1, B2) + 64)
    xfm = cb.take(2 * DC * GT)
    Qe = cb.take(DC * GT)
    kbf = cb.take(2 * D)
    vaug = cb.take(2 * NH * 65)
    ctxbf = cb.take(2 * D)
    ctxfm = cb.take(2 * D)
    wq_bf = cb.take(2 * D)
    wtile = cb.take(2 * 1024)
    kvaug = cb.take(cfg.NSEQ * NPAIR * 130)
    x1t = cb.take(2 * D)
    cb2 = Carver(AR_B, max(B1, B2) + 64)
    negmu_r = cb2.row(T)
    x1fm = cb2.take(2 * DC * GT)
    hgrp = cb2.take(HC * GT)
    hqt = cb2.take(2 * 512)
    cb3 = Carver(AR_B, max(B1, B2) + 64)
    cb3.off = T
    hld = cb3.take(2 * HC * 256)
    x1t_b = cb3.take(2 * D)

    F1 = 1024 + 2048 + 2048 + 1024 + 1024 + 1024 + 1024 + 2048 + 1024 + 1024 + NPAIR * 65
    F2 = T + T + 1024 + 1024 + 2048 + 1024
    AR_F = sb("AR_F", 128, max(F1, F2, 2 * HFF + 2048 + 2 * D) + 32, F32)
    cf = Carver(AR_F, max(F1, F2, 2 * HFF + 2048 + 2 * D) + 32)
    eluu = cf.take(1024)
    elue = cf.take(2048)
    xf32 = cf.take(2 * D)
    xbeta = cf.take(D)
    v0t = cf.take(D)
    nt = cf.take(D)
    v1t = cf.take(D)
    wnat = cf.take(2048)
    qs1 = cf.take(1024)
    qs2 = cf.take(1024)
    raug_f = cf.take(NPAIR * 65)
    lnwrow = AR_F[0:1, 0:HFF]
    lnbrow = AR_F[0:1, HFF:2 * HFF]
    gbrow = AR_F[0:1, 2 * HFF:2 * HFF + 2 * D]
    cf2 = Carver(AR_F, max(F1, F2, 2 * HFF + 2048 + 2 * D) + 32)
    sumh_r = cf2.row(T)
    sumq_r = cf2.row(T)
    x1c = cf2.take(D)
    v2t = cf2.take(D)
    outt = cf2.take(2 * D)
    c0row = cf2.row(D)


    AR_W = sb("AR_W", 128, DC * max(HFF, 4 * D), BF16)
    WkT = AR_W[:, 0 * DC * D:1 * DC * D]
    WvT = AR_W[:, 1 * DC * D:2 * DC * D]
    WqT = AR_W[:, 2 * DC * D:3 * DC * D]
    WdT = AR_W[:, 3 * DC * D:4 * DC * D]
    WfcT = AR_W[:, 0:DC * HFF]
    wT_map = {"Wk": WkT, "Wv": WvT, "Wq": WqT, "Wd": WdT}

    pA = ps("pA", 512)
    pB = ps("pB", 512)
    pC = ps("pC", 512)
    pD = ps("pD", 512)
    pR1 = ps("pR1", 512)
    pR2 = ps("pR2", 512)
    pS1 = ps("pS1", 512)
    pS2 = ps("pS2", 512)

    def t_ms(fn, deps=()):
        return sch.op("T", fn, deps, inc=True)

    def chain_mm(mms, deps=()):
        tok = None
        for i, f in enumerate(mms):
            last = i == len(mms) - 1
            if i == 0:
                tok = sch.op("T", f, deps, inc=last)
            else:
                tok = sch.op("T", f, inc=last)
        return tok

    v_ones = sch.op("V", lambda e: e.memset(ones_col[:, :], 1.0), inc=True)
    sch.op("V", lambda e: e.memset(ones_row[:, :], 1.0))
    sch.op("V", lambda e: e.memset(onesf_row[:, :], 1.0))
    sch.op("V", lambda e: e.memset(onesf_col[:, :], 1.0))
    sch.op("V", lambda e: e.memset(onef[:, :], 1.0))
    sch.op("V", lambda e: e.memset(kvaug[:, :], 0.0))
    sch.op("V", lambda e: e.memset(vaug[:, :], 1.0))
    sch.op("V", lambda e: e.memset(epsc[:, 0:1], 1e-12))
    v_eps = sch.op("V", lambda e: e.memset(epsc[:, 1:2], 1e-5), inc=True)
    v_init = sch.op("V", lambda e: e.memset(accq[:, :], 0.0), inc=True)

    d_b = {}
    for i, b in enumerate(("bq", "bk", "bv", "bd")):
        d_b[b] = sch.op("G", lambda e, i=i, b=b: e.dma_start(
            brows[0:1, D * i:D * (i + 1)], b_attn[b][:, :]), dma=True)
    d_gam = sch.op("S", lambda e: e.dma_start(gbrow[0:1, 0:D], gam_h[:, :]), dma=True)
    d_bet = sch.op("S", lambda e: e.dma_start(gbrow[0:1, D:2 * D], bet_h[:, :]), dma=True)
    d_lnw = sch.op("S", lambda e: e.dma_start(lnwrow, lnw_h[:, :]), dma=True)
    d_lnb = sch.op("S", lambda e: e.dma_start(lnbrow, lnb_h[:, :]), dma=True)

    gb_last = None
    for j in range(2 * D // 512):
        deps = [d_gam, d_bet, v_init] + ([gb_last] if gb_last else [])
        tk = t_ms(lambda e, j=j: e.matmul(pS1[:, 0:512], onesf_row[0:1, :],
                                          gbrow[0:1, 512 * j:512 * (j + 1)],
                                          start=True, stop=True), deps=deps)
        gb_last = sch.op("V", lambda e, j=j: e.tensor_copy(
            gb_bc[:, 512 * j:512 * (j + 1)], pS1[:, 0:512]), deps=[tk], inc=True)
    v_gb = gb_last
    v_sc = sch.op("V", lambda e: e.tensor_tensor_scan(
        idr[0:1, :], onesf_row[0:1, :], onesf_row[0:1, :], 1.0,
        ALU.add, ALU.mult), inc=True)
    t_idc = t_ms(lambda e: e.transpose(pS2[:, 2:3], idr[0:1, :], onef[0:1, 0:1]),
                 deps=[v_sc])
    v_idc = sch.op("V", lambda e: e.tensor_copy(idc[:, 0:1], pS2[:, 2:3]),
                   deps=[t_idc], inc=True)
    t_idb = t_ms(lambda e: e.matmul(pS1[:, 0:128], onesf_row[0:1, :], idr[0:1, :],
                                    start=True, stop=True), deps=[v_sc, v_gb])
    v_id = sch.op("V", lambda e: e.tensor_scalar(
        idmat[:, :], pS1[:, 0:128], idc[:, 0:1], None, ALU.is_equal),
        deps=[t_idb, v_idc], inc=True)

    col_last = None
    for k in range(HC):
        deps = [d_lnw, v_init] + ([col_last] if col_last else [])
        t1 = t_ms(lambda e, k=k: e.transpose(pS2[:, 0:1], lnwrow[0:1, 128 * k:128 * (k + 1)],
                                             onef[0:1, 0:1]), deps=deps)
        c1 = sch.op("V", lambda e, k=k: e.tensor_copy(lnw_cols[:, k:k + 1], pS2[:, 0:1]),
                    deps=[t1], inc=True)
        t2 = t_ms(lambda e, k=k: e.transpose(pS2[:, 1:2], lnbrow[0:1, 128 * k:128 * (k + 1)],
                                             onef[0:1, 0:1]), deps=[d_lnb, c1])
        col_last = sch.op("V", lambda e, k=k: e.tensor_copy(lnb_cols[:, k:k + 1], pS2[:, 1:2]),
                          deps=[t2], inc=True)
    v_cols = col_last

    d_xbf = []
    for g in range(NG):
        d_xbf.append(sch.op("G", lambda e, g=g: e.dma_start(
            xbf_h[GT * g:GT * (g + 1), :], x_h[GT * g:GT * (g + 1), :]), dma=True))

    wq_slot_free = [[], []]
    w_done = {}
    sbt_chain = [None]
    guard_pS = [None]


    w_staged = {}

    def emit_weight_oc(wname, oc, unit_idx):
        slot = unit_idx % 2
        d = sch.op("G", lambda e, wname=wname, oc=oc, slot=slot: e.dma_start(
            wq_bf[:, 1024 * slot:1024 * (slot + 1)],
            W_attn[wname][128 * oc:128 * (oc + 1), :]), dma=True,
            deps=wq_slot_free[slot])
        st = sch.op("G", lambda e, wname=wname, oc=oc, slot=slot: e.dma_start(
            wbf_h[wname][128 * oc:128 * (oc + 1), :],
            wq_bf[:, 1024 * slot:1024 * (slot + 1)]), dma=True, deps=[d])
        wq_slot_free[slot] = [st]
        w_staged.setdefault(wname, []).append(st)

    def emit_weight_loads(wname):
        WT = wT_map[wname]
        toks = []
        for k in range(DC):
            for oc in range(DC):
                q = "S"
                toks.append(sch.op(q, lambda e, WT=WT, k=k, oc=oc, wname=wname: e.dma_start(
                    WT[:, D * k + 128 * oc: D * k + 128 * oc + 128],
                    wbf_h[wname][128 * oc:128 * (oc + 1), 128 * k:128 * (k + 1)],
                    transpose=True), deps=w_staged[wname], dma=True))
        w_done[wname] = toks

    ui = 0
    for oc in range(DC):
        emit_weight_oc("Wk", oc, ui); ui += 1
        emit_weight_oc("Wv", oc, ui); ui += 1
    emit_weight_loads("Wk")
    emit_weight_loads("Wv")

    from collections import deque
    bg = deque()
    for oc in range(DC):
        bg.append(("wq", oc))
    bg.append(("wqload", 0))
    for oc in range(DC):
        bg.append(("wd", oc))
    bg.append(("wdload", 0))

    bgstate = dict(ui=ui, am_last=v_cols, qf_last=[[], []], qp_last=[[], []],
                   wfcq_done=None, wprq_done=None, am_fc_tok=None, am_pr_tok=None,
                   v_thr=None)

    def emit_absmean(i):
        slot = bgstate["ui"] % 2
        bgstate["ui"] += 1
        if i < RT:
            d = sch.op("G", lambda e, i=i, slot=slot: e.dma_start(
                wnat[:, 1024 * slot:1024 * (slot + 1)], Wfc_h[128 * i:128 * (i + 1), :]),
                dma=True, deps=[bgstate["am_last"]])
            v = sch.op("V", lambda e, i=i, slot=slot: e.reduce_sum(
                redq[:, i:i + 1], wnat[:, 1024 * slot:1024 * (slot + 1)], AX.X,
                apply_absolute_value=True), deps=[d], inc=True)
            bgstate["am_last"] = v
        else:
            ii = i - RT
            nps = HFF // 1024
            vs = []
            for pcs in range(nps):
                slot2 = bgstate["ui"] % 2
                bgstate["ui"] += 1
                d = sch.op("G", lambda e, ii=ii, pcs=pcs, slot2=slot2: e.dma_start(
                    wnat[:, 1024 * slot2:1024 * (slot2 + 1)],
                    Wpr_h[128 * ii:128 * (ii + 1), 1024 * pcs:1024 * (pcs + 1)]),
                    dma=True, deps=[bgstate["am_last"]])
                v = sch.op("V", lambda e, ii=ii, pcs=pcs, slot2=slot2: e.reduce_sum(
                    qs1[:, ii * nps + pcs:ii * nps + pcs + 1],
                    wnat[:, 1024 * slot2:1024 * (slot2 + 1)], AX.X,
                    apply_absolute_value=True), deps=[d], inc=True)
                bgstate["am_last"] = v
                vs.append(v)
            v = sch.op("V", lambda e, ii=ii, nps=nps: e.reduce_sum(
                redq[:, RT + ii:RT + ii + 1], qs1[:, ii * nps:(ii + 1) * nps], AX.X),
                deps=vs, inc=True)
            bgstate["am_last"] = v
        if i == RT - 1:
            bgstate["am_fc_tok"] = sch.op("V", lambda e: e.reduce_sum(
                accq[:, 0:1], redq[:, 0:RT], AX.X), deps=[bgstate["am_last"]], inc=True)
        if i == RT + DC - 1:
            bgstate["am_pr_tok"] = sch.op("V", lambda e: e.reduce_sum(
                accq[:, 1:2], redq[:, RT:RT + DC], AX.X), deps=[bgstate["am_last"]], inc=True)

    def emit_thr():
        tg1 = t_ms(lambda e: e.matmul(pS2[0:1, 4:5], accq[:, 0:1], onesf_col[:, 0:1],
                                      start=True, stop=True),
                   deps=[bgstate["am_fc_tok"], v_cols, guard_pS[0]])
        tg2 = t_ms(lambda e: e.matmul(pS2[0:1, 5:6], accq[:, 1:2], onesf_col[:, 0:1],
                                      start=True, stop=True), deps=[bgstate["am_pr_tok"]])
        v1 = sch.op("V", lambda e: e.tensor_scalar(
            smallf[0:1, 0:1], pS2[0:1, 4:5], 0.5 / (HFF * D), 0.5e-6, ALU.mult, ALU.add),
            deps=[tg1], inc=True)
        v2 = sch.op("V", lambda e: e.tensor_scalar(
            smallf[0:1, 1:2], pS2[0:1, 5:6], 0.5 / (HFF * D), 0.5e-6, ALU.mult, ALU.add),
            deps=[tg2], inc=True)
        tb = t_ms(lambda e: e.matmul(pS2[:, 8:10], onesf_row[0:1, :], smallf[0:1, 0:2],
                                     start=True, stop=True), deps=[v1, v2])
        v3 = sch.op("V", lambda e: e.tensor_copy(thr_cols[:, 0:2], pS2[:, 8:10]),
                    deps=[tb], inc=True)
        v4 = sch.op("V", lambda e: e.tensor_scalar(thr_cols[:, 2:4], thr_cols[:, 0:2],
                                                   -1.0, None, ALU.mult), inc=True)
        bgstate["v_thr"] = v4

    def emit_quant(kind, idx):
        slot = idx % 2
        if kind == "qf":
            W_src = Wfc_h[128 * idx:128 * (idx + 1), :]
            dst = wfcq_h[128 * idx:128 * (idx + 1), :]
            thr_i = 0
            lastlist = bgstate["qf_last"]
        else:
            ii, pcs = divmod(idx, HFF // 1024)
            W_src = Wpr_h[128 * ii:128 * (ii + 1), 1024 * pcs:1024 * (pcs + 1)]
            dst = wprq_h[128 * ii:128 * (ii + 1), 1024 * pcs:1024 * (pcs + 1)]
            thr_i = 1
            lastlist = bgstate["qp_last"]
        dep = [lastlist[slot], bgstate["am_last"]]
        d = sch.op("G", lambda e, W_src=W_src, slot=slot: e.dma_start(
            wnat[:, 1024 * slot:1024 * (slot + 1)], W_src), dma=True, deps=dep)
        bgstate.setdefault(kind + "_loads", []).append(d)
        sch.op("V", lambda e, slot=slot: e.tensor_scalar(
            qs1[:, 0:1024], wnat[:, 1024 * slot:1024 * (slot + 1)],
            thr_cols[:, thr_i:thr_i + 1], None, ALU.is_ge),
            deps=[d, bgstate["v_thr"]])
        sch.op("V", lambda e, slot=slot: e.tensor_scalar(
            qs2[:, 0:1024], wnat[:, 1024 * slot:1024 * (slot + 1)],
            thr_cols[:, 2 + thr_i:3 + thr_i], None, ALU.is_le))
        v3 = sch.op("V", lambda e, slot=slot: e.tensor_tensor(
            wq_bf[:, 1024 * slot:1024 * (slot + 1)], qs1[:, 0:1024], qs2[:, 0:1024],
            ALU.subtract), deps=[wq_slot_free[slot]], inc=True)
        st = sch.op("G", lambda e, dst=dst, slot=slot: e.dma_start(
            dst, wq_bf[:, 1024 * slot:1024 * (slot + 1)]), deps=[v3], dma=True)
        lastlist[slot] = [st]
        wq_slot_free[slot] = [st]
        bgstate.setdefault(kind + "_stores", []).append(st)
        return st

    nps_pr = HFF // 1024
    for i in range(RT + DC):
        bg.append(("am", i))
    bg.append(("thr", 0))
    for i in range(RT):
        bg.append(("qf", i))
    for i in range(DC * nps_pr):
        bg.append(("qp", i))

    def drain_bg(n):
        for _ in range(n):
            if not bg:
                return
            kind, i = bg.popleft()
            if kind == "wq":
                emit_weight_oc("Wq", i, bgstate["ui"]); bgstate["ui"] += 1
            elif kind == "wqload":
                emit_weight_loads("Wq")
            elif kind == "wd":
                emit_weight_oc("Wd", i, bgstate["ui"]); bgstate["ui"] += 1
            elif kind == "wdload":
                emit_weight_loads("Wd")
            elif kind == "am":
                emit_absmean(i)
            elif kind == "thr":
                emit_thr()
            elif kind == "qf":
                emit_quant("qf", i)
                if i == RT - 1:
                    bgstate["wfcq_done"] = bgstate["qf_stores"]
            elif kind == "qp":
                emit_quant("qp", i)
                if i == DC * nps_pr - 1:
                    bgstate["wprq_done"] = bgstate["qp_stores"]

    def drain_all_bg():
        while bg:
            drain_bg(1)

    def win_loads(dst, src_h, g, slot, deps):
        toks = []
        for tc in range(CPG):
            row0 = GT * g + 128 * tc
            for k in range(DC):
                q = "S"
                toks.append(sch.op(q, lambda e, dst=dst, src_h=src_h, row0=row0, k=k, tc=tc,
                                   slot=slot: e.dma_start(
                                       dst[:, slot * DC * GT + GT * k + 128 * tc:
                                           slot * DC * GT + GT * k + 128 * tc + 128],
                                       src_h[row0:row0 + 128, 128 * k:128 * (k + 1)],
                                       transpose=True), deps=deps, dma=True))
        return toks

    xfm_st = {}
    xfm_free = [None, None]
    k_ps_free = None
    v_ps_free = None
    elue_free = [None, None]
    kbf_free = [None, None]
    vaug_free = [None, None]
    kv_seq_tok = {}
    last_kv_mm = None
    kv_acc = None

    for g in range(NG):
        seq = g // GPS
        slot = g % 2
        wl = win_loads(xfm, xbf_h, g, slot,
                       [d_xbf[g], v_cols] + ([xfm_free[slot]] if xfm_free[slot] else []))
        st_xfm = sch.op("G", lambda e, g=g, slot=slot: e.dma_start(
            xfm_h2[:, 4096 * g:4096 * (g + 1)],
            xfm[:, slot * DC * GT:slot * DC * GT + 4096]), deps=wl, dma=True)
        xfm_st[g] = st_xfm
        if KDUMP and g == 0:
            dxf_t = sch.op("G", lambda e: e.dma_start(dxf_h[:, :], xfm[:, 0:4096]),
                           deps=wl, dma=True)
            wl = wl + [dxf_t]
        xoff = slot * DC * GT
        for tc in range(CPG):
            c = g * CPG + tc
            cs = c % CPS
            pp = c % 2
            mms = []
            for k in range(DC):
                for half, bank in ((0, pA), (1, pB)):
                    def mk(e, k=k, bank=bank, half=half, tc=tc, xoff=xoff):
                        return e.matmul(bank[:, 0:512],
                                        xfm[:, xoff + GT * k + 128 * tc: xoff + GT * k + 128 * tc + 128],
                                        WkT[:, D * k + 512 * half: D * k + 512 * (half + 1)],
                                        start=(k == 0), stop=False)
                    mms.append(mk)
            for half, bank in ((0, pA), (1, pB)):
                def mkb(e, bank=bank, half=half):
                    return e.matmul(bank[:, 0:512], ones_row[0:1, 0:128],
                                    brows[0:1, D + 512 * half: D + 512 * (half + 1)],
                                    start=False, stop=True)
                mms.append(mkb)
            tok_kmm = chain_mm(mms, deps=[wl, w_done["Wk"],
                                          d_b["bk"], v_ones, k_ps_free])
            mms = []
            for k in range(DC):
                for half, bank in ((0, pC), (1, pD)):
                    def mv_(e, k=k, bank=bank, half=half, tc=tc, xoff=xoff):
                        return e.matmul(bank[:, 0:512],
                                        xfm[:, xoff + GT * k + 128 * tc: xoff + GT * k + 128 * tc + 128],
                                        WvT[:, D * k + 512 * half: D * k + 512 * (half + 1)],
                                        start=(k == 0), stop=False)
                    mms.append(mv_)
            for half, bank in ((0, pC), (1, pD)):
                def mvb(e, bank=bank, half=half):
                    return e.matmul(bank[:, 0:512], ones_row[0:1, 0:128],
                                    brows[0:1, 2 * D + 512 * half: 2 * D + 512 * (half + 1)],
                                    start=False, stop=True)
                mms.append(mvb)
            tok_vmm = chain_mm(mms, deps=[w_done["Wv"], d_b["bv"],
                                          v_ps_free])

            a_u1 = sch.op("A", lambda e: e.activation(
                eluu[:, 0:512], pA[:, 0:512], AF.Relu, scale=-1.0), deps=[tok_kmm])
            a_u2 = sch.op("A", lambda e: e.activation(
                eluu[:, 512:1024], pB[:, 0:512], AF.Relu, scale=-1.0), inc=True)
            sch.op("A", lambda e, pp=pp: e.activation(
                elue[:, D * pp:D * pp + 512], eluu[:, 0:512], AF.Exp, scale=-1.0),
                deps=[elue_free[pp]])
            a_e2 = sch.op("A", lambda e, pp=pp: e.activation(
                elue[:, D * pp + 512:D * pp + 1024], eluu[:, 512:1024], AF.Exp, scale=-1.0),
                inc=True)
            p1_last_a = a_e2
            sch.op("V", lambda e, pp=pp: e.scalar_tensor_tensor(
                kbf[:, D * pp:D * pp + 512], pA[:, 0:512], 0.0,
                elue[:, D * pp:D * pp + 512], ALU.max, ALU.add),
                deps=[a_e2, kbf_free[pp]])
            v_k2 = sch.op("V", lambda e, pp=pp: e.scalar_tensor_tensor(
                kbf[:, D * pp + 512:D * pp + 1024], pB[:, 0:512], 0.0,
                elue[:, D * pp + 512:D * pp + 1024], ALU.max, ALU.add), inc=True)
            k_ps_free = v_k2
            elue_free[pp] = v_k2
            def vev1(e, pp=pp):
                dst = vaug[:, pp * NH * 65:(pp + 1) * NH * 65].rearrange(
                    "p (h c) -> p h c", h=NH)[:, 0:8, 0:64]
                return e.tensor_copy(dst, pC[:, 0:512].rearrange("p (h c) -> p h c", h=8))
            sch.op("V", vev1, deps=[tok_vmm, vaug_free[pp]])
            def vev2(e, pp=pp):
                dst = vaug[:, pp * NH * 65:(pp + 1) * NH * 65].rearrange(
                    "p (h c) -> p h c", h=NH)[:, 8:16, 0:64]
                return e.tensor_copy(dst, pD[:, 0:512].rearrange("p (h c) -> p h c", h=8))
            v_v2 = sch.op("V", vev2, inc=True)
            v_ps_free = v_v2
            if cs == 0:
                kv_acc = sch.op("V", lambda e: e.memset(raug_f[:, :], 0.0),
                                deps=[kv_seq_tok.get(seq - 1)])
            mms = []
            for hh in range(NH):
                j = hh // 2
                half = hh % 2
                bank = pR1 if hh < 8 else pR2
                off = 65 * (j % 4)
                def mkv(e, hh=hh, half=half, bank=bank, off=off, pp=pp):
                    kw = dict(tile_position=(0, 64)) if half == 1 else {}
                    return e.matmul(
                        bank[64 * half:64 * (half + 1), off:off + 65],
                        kbf[:, D * pp + 64 * hh: D * pp + 64 * (hh + 1)],
                        vaug[:, pp * NH * 65 + 65 * hh: pp * NH * 65 + 65 * (hh + 1)],
                        start=True, stop=True, **kw)
                mms.append(mkv)
            kv_deps = [v_k2, v_v2, kv_acc]
            tok_kv = chain_mm(mms, deps=kv_deps)
            sch.op("V", lambda e: e.tensor_tensor(
                raug_f[:, 0:260], raug_f[:, 0:260], pR1[:, 0:260], ALU.add),
                deps=[tok_kv])
            kv_acc = sch.op("V", lambda e: e.tensor_tensor(
                raug_f[:, 260:520], raug_f[:, 260:520], pR2[:, 0:260], ALU.add))
            kbf_free[pp] = tok_kv
            vaug_free[pp] = tok_kv
            last_kv_mm = kv_acc
            if KDUMP and g == 0 and tc == 0:
                dt1 = sch.op("G", lambda e: e.dma_start(dkb_h[:, :], kbf[:, :]),
                             deps=[v_k2], dma=True)
                dt2 = sch.op("G", lambda e: e.dma_start(dvg_h[:, :], vaug[:, :]),
                             deps=[v_v2], dma=True)
                kbf_free[pp] = [tok_kv, dt1]
                vaug_free[pp] = [tok_kv, dt2]
            if tc == CPG - 1:
                xfm_free[slot] = [tok_kv, xfm_st[g]]
        if (g + 1) % GPS == 0:
            def veps(e):
                ap = raug_f[:, :].rearrange("p (j c) -> p j c", j=NPAIR)[:, :, 64:65]
                return e.tensor_scalar(ap, ap, 1e-6, None, ALU.add)
            v_eps_r = sch.op("V", veps, inc=True)
            base = seq * NPAIR * 130
            for j in range(NPAIR):
                sch.op("V", lambda e, j=j, base=base: e.tensor_copy(
                    kvaug[0:64, base + 130 * j: base + 130 * j + 65],
                    raug_f[0:64, 65 * j:65 * (j + 1)]), deps=[v_eps_r])
                tok_cp = sch.op("V", lambda e, j=j, base=base: e.tensor_copy(
                    kvaug[64:128, base + 130 * j + 65: base + 130 * (j + 1)],
                    raug_f[64:128, 65 * j:65 * (j + 1)]), inc=(j == NPAIR - 1))
            kv_seq_tok[seq] = tok_cp
        drain_bg(9)

    if KDUMP:
        sch.op("G", lambda e: e.dma_start(dkv_h[:, :], kvaug[:, :]),
               deps=[kv_seq_tok[cfg.NSEQ - 1]], dma=True)
        sch.op("G", lambda e: e.dma_start(dwk_h[:, :], WkT[:, :]),
               deps=[w_done["Wk"]], dma=True)
    p1_kv_done = kv_seq_tok[cfg.NSEQ - 1]
    p1_T_done = last_kv_mm
    p1_A_done = p1_last_a
    if upto == "p1":
        finish([p1_kv_done, p1_T_done, p1_A_done])
        es.close()
        return nc
    if upto == "setup":
        finish([v_cols, v_gb, p1_kv_done])
        es.close()
        return nc

    xfm_free2 = dict(enumerate(xfm_free))
    qe_free = None
    qpsum_free = v_id
    ctx_ps_free = p1_kv_done
    ctx_ps_free_a = p1_A_done
    xf32_free = [[v_gb, v_cols], [v_gb, v_cols]]
    x1t_free = [None, None]
    nt_free = None
    v0_free = None
    v1_free = None
    ctxbf_free = [[], []]
    ctxfm_free = [None, None]
    ctx_pr = {0: [p1_kv_done], 1: [p1_kv_done]}
    ctx_pr_par = [0]
    rden_free = [None, None]
    mv_free = None
    qelu_free = [None, None]
    x1_store_toks = []

    p2a_stage = 99
    if isinstance(upto, tuple) and upto[0] == "p2a":
        p2a_stage = upto[1]
        upto = "p2a"
    stage_toks = []
    for g in range(NG):
        seq = g // GPS
        slot = g % 2
        wl = [sch.op("A", lambda e, g=g, slot=slot: e.dma_start(
            xfm[:, slot * DC * GT:slot * DC * GT + 4096],
            xfm_h2[:, 4096 * g:4096 * (g + 1)]),
            deps=[xfm_st[g]] + ([xfm_free2.get(slot)] if xfm_free2.get(slot) else []),
            dma=True)]
        xoff = slot * DC * GT
        q_v_toks = []
        for oc in range(DC):
            bank = pS1 if oc % 2 == 0 else pS2
            mms = []
            for k in range(DC):
                def mq(e, oc=oc, k=k, bank=bank, xoff=xoff):
                    return e.matmul(bank[:, 0:512],
                                    WqT[:, D * k + 128 * oc: D * k + 128 * oc + 128],
                                    xfm[:, xoff + GT * k: xoff + GT * (k + 1)],
                                    start=(k == 0), stop=False)
                mms.append(mq)
            def mqb(e, oc=oc, bank=bank):
                return e.matmul(bank[:, 0:512], brows[0:1, 128 * oc:128 * (oc + 1)],
                                ones_row[0:1, 0:512], start=False, stop=True)
            mms.append(mqb)
            qdeps = [wl, w_done["Wq"], d_b["bq"], v_ones,
                     v_gb, v_cols, qpsum_free]
            if len(q_v_toks) >= 2:
                qdeps.append(q_v_toks[-2])
            tok_q = chain_mm(mms, deps=qdeps)
            if p2a_stage < 2:
                stage_toks.append(tok_q)
                qpsum_free = tok_q
                continue
            op_ = oc % 2
            sch.op("A", lambda e, bank=bank, op_=op_: e.activation(
                eluu[:, 512 * op_:512 * (op_ + 1)], bank[:, 0:512], AF.Relu, scale=-1.0),
                deps=[tok_q, p1_kv_done, qelu_free[op_]])
            a2 = sch.op("A", lambda e, op_=op_: e.activation(
                elue[:, 512 * op_:512 * (op_ + 1)], eluu[:, 512 * op_:512 * (op_ + 1)],
                AF.Exp, scale=-1.0), deps=[qelu_free[op_]], inc=True)
            vq = sch.op("V", lambda e, oc=oc, bank=bank, op_=op_: e.scalar_tensor_tensor(
                Qe[:, GT * oc:GT * oc + 512], bank[:, 0:512], 0.0,
                elue[:, 512 * op_:512 * (op_ + 1)], ALU.max, ALU.add),
                deps=[a2, qe_free], inc=True)
            q_v_toks.append(vq)
            qelu_free[op_] = vq
        xfm_free2[slot] = tok_q
        if KDUMP and g == 0:
            dqe_t = sch.op("G", lambda e: e.dma_start(dqe_h[:, :], Qe[:, 0:4096]),
                           deps=q_v_toks, dma=True)
            qe_free = [dqe_t, qe_free]

        if p2a_stage < 3:
            continue
        for tc in range(CPG):
            c = g * CPG + tc
            pp = c % 2
            d_x32 = sch.op("A", lambda e, c=c, pp=pp: e.dma_start(
                xf32[:, D * pp:D * (pp + 1)], x_h[128 * c:128 * (c + 1), :]),
                dma=True, deps=[xf32_free[pp]])
            v_xb = sch.op("V", lambda e, pp=pp: e.tensor_tensor(
                xbeta[:, 0:D], xf32[:, D * pp:D * (pp + 1)],
                gb_bc[:, D:2 * D], ALU.add), deps=[d_x32, v_gb], inc=True)
            mms = []
            for j in range(NPAIR):
                bank = (pA, pB, pC)[j // 3]
                off = 130 * (j % 3)
                base = seq * NPAIR * 130
                def mc(e, j=j, bank=bank, off=off, base=base, tc=tc):
                    return e.matmul(bank[:, off:off + 130],
                                    Qe[:, GT * j + 128 * tc: GT * j + 128 * tc + 128],
                                    kvaug[:, base + 130 * j: base + 130 * (j + 1)],
                                    start=True, stop=True)
                mms.append(mc)
            cdeps = list(q_v_toks) + [p1_kv_done, ctx_ps_free, ctx_ps_free_a]
            tok_ctx = chain_mm(mms, deps=cdeps)
            if tc == CPG - 1:
                qe_free = tok_ctx
            if p2a_stage < 4:
                stage_toks.append(tok_ctx)
                ctx_ps_free = tok_ctx
                ctx_ps_free_a = None
                continue
            for j in range(NPAIR):
                bank = (pA, pB, pC)[j // 3]
                off = 130 * (j % 3)
                def vr(e, j=j, bank=bank, off=off, pp=pp):
                    src = bank[:, off:off + 130].rearrange(
                        "p (a b) -> p a b", a=2)[:, :, 64:65].squeeze(2)
                    return e.reciprocal(rden[:, pp * NH + 2 * j: pp * NH + 2 * j + 2], src)
                v_rc = sch.op("V", vr, deps=[tok_ctx, rden_free[pp]],
                              inc=(j == NPAIR - 1))
            if p2a_stage < 5:
                stage_toks.append(v_rc)
                ctx_ps_free = v_rc
                ctx_ps_free_a = None
                rden_free[pp] = v_rc
                continue
            for j in range(NPAIR):
                bank = (pA, pB, pC)[j // 3]
                off = 130 * (j % 3)
                for half in range(2):
                    hh = 2 * j + half
                    v_ctxe = sch.op("V", lambda e, bank=bank, off=off, half=half,
                                    hh=hh, pp=pp: e.tensor_scalar(
                                        ctxbf[:, D * pp + 64 * hh: D * pp + 64 * (hh + 1)],
                                        bank[:, off + 65 * half: off + 65 * half + 64],
                                        rden[:, pp * NH + hh: pp * NH + hh + 1],
                                        None, ALU.mult),
                                    deps=[v_rc] + ctxbf_free[pp], inc=True)
            a_ctxe = v_ctxe
            if KDUMP and c == 0:
                dcx_t = sch.op("G", lambda e: e.dma_start(dcx_h[:, :], ctxbf[:, 0:D]),
                               deps=[v_ctxe], dma=True)
                v_ctxe = [v_ctxe, dcx_t]
                a_ctxe = v_ctxe
            ctx_ps_free, ctx_ps_free_a = v_ctxe, a_ctxe
            rden_free[pp] = v_ctxe
            if p2a_stage < 6:
                stage_toks.append(v_ctxe)
                stage_toks.append(a_ctxe)
                ctxbf_free[pp] = [v_ctxe, a_ctxe]
                continue
            ct_toks = []
            for k in range(DC):
                p = ctx_pr_par[0]
                bank = pR1 if p == 0 else pR2
                t = sch.op("T", lambda e, k=k, pp=pp, bank=bank: e.matmul(
                    bank[:, 0:128], ctxbf[:, D * pp + 128 * k: D * pp + 128 * (k + 1)],
                    idmat[:, :], start=True, stop=True),
                    deps=[v_id, v_ctxe, a_ctxe] + ctx_pr[p], inc=True)
                v = sch.op("V", lambda e, k=k, pp=pp, bank=bank: e.tensor_copy(
                    ctxfm[:, D * pp + 128 * k: D * pp + 128 * (k + 1)], bank[:, 0:128]),
                    deps=[t, ctxfm_free[pp]], inc=True)
                ctx_pr[p] = [v]
                ctx_pr_par[0] ^= 1
                ct_toks.append(v)
            ctxbf_free[pp] = ct_toks
            if p2a_stage < 7:
                stage_toks.extend(ct_toks)
                continue
            mms = []
            for k in range(DC):
                for half, bank in ((0, pS1), (1, pS2)):
                    def mh(e, k=k, half=half, bank=bank, pp=pp):
                        return e.matmul(bank[:, 0:512],
                                        ctxfm[:, D * pp + 128 * k: D * pp + 128 * (k + 1)],
                                        WdT[:, D * k + 512 * half: D * k + 512 * (half + 1)],
                                        start=(k == 0), stop=False)
                    mms.append(mh)
            for half, bank in ((0, pS1), (1, pS2)):
                def mhb(e, half=half, bank=bank):
                    return e.matmul(bank[:, 0:512], ones_row[0:1, 0:128],
                                    brows[0:1, 3 * D + 512 * half: 3 * D + 512 * (half + 1)],
                                    start=False, stop=True)
                mms.append(mhb)
            hdeps = [ct_toks, w_done["Wd"], d_b["bd"],
                     qpsum_free] + q_v_toks
            tok_hs = chain_mm(mms, deps=hdeps)
            ctxfm_free[pp] = tok_hs
            if p2a_stage < 8:
                stage_toks.append(tok_hs)
                qpsum_free = tok_hs
                guard_pS[0] = tok_hs
                continue
            v_v0a = sch.op("V", lambda e, pp=pp: e.tensor_tensor(
                v0t[:, 0:512], pS1[:, 0:512], xf32[:, D * pp:D * pp + 512], ALU.add),
                deps=[tok_hs, d_x32, v0_free])
            v_v0b = sch.op("V", lambda e, pp=pp: e.tensor_tensor(
                v0t[:, 512:1024], pS2[:, 0:512], xf32[:, D * pp + 512:D * pp + 1024],
                ALU.add), inc=True)
            qpsum_free = v_v0b
            guard_pS[0] = v_v0b
            sch.op("V", lambda e: e.bn_stats(stat[:, 0:6], v0t[:, 0:512]), deps=[mv_free])
            sch.op("V", lambda e: e.bn_stats(stat[:, 6:12], v0t[:, 512:1024]))
            v_ag = sch.op("V", lambda e: e.bn_aggr(mv[:, 0:2], stat[:, 0:12]), inc=True)
            sch.op("A", lambda e: e.activation(mv[:, 2:3], mv[:, 1:2], AF.Ln, bias=epsc[:, 0:1]),
                   deps=[v_ag, v_eps])
            a_r = sch.op("A", lambda e: e.activation(mv[:, 3:4], mv[:, 2:3], AF.Exp,
                                                     scale=-0.5), inc=True)
            a_n = sch.op("V", lambda e: e.tensor_scalar(
                nt[:, 0:D], v0t[:, 0:D], mv[:, 0:1], mv[:, 3:4], ALU.subtract, ALU.mult),
                deps=[a_r, nt_free], inc=True)
            v0_free = a_n
            sch.op("V", lambda e: e.tensor_tensor(nt[:, 0:D], nt[:, 0:D], gb_bc[:, 0:D],
                                                  ALU.mult), deps=[a_n])
            v_v1b = sch.op("V", lambda e: e.tensor_tensor(
                v1t[:, 0:D], nt[:, 0:D], xbeta[:, 0:D], ALU.add),
                deps=[v_xb, v1_free], inc=True)
            nt_free = v_v1b
            xf32_free[pp] = v_v1b
            sch.op("V", lambda e: e.bn_stats(stat[:, 12:18], v1t[:, 0:512]))
            sch.op("V", lambda e: e.bn_stats(stat[:, 18:24], v1t[:, 512:1024]))
            v_ag1 = sch.op("V", lambda e: e.bn_aggr(mv[:, 6:8], stat[:, 12:24]), inc=True)
            sch.op("A", lambda e: e.activation(mv[:, 2:3], mv[:, 7:8], AF.Ln, bias=epsc[:, 1:2]),
                   deps=[v_ag1, v_eps])
            a_r1 = sch.op("A", lambda e: e.activation(mv[:, 3:4], mv[:, 2:3], AF.Exp,
                                                      scale=-0.5), inc=True)
            a_x1 = sch.op("V", lambda e, pp=pp: e.tensor_scalar(
                x1t[:, D * pp:D * (pp + 1)], v1t[:, 0:D], mv[:, 6:7], mv[:, 3:4],
                ALU.subtract, ALU.mult), deps=[a_r1, x1t_free[pp]], inc=True)
            mv_free = a_x1
            v1_free = a_x1
            st = sch.op("G", lambda e, c=c, pp=pp: e.dma_start(
                x1_h[128 * c:128 * (c + 1), :], x1t[:, D * pp:D * (pp + 1)]),
                deps=[a_x1], dma=True)
            x1t_free[pp] = st
            x1_store_toks.append(st)
        drain_bg(9)

    drain_all_bg()
    if p2a_stage < 99:
        p2a_T_done = p2a_V_done = p2a_A_done = None
    else:
        p2a_T_done = tok_hs
        p2a_V_done = a_x1
        p2a_A_done = a_r1
    if upto == "p2a":
        if p2a_stage < 99:
            finish([t for t in stage_toks[-8:] if t] +
                   [bgstate["wfcq_done"], bgstate["wprq_done"]])
            es.close()
            return nc
        finish([p2a_T_done, p2a_V_done, p2a_A_done, x1_store_toks[-1],
                bgstate["wfcq_done"], bgstate["wprq_done"]])
        es.close()
        return nc

    stg_free = [None, None, None, None]
    ps_tr = {0: [qpsum_free], 1: [qpsum_free]}
    ps_par = [0]

    def pe_transpose(dst_slice, src_slice, deps_mm, deps_evac):
        p = ps_par[0]
        bank = pS1 if p == 0 else pS2
        t = sch.op("T", lambda e, bank=bank, src_slice=src_slice: e.matmul(
            bank[:, 0:128], src_slice, idmat[:, :], start=True, stop=True),
            deps=[v_id] + deps_mm + ps_tr[p], inc=True)
        v = sch.op("V", lambda e, bank=bank, dst_slice=dst_slice: e.tensor_copy(
            dst_slice, bank[:, 0:128]), deps=[t] + deps_evac, inc=True)
        ps_tr[p] = [v]
        ps_par[0] ^= 1
        return t, v

    wfcT_by_hc = {}
    wfcT_loads = []
    for hc in range(HC):
        s = hc % 4
        ld = sch.op("A", lambda e, hc=hc, s=s: e.dma_start(
            wstg[:, 1024 * s:1024 * (s + 1)], wfcq_h[128 * hc:128 * (hc + 1), :]),
            deps=[bgstate["wfcq_done"], stg_free[s]], dma=True)
        toks = []
        t_last = None
        for k in range(DC):
            if k < 4:
                rdep = [p1_T_done]
            elif k < 6:
                rdep = [qpsum_free]
            else:
                rdep = [p2a_T_done]
            t_last, v = pe_transpose(
                WfcT[:, HFF * k + 128 * hc:HFF * k + 128 * hc + 128],
                wstg[:, 1024 * s + 128 * k:1024 * s + 128 * k + 128],
                [ld], rdep)
            toks.append(v)
        stg_free[s] = t_last
        wfcT_by_hc[hc] = toks
        wfcT_loads.extend(toks)

    if KDUMP:
        sch.op("G", lambda e: e.dma_start(dqh_h[:, :], wfcq_h[:, :]),
               deps=[bgstate["wfcq_done"]], dma=True)
        sch.op("G", lambda e: e.dma_start(dqp_h[:, :], wprq_h[:, :]),
               deps=[bgstate["wprq_done"]], dma=True)
        sch.op("G", lambda e: e.dma_start(dwf_h[:, :], WfcT[:, 0:DC * HFF]),
               deps=wfcT_loads, dma=True)

    x1fm_free = [None, None]
    fc_ps_free = [[ctx_ps_free, ctx_ps_free_a], [ctx_ps_free, ctx_ps_free_a]]
    hqt_free = [None, None]
    stat_ps_free = ctx_pr[0] + ctx_pr[1]
    hgrp_free = None
    hst_toks = []
    lastrow = {}

    for g in range(NG):
        slot = g % 2
        wl = []
        for tc in range(CPG):
            c = g * CPG + tc
            s = tc
            ld = sch.op("A", lambda e, c=c, s=s: e.dma_start(
                wstg[:, 1024 * s:1024 * (s + 1)], x1_h[128 * c:128 * (c + 1), :]),
                deps=[x1_store_toks[c], stg_free[s]], dma=True)
            t_last = None
            for k in range(DC):
                edep = [x1fm_free[slot]] if x1fm_free[slot] else []
                t_last, v = pe_transpose(
                    x1fm[:, slot * DC * GT + GT * k + 128 * tc:
                         slot * DC * GT + GT * k + 128 * tc + 128],
                    wstg[:, 1024 * s + 128 * k:1024 * s + 128 * k + 128],
                    [ld], edep)
                wl.append(v)
            stg_free[s] = t_last
        xoff = slot * DC * GT
        lastst = None
        a_gelu_last = None
        first_extra = [hgrp_free] if hgrp_free else []
        for hc in range(HC):
            bank = pA if hc % 2 == 0 else pB
            hcp = hc % 2
            mms = []
            for k in range(DC):
                def mf(e, hc=hc, k=k, bank=bank, xoff=xoff):
                    return e.matmul(bank[:, 0:512],
                                    WfcT[:, HFF * k + 128 * hc: HFF * k + 128 * hc + 128],
                                    x1fm[:, xoff + GT * k: xoff + GT * (k + 1)],
                                    start=(k == 0), stop=(k == DC - 1))
                mms.append(mf)
            fdeps = [wl, wfcT_by_hc[hc], fc_ps_free[hcp]]
            tok_fc = chain_mm(mms, deps=fdeps)
            a_g = sch.op("A", lambda e, hc=hc, bank=bank: e.activation(
                hgrp[:, GT * hc:GT * (hc + 1)], bank[:, 0:512], AF.Gelu),
                deps=[tok_fc] + (first_extra if hc == 0 else []), inc=True)
            first_extra = []
            fc_ps_free[hcp] = [a_g]
            a_q = sch.op("A", lambda e, hc=hc, hcp=hcp: e.activation(
                hqt[:, 512 * hcp:512 * (hcp + 1)], hgrp[:, GT * hc:GT * (hc + 1)],
                AF.Square), deps=[hqt_free[hcp]], inc=True)
            a_gelu_last = a_q
            def ms1(e, hc=hc):
                return e.matmul(pR1[0:1, 0:512], ones_col[:, 0:1],
                                hgrp[:, GT * hc:GT * (hc + 1)],
                                start=(hc == 0), stop=(hc == HC - 1))
            def ms2(e, hc=hc, hcp=hcp):
                return e.matmul(pR2[0:1, 0:512], ones_col[:, 0:1],
                                hqt[:, 512 * hcp:512 * (hcp + 1)],
                                start=(hc == 0), stop=(hc == HC - 1))
            sdeps = [a_g, a_q, v_ones]
            if hc == 0:
                sdeps.append(stat_ps_free)
            sch.op("T", ms1, deps=sdeps)
            t2 = sch.op("T", ms2, inc=(hc == HC - 1))
            hqt_free[hcp] = t2
            if hc == HC - 1:
                lastst = t2
        sch.op("V", lambda e, g=g: e.tensor_copy(
            sumh_r[0:1, GT * g:GT * (g + 1)], pR1[0:1, 0:512]), deps=[lastst])
        v_sr2 = sch.op("V", lambda e, g=g: e.tensor_copy(
            sumq_r[0:1, GT * g:GT * (g + 1)], pR2[0:1, 0:512]), inc=True)
        lastrow[g] = v_sr2
        stat_ps_free = v_sr2
        sth = sch.op("G", lambda e, g=g: e.dma_start(
            h_sh[:, :, GT * g:GT * (g + 1)],
            hgrp[:, :].rearrange("p (h t) -> p h t", h=HC)),
            deps=[lastst, a_gelu_last], dma=True)
        hst_toks.append(sth)
        hgrp_free = sth
        x1fm_free[slot] = tok_fc

    if KDUMP:
        sch.op("G", lambda e: e.dma_start(
            dhh_h[:, :], h_sh[:, :, :].rearrange("p h t -> p (h t)")),
            deps=hst_toks, dma=True)

    if upto == "p2bi":
        finish([lastrow[NG - 1], hst_toks[-1]] + fc_ps_free[0] + fc_ps_free[1])
        es.close()
        return nc

    W2T = WfcT
    c0_last = None
    s2_last = None
    w2_scale_last = None
    w2dep = [bgstate["wprq_done"], hst_toks[-1]] + fc_ps_free[0] + fc_ps_free[1]
    w2_by_k = {k: [] for k in range(HC)}
    for o in range(DC):
        for qtr in range(HFF // 1024):
            s = (o * (HFF // 1024) + qtr) % 4
            ld = sch.op("A", lambda e, o=o, qtr=qtr, s=s: e.dma_start(
                wstg[:, 1024 * s:1024 * (s + 1)],
                wprq_h[128 * o:128 * (o + 1), 1024 * qtr:1024 * (qtr + 1)]),
                deps=[bgstate["wprq_done"], stg_free[s]], dma=True)
            t_last = None
            for kk in range(8):
                k = 8 * qtr + kk
                t_last, v = pe_transpose(
                    W2T[:, D * k + 128 * o:D * k + 128 * (o + 1)],
                    wstg[:, 1024 * s + 128 * kk:1024 * s + 128 * kk + 128],
                    [ld], w2dep)
                w2_by_k[k].append(v)
            stg_free[s] = t_last
    for k in range(HC):
        d = w2_by_k[k]
        sch.op("T", lambda e, k=k: e.matmul(
            pR1[0:1, 0:512], lnb_cols[:, k:k + 1], W2T[:, D * k:D * k + 512],
            start=(k == 0), stop=(k == HC - 1)),
            deps=[d, v_cols, lastrow[NG - 1]])
        c0_last = sch.op("T", lambda e, k=k: e.matmul(
            pR2[0:1, 0:512], lnb_cols[:, k:k + 1], W2T[:, D * k + 512:D * (k + 1)],
            start=(k == 0), stop=(k == HC - 1)), inc=True)
        vsc = sch.op("V", lambda e, k=k: e.tensor_scalar(
            W2T[:, D * k:D * (k + 1)], W2T[:, D * k:D * (k + 1)],
            lnw_cols[:, k:k + 1], None, ALU.mult), deps=[c0_last], inc=True)
        w2_scale_last = vsc
        sch.op("T", lambda e, k=k: e.matmul(
            pC[0:1, 0:512], ones_col[:, 0:1], W2T[:, D * k:D * k + 512],
            start=(k == 0), stop=(k == HC - 1)), deps=[vsc])
        s2_last = sch.op("T", lambda e, k=k: e.matmul(
            pD[0:1, 0:512], ones_col[:, 0:1], W2T[:, D * k + 512:D * (k + 1)],
            start=(k == 0), stop=(k == HC - 1)), inc=True)
    w2_ready = w2_scale_last

    v_c0a = sch.op("V", lambda e: e.tensor_copy(c0row[0:1, 0:512], pR1[0:1, 0:512]),
                   deps=[c0_last, s2_last])
    v_c0b = sch.op("V", lambda e: e.tensor_copy(c0row[0:1, 512:1024], pR2[0:1, 0:512]),
                   inc=True)
    tb1 = t_ms(lambda e: e.matmul(pS1[:, 0:512], onesf_row[0:1, :], c0row[0:1, 0:512],
                                  start=True, stop=True), deps=[v_c0b, qpsum_free])
    v_cb1 = sch.op("V", lambda e: e.tensor_copy(c0_bc[:, 0:512], pS1[:, 0:512]),
                   deps=[tb1], inc=True)
    tb2 = t_ms(lambda e: e.matmul(pS1[:, 0:512], onesf_row[0:1, :], c0row[0:1, 512:1024],
                                  start=True, stop=True), deps=[v_cb1])
    v_cb2 = sch.op("V", lambda e: e.tensor_copy(c0_bc[:, 512:1024], pS1[:, 0:512]),
                   deps=[tb2], inc=True)
    v_s2a = sch.op("V", lambda e: e.tensor_copy(s2_row[0:1, 0:512], pC[0:1, 0:512]),
                   deps=[s2_last])
    v_s2b = sch.op("V", lambda e: e.tensor_copy(s2_row[0:1, 512:1024], pD[0:1, 0:512]),
                   inc=True)

    row_v = []
    row_a = []
    for pcs in range((T + 1023) // 1024):
        lo, hi = 1024 * pcs, min(T, 1024 * (pcs + 1))
        sch.op("V", lambda e, lo=lo, hi=hi: e.tensor_scalar(
            sumh_r[0:1, lo:hi], sumh_r[0:1, lo:hi], 1.0 / HFF, None, ALU.mult),
            deps=[lastrow[NG - 1]])
        v2 = sch.op("V", lambda e, lo=lo, hi=hi: e.tensor_scalar(
            negmu_r[0:1, lo:hi], sumh_r[0:1, lo:hi], -1.0, None, ALU.mult), inc=True)
        sch.op("V", lambda e, lo=lo, hi=hi: e.tensor_scalar(
            sumq_r[0:1, lo:hi], sumq_r[0:1, lo:hi], 1.0 / HFF, None, ALU.mult))
        sch.op("V", lambda e, lo=lo, hi=hi: e.tensor_tensor(
            c0row[0:1, 0:hi - lo], sumh_r[0:1, lo:hi], sumh_r[0:1, lo:hi], ALU.mult),
            deps=[v_cb2])
        v4 = sch.op("V", lambda e, lo=lo, hi=hi: e.tensor_tensor(
            sumq_r[0:1, lo:hi], sumq_r[0:1, lo:hi], c0row[0:1, 0:hi - lo], ALU.subtract),
            inc=True)
        row_v.append((v2, v4))

    rcol_last = None
    rows_v_done = row_v[-1][1]
    for pcs in range((T + 1023) // 1024):
        lo, hi = 1024 * pcs, min(T, 1024 * (pcs + 1))
        a1 = sch.op("A", lambda e, lo=lo, hi=hi: e.activation(
            sumq_r[0:1, lo:hi], sumq_r[0:1, lo:hi], AF.Ln, bias=epsc[0:1, 1:2]), deps=[rows_v_done, v_eps])
        a2 = sch.op("A", lambda e, lo=lo, hi=hi: e.activation(
            sumq_r[0:1, lo:hi], sumq_r[0:1, lo:hi], AF.Exp, scale=-0.5), inc=True)
        row_a.append(a2)
    rows_a_done = row_a[-1]

    for ci in range(NC):
        dep = [rows_a_done] + ([rcol_last] if rcol_last else [])
        tkt = t_ms(lambda e, ci=ci: e.transpose(pS2[:, 2:3],
                                                sumq_r[0:1, 128 * ci:128 * (ci + 1)],
                                                onef[0:1, 0:1]), deps=dep)
        rcol_last = sch.op("V", lambda e, ci=ci: e.tensor_copy(
            rff_cols[:, ci:ci + 1], pS2[:, 2:3]), deps=[tkt], inc=True)
    v_rcols = rcol_last

    NHG = T // 256
    hld_free = [None, None]
    x1tb_free = [None, None]
    x1c_free = None
    v2_free = None
    out_free = [None, None]
    proj_ps_free = [s2_last, s2_last]
    mv2_free = None
    out_toks = []

    for hgi in range(NHG):
        slot = hgi % 2
        qh = "A"
        d_h = sch.op(qh, lambda e, hgi=hgi, slot=slot: e.dma_start(
            hld[:, slot * HC * 256:(slot + 1) * HC * 256].rearrange("p (h t) -> p h t", h=HC),
            h_sh[:, :, 256 * hgi:256 * (hgi + 1)]),
            deps=[hst_toks[hgi // 2], hld_free[slot]], dma=True)
        for t2_ in range(2):
            c = 2 * hgi + t2_
            pp = c % 2
            bank1, bank2 = (pA, pB) if pp == 0 else (pC, pD)
            d_x1 = sch.op("A", lambda e, c=c, pp=pp: e.dma_start(
                x1t_b[:, D * pp:D * (pp + 1)], x1_h[128 * c:128 * (c + 1), :]),
                deps=[x1_store_toks[c], x1tb_free[pp]], dma=True)
            v_x1c = sch.op("V", lambda e, pp=pp: e.tensor_tensor(
                x1c[:, 0:D], x1t_b[:, D * pp:D * (pp + 1)], c0_bc[:, :], ALU.add),
                deps=[d_x1, v_cb2, x1c_free], inc=True)
            x1tb_free[pp] = v_x1c
            mms = []
            for k in range(HC):
                for half, bank in ((0, bank1), (1, bank2)):
                    def mp(e, k=k, half=half, bank=bank, slot=slot, t2_=t2_):
                        return e.matmul(
                            bank[:, 0:512],
                            hld[:, slot * HC * 256 + 256 * k + 128 * t2_:
                                 slot * HC * 256 + 256 * k + 128 * t2_ + 128],
                            W2T[:, D * k + 512 * half:D * k + 512 * (half + 1)],
                            start=(k == 0), stop=False)
                    mms.append(mp)
            for half, bank in ((0, bank1), (1, bank2)):
                def mpf(e, half=half, bank=bank, c=c):
                    return e.matmul(bank[:, 0:512],
                                    negmu_r[0:1, 128 * c:128 * (c + 1)],
                                    s2_row[0:1, 512 * half:512 * (half + 1)],
                                    start=False, stop=True)
                mms.append(mpf)
            pdeps = [d_h, v_s2b, rows_v_done, w2_ready, proj_ps_free[pp]]
            tok_p = chain_mm(mms, deps=pdeps)
            if t2_ == 1:
                hld_free[slot] = tok_p
            v_ya = sch.op("V", lambda e, c=c, bank1=bank1: e.scalar_tensor_tensor(
                v2t[:, 0:512], bank1[:, 0:512], rff_cols[:, c:c + 1],
                x1c[:, 0:512], ALU.mult, ALU.add),
                deps=[tok_p, v_rcols, v_x1c, v2_free])
            v_yb = sch.op("V", lambda e, c=c, bank2=bank2: e.scalar_tensor_tensor(
                v2t[:, 512:1024], bank2[:, 0:512], rff_cols[:, c:c + 1],
                x1c[:, 512:1024], ALU.mult, ALU.add), inc=True)
            proj_ps_free[pp] = v_yb
            x1c_free = v_yb
            sch.op("V", lambda e: e.bn_stats(stat[:, 0:6], v2t[:, 0:512]), deps=[mv2_free])
            sch.op("V", lambda e: e.bn_stats(stat[:, 6:12], v2t[:, 512:1024]))
            v_ag2 = sch.op("V", lambda e: e.bn_aggr(mv[:, 0:2], stat[:, 0:12]), inc=True)
            sch.op("A", lambda e: e.activation(mv[:, 2:3], mv[:, 1:2], AF.Ln, bias=epsc[:, 1:2]),
                   deps=[v_ag2, v_eps])
            a_r2 = sch.op("A", lambda e: e.activation(mv[:, 3:4], mv[:, 2:3], AF.Exp,
                                                      scale=-0.5), inc=True)
            a_o = sch.op("V", lambda e, pp=pp: e.tensor_scalar(
                outt[:, D * pp:D * (pp + 1)], v2t[:, 0:D], mv[:, 0:1], mv[:, 3:4],
                ALU.subtract, ALU.mult), deps=[a_r2, out_free[pp]], inc=True)
            mv2_free = a_o
            v2_free = a_o
            sto = sch.op("G", lambda e, c=c, pp=pp: e.dma_start(
                out_h[128 * c:128 * (c + 1), :], outt[:, D * pp:D * (pp + 1)]),
                deps=[a_o], dma=True)
            out_free[pp] = sto
            out_toks.append(sto)

    sch.op("S", lambda e: e.nop(), deps=out_toks)
    sch.build(nc)
    es.close()
    return nc


_CACHE = {}


def _get_program(cfg=None):
    key = "full" if cfg is None else (cfg.S, cfg.NSEQ, cfg.HFF)
    if key not in _CACHE:
        _CACHE[key] = build_program(cfg or Cfg())
    return _CACHE[key]


def kernel(L0_attn_input_tensor, L0_attn_attention_mask, L0_attn_Wq, L0_attn_bq,
           L0_attn_Wk, L0_attn_bk, L0_attn_Wv, L0_attn_bv, L0_attn_Wd, L0_attn_bd,
           L0_attn_gamma, L0_attn_beta, L0_ffn_W_fc, L0_ffn_ln_w, L0_ffn_ln_b,
           L0_ffn_W_proj):
    x = np.ascontiguousarray(np.asarray(L0_attn_input_tensor, dtype=np.float32))
    B, S, D = x.shape
    nc = _get_program()
    w = dict(
        Wq=np.ascontiguousarray(L0_attn_Wq, dtype=np.float32),
        Wk=np.ascontiguousarray(L0_attn_Wk, dtype=np.float32),
        Wv=np.ascontiguousarray(L0_attn_Wv, dtype=np.float32),
        Wd=np.ascontiguousarray(L0_attn_Wd, dtype=np.float32),
        bq=np.ascontiguousarray(np.asarray(L0_attn_bq, dtype=np.float32).reshape(1, D)),
        bk=np.ascontiguousarray(np.asarray(L0_attn_bk, dtype=np.float32).reshape(1, D)),
        bv=np.ascontiguousarray(np.asarray(L0_attn_bv, dtype=np.float32).reshape(1, D)),
        bd=np.ascontiguousarray(np.asarray(L0_attn_bd, dtype=np.float32).reshape(1, D)),
        gamma=np.ascontiguousarray(np.asarray(L0_attn_gamma, dtype=np.float32).reshape(1, D)),
        beta=np.ascontiguousarray(np.asarray(L0_attn_beta, dtype=np.float32).reshape(1, D)),
        Wfc=np.ascontiguousarray(L0_ffn_W_fc, dtype=np.float32),
        lnw=np.ascontiguousarray(np.asarray(L0_ffn_ln_w, dtype=np.float32).reshape(1, -1)),
        lnb=np.ascontiguousarray(np.asarray(L0_ffn_ln_b, dtype=np.float32).reshape(1, -1)),
        Wproj=np.ascontiguousarray(L0_ffn_W_proj, dtype=np.float32),
    )
    per = B // N_CORES
    in_maps = []
    for c in range(N_CORES):
        m = dict(w)
        m["x"] = np.ascontiguousarray(x[per * c:per * (c + 1)].reshape(per * S, D))
        in_maps.append(m)
    res = run_bass_kernel_spmd(nc, in_maps, core_ids=list(range(N_CORES)))
    outs = [np.asarray(r["out"]).reshape(per, S, D) for r in res.results]
    return np.concatenate(outs, axis=0).astype(np.float32)


# version 07
# speedup vs baseline: 1.0805x; 1.0071x over previous
import sys
import numpy as np

sys.path.insert(0, "/opt/trn_rl_repo")
import concourse.bass as bass
import concourse.mybir as mybir
from concourse.bass_utils import run_bass_kernel_spmd

F32 = mybir.dt.float32
BF16 = mybir.dt.bfloat16
AF = mybir.ActivationFunctionType
ALU = mybir.AluOpType
AX = mybir.AxisListType

N_CORES = 8


class Sched:
    STREAMS = ("T", "V", "A", "G", "S")
    NROT_BY = {"S": 3, "A": 3, "G": 2, "T": 1, "V": 1}

    def __init__(self):
        self.ops = {s: [] for s in self.STREAMS}
        self.cnt = {}
        self.known = {s: {} for s in self.STREAMS}
        self.dma_idx = {s: 0 for s in self.STREAMS}
        for s in self.STREAMS:
            self.cnt["c" + s] = 0

    @staticmethod
    def _flat(deps, out):
        for t in deps:
            if t is None:
                continue
            if isinstance(t, list):
                Sched._flat(t, out)
            else:
                out.append(t)

    def op(self, stream, fn, deps=(), inc=False, dma=False):
        waits = []
        k = self.known[stream]
        tok = None
        incspec = None
        if stream in ("V", "A", "G") and not dma:
            sem = "c" + stream
            prev = self.cnt[sem]
            if prev > 0:
                waits.append((sem, prev))
            inc = True
        if dma:
            i = self.dma_idx[stream]
            self.dma_idx[stream] += 1
            nrot = self.NROT_BY[stream]
            sem = "d%s%d" % (stream, i % nrot)
            base = 16 * (i // nrot)
            if base > 0 and k.get(sem, 0) < base:
                k[sem] = base
                waits.append((sem, base))
            tok = (sem, base + 16)
            incspec = (sem, 16)
        elif inc:
            sem = "c" + stream
            self.cnt[sem] += 1
            tok = (sem, self.cnt[sem])
            incspec = (sem, 1)
        flat = []
        self._flat(deps, flat)
        best = {}
        for sem, val in flat:
            if sem == "c" + stream and not dma:
                continue
            if val > best.get(sem, -1):
                best[sem] = val
        for sem in sorted(best):
            val = best[sem]
            if k.get(sem, 0) >= val:
                continue
            k[sem] = val
            waits.append((sem, val))
        self.ops[stream].append((waits, fn, incspec))
        return tok

    def build(self, nc):
        from contextlib import ExitStack
        es = ExitStack()
        sems = {}
        for s in self.STREAMS:
            sems["c" + s] = es.enter_context(nc.semaphore("sem_c" + s))
            for r in range(self.NROT_BY[s]):
                sems["d%s%d" % (s, r)] = es.enter_context(
                    nc.semaphore("sem_d%s%d" % (s, r)))
        engmap = {"T": "tensor", "V": "vector", "A": "scalar", "G": "gpsimd", "S": "sync"}
        block = es.enter_context(nc.Block())
        for s in self.STREAMS:
            def body(e, _s=s):
                for waits, fn, incspec in self.ops[_s]:
                    for sem, val in waits:
                        e.wait_ge(sems[sem], val)
                    inst = fn(e)
                    if incspec is not None:
                        inst.then_inc(sems[incspec[0]], incspec[1])
            getattr(block, engmap[s])(body)
        es.close()


class Cfg:
    def __init__(self, S=2048, NSEQ=2, HFF=4096):
        self.S, self.NSEQ, self.HFF = S, NSEQ, HFF
        self.D = 1024
        self.NH = 16
        self.T = S * NSEQ
        self.HD = 64
        self.NC = self.T // 128
        self.CPS = S // 128
        self.DC = 8
        self.HC = HFF // 128
        self.GT = 512
        self.CPG = 4
        self.NG = self.T // 512
        self.GPS = S // 512
        assert S % 512 == 0 and HFF % 1024 == 0 and self.NG >= 1


class Carver:
    def __init__(self, ap, size):
        self.ap, self.size, self.off = ap, size, 0

    def take(self, els):
        assert self.off + els <= self.size, (self.off, els, self.size)
        s = self.ap[:, self.off:self.off + els]
        self.off += els
        return s

    def row(self, els):
        assert self.off + els <= self.size
        s = self.ap[0:1, self.off:self.off + els]
        self.off += els
        return s


def build_program(cfg: Cfg, upto=None):
    nc = bass.Bass("TRN2", target_bir_lowering=False)
    S, D, NH, HFF, T = cfg.S, cfg.D, cfg.NH, cfg.HFF, cfg.T
    NC, CPS, DC, HC = cfg.NC, cfg.CPS, cfg.DC, cfg.HC
    GT, CPG, NG, GPS = cfg.GT, cfg.CPG, cfg.NG, cfg.GPS
    NPAIR = NH // 2
    RT = HFF // 128
    sch = Sched()

    _fin = {}
    def finish(deps):
        sch.op("S", lambda e: e.nop(), deps=deps)
        sch.build(nc)
        _fin["done"] = True

    def din(name, shape, dt=F32):
        return nc.dram_tensor(name, list(shape), dt, kind="ExternalInput").ap()

    x_h = din("x", (T, D))
    W_attn = {w: din(w, (D, D)) for w in ("Wq", "Wk", "Wv", "Wd")}
    b_attn = {b: din(b, (1, D)) for b in ("bq", "bk", "bv", "bd")}
    gam_h = din("gamma", (1, D)); bet_h = din("beta", (1, D))
    Wfc_h = din("Wfc", (HFF, D))
    lnw_h = din("lnw", (1, HFF)); lnb_h = din("lnb", (1, HFF))
    Wpr_h = din("Wproj", (D, HFF))
    out_h = nc.dram_tensor("out", [T, D], F32, kind="ExternalOutput").ap()
    xbf_h = nc.dram_tensor("xbf_s", [T, D], BF16, kind="Internal").ap()
    import os as _os
    x1_h = nc.dram_tensor("x1_s", [T, D], BF16,
                          kind="ExternalOutput" if _os.environ.get("KDBG_X1") else "Internal").ap()
    KDUMP = bool(_os.environ.get("KDBG_DUMPS"))
    if KDUMP:
        dkv_h = nc.dram_tensor("d_kv", [128, cfg.NSEQ * 8 * 130], BF16, kind="ExternalOutput").ap()
        dqe_h = nc.dram_tensor("d_qe", [128, 8 * 512], BF16, kind="ExternalOutput").ap()
        dkb_h = nc.dram_tensor("d_kb", [128, 2 * D], BF16, kind="ExternalOutput").ap()
        dvg_h = nc.dram_tensor("d_vg", [128, 2 * NH * 65], BF16, kind="ExternalOutput").ap()
        dcx_h = nc.dram_tensor("d_cx", [128, D], BF16, kind="ExternalOutput").ap()
        dxf_h = nc.dram_tensor("d_xf", [128, 8 * 512], BF16, kind="ExternalOutput").ap()
        dwk_h = nc.dram_tensor("d_wk", [128, DC * D], BF16, kind="ExternalOutput").ap()
        dqh_h = nc.dram_tensor("d_qh", [HFF, D], BF16, kind="ExternalOutput").ap()
        dqp_h = nc.dram_tensor("d_qp", [D, HFF], BF16, kind="ExternalOutput").ap()
        dwf_h = nc.dram_tensor("d_wf", [128, DC * HFF], BF16, kind="ExternalOutput").ap()
        dhh_h = nc.dram_tensor("d_hh", [128, HC * T], BF16, kind="ExternalOutput").ap()
    wfcq_h = nc.dram_tensor("wfcq_s", [HFF, D], BF16, kind="Internal").ap()
    wprq_h = nc.dram_tensor("wprq_s", [D, HFF], BF16, kind="Internal").ap()
    wbf_h = {w: nc.dram_tensor(w + "bf_s", [D, D], BF16, kind="Internal").ap()
             for w in ("Wq", "Wk", "Wv", "Wd")}
    h_sh = nc.dram_tensor("h_s", [128, HC, T], BF16, kind="Internal").ap()
    xfm_h2 = nc.dram_tensor("xfm_s", [128, (T // 512) * 4096], BF16, kind="Internal").ap()

    from contextlib import ExitStack
    es = ExitStack()

    def sb(name, p, f, dt):
        return es.enter_context(nc.sbuf_tensor(name, [p, f], dt))

    def ps(name, f):
        return es.enter_context(nc.psum_tensor(name, [128, f], F32))

    ones_col = sb("ones_col", 128, 1, BF16)
    ones_row = sb("ones_row", 1, 512, BF16)
    onesf_row = sb("onesf_row", 1, 128, F32)
    onesf_col = sb("onesf_col", 128, 1, F32)
    onef = sb("onef", 1, 1, F32)
    brows = sb("brows", 1, 4 * D, BF16)
    gb_bc = sb("gb_bc", 128, 2 * D, BF16)
    c0_bc = sb("c0_bc", 128, D, BF16)
    s2_row = sb("s2_row", 1, D, BF16)
    lnw_cols = sb("lnw_cols", 128, HC, F32)
    lnb_cols = sb("lnb_cols", 128, HC, BF16)
    rff_cols = sb("rff_cols", 128, NC, F32)
    thr_cols = sb("thr_cols", 128, 4, F32)
    accq = sb("accq", 128, 2, F32)
    redq = sb("redq", 128, RT + DC, F32)
    smallf = sb("smallf", 1, 16, F32)
    mv = sb("mv", 128, 8, F32)
    stat = sb("stat", 128, 24, F32)
    rden = sb("rden", 128, 2 * NH, F32)
    epsc = sb("epsc", 128, 2, F32)
    idmat = sb("idmat", 128, 128, BF16)
    idr = sb("idr", 1, 128, F32)
    idc = sb("idc", 128, 1, F32)
    wstg = sb("wstg", 128, 4096, BF16)

    B1 = 8192 + 4096 + 2048 + 2 * NH * 65 + 2048 + 2048 + 2048 + 2048 + cfg.NSEQ * NPAIR * 130 + 2048
    B2 = T + max(2 * DC * GT + HC * GT + 1024, 2 * HC * 256 + 2 * D)
    AR_B = sb("AR_B", 128, max(B1, B2) + 64, BF16)
    cb = Carver(AR_B, max(B1, B2) + 64)
    xfm = cb.take(2 * DC * GT)
    Qe = cb.take(DC * GT)
    kbf = cb.take(2 * D)
    vaug = cb.take(2 * NH * 65)
    ctxbf = cb.take(2 * D)
    ctxfm = cb.take(2 * D)
    wq_bf = cb.take(2 * D)
    wtile = cb.take(2 * 1024)
    kvaug = cb.take(cfg.NSEQ * NPAIR * 130)
    x1t = cb.take(2 * D)
    cb2 = Carver(AR_B, max(B1, B2) + 64)
    negmu_r = cb2.row(T)
    x1fm = cb2.take(2 * DC * GT)
    hgrp = cb2.take(HC * GT)
    hqt = cb2.take(2 * 512)
    cb3 = Carver(AR_B, max(B1, B2) + 64)
    cb3.off = T
    hld = cb3.take(2 * HC * 256)
    x1t_b = cb3.take(2 * D)

    F1 = 1024 + 2048 + 2048 + 1024 + 1024 + 1024 + 1024 + 2048 + 1024 + 1024 + NPAIR * 65
    F2 = T + T + 1024 + 1024 + 2048 + 1024
    AR_F = sb("AR_F", 128, max(F1, F2, 2 * HFF + 2048 + 2 * D) + 32, F32)
    cf = Carver(AR_F, max(F1, F2, 2 * HFF + 2048 + 2 * D) + 32)
    eluu = cf.take(1024)
    elue = cf.take(2048)
    xf32 = cf.take(2 * D)
    xbeta = cf.take(D)
    v0t = cf.take(D)
    nt = cf.take(D)
    v1t = cf.take(D)
    wnat = cf.take(2048)
    qs1 = cf.take(1024)
    qs2 = cf.take(1024)
    raug_f = cf.take(NPAIR * 65)
    lnwrow = AR_F[0:1, 0:HFF]
    lnbrow = AR_F[0:1, HFF:2 * HFF]
    gbrow = AR_F[0:1, 2 * HFF:2 * HFF + 2 * D]
    cf2 = Carver(AR_F, max(F1, F2, 2 * HFF + 2048 + 2 * D) + 32)
    sumh_r = cf2.row(T)
    sumq_r = cf2.row(T)
    x1c = cf2.take(D)
    v2t = cf2.take(D)
    outt = cf2.take(2 * D)
    c0row = cf2.row(D)


    AR_W = sb("AR_W", 128, DC * max(HFF, 4 * D), BF16)
    WkT = AR_W[:, 0 * DC * D:1 * DC * D]
    WvT = AR_W[:, 1 * DC * D:2 * DC * D]
    WqT = AR_W[:, 2 * DC * D:3 * DC * D]
    WdT = AR_W[:, 3 * DC * D:4 * DC * D]
    WfcT = AR_W[:, 0:DC * HFF]
    wT_map = {"Wk": WkT, "Wv": WvT, "Wq": WqT, "Wd": WdT}

    pA = ps("pA", 512)
    pB = ps("pB", 512)
    pC = ps("pC", 512)
    pD = ps("pD", 512)
    pR1 = ps("pR1", 512)
    pR2 = ps("pR2", 512)
    pS1 = ps("pS1", 512)
    pS2 = ps("pS2", 512)

    def t_ms(fn, deps=()):
        return sch.op("T", fn, deps, inc=True)

    def chain_mm(mms, deps=()):
        tok = None
        for i, f in enumerate(mms):
            last = i == len(mms) - 1
            if i == 0:
                tok = sch.op("T", f, deps, inc=last)
            else:
                tok = sch.op("T", f, inc=last)
        return tok

    v_ones = sch.op("V", lambda e: e.memset(ones_col[:, :], 1.0), inc=True)
    sch.op("V", lambda e: e.memset(ones_row[:, :], 1.0))
    sch.op("V", lambda e: e.memset(onesf_row[:, :], 1.0))
    sch.op("V", lambda e: e.memset(onesf_col[:, :], 1.0))
    sch.op("V", lambda e: e.memset(onef[:, :], 1.0))
    sch.op("V", lambda e: e.memset(kvaug[:, :], 0.0))
    sch.op("V", lambda e: e.memset(vaug[:, :], 1.0))
    sch.op("V", lambda e: e.memset(epsc[:, 0:1], 1e-12))
    v_eps = sch.op("V", lambda e: e.memset(epsc[:, 1:2], 1e-5), inc=True)
    v_init = sch.op("V", lambda e: e.memset(accq[:, :], 0.0), inc=True)

    d_b = {}
    for i, b in enumerate(("bq", "bk", "bv", "bd")):
        d_b[b] = sch.op("G", lambda e, i=i, b=b: e.dma_start(
            brows[0:1, D * i:D * (i + 1)], b_attn[b][:, :]), dma=True)
    d_gam = sch.op("S", lambda e: e.dma_start(gbrow[0:1, 0:D], gam_h[:, :]), dma=True)
    d_bet = sch.op("S", lambda e: e.dma_start(gbrow[0:1, D:2 * D], bet_h[:, :]), dma=True)
    d_lnw = sch.op("S", lambda e: e.dma_start(lnwrow, lnw_h[:, :]), dma=True)
    d_lnb = sch.op("S", lambda e: e.dma_start(lnbrow, lnb_h[:, :]), dma=True)

    gb_last = None
    for j in range(2 * D // 512):
        deps = [d_gam, d_bet, v_init] + ([gb_last] if gb_last else [])
        tk = t_ms(lambda e, j=j: e.matmul(pS1[:, 0:512], onesf_row[0:1, :],
                                          gbrow[0:1, 512 * j:512 * (j + 1)],
                                          start=True, stop=True), deps=deps)
        gb_last = sch.op("V", lambda e, j=j: e.tensor_copy(
            gb_bc[:, 512 * j:512 * (j + 1)], pS1[:, 0:512]), deps=[tk], inc=True)
    v_gb = gb_last
    v_sc = sch.op("V", lambda e: e.tensor_tensor_scan(
        idr[0:1, :], onesf_row[0:1, :], onesf_row[0:1, :], 1.0,
        ALU.add, ALU.mult), inc=True)
    t_idc = t_ms(lambda e: e.transpose(pS2[:, 2:3], idr[0:1, :], onef[0:1, 0:1]),
                 deps=[v_sc])
    v_idc = sch.op("V", lambda e: e.tensor_copy(idc[:, 0:1], pS2[:, 2:3]),
                   deps=[t_idc], inc=True)
    t_idb = t_ms(lambda e: e.matmul(pS1[:, 0:128], onesf_row[0:1, :], idr[0:1, :],
                                    start=True, stop=True), deps=[v_sc, v_gb])
    v_id = sch.op("V", lambda e: e.tensor_scalar(
        idmat[:, :], pS1[:, 0:128], idc[:, 0:1], None, ALU.is_equal),
        deps=[t_idb, v_idc], inc=True)

    col_last = None
    for k in range(HC):
        deps = [d_lnw, v_init] + ([col_last] if col_last else [])
        t1 = t_ms(lambda e, k=k: e.transpose(pS2[:, 0:1], lnwrow[0:1, 128 * k:128 * (k + 1)],
                                             onef[0:1, 0:1]), deps=deps)
        c1 = sch.op("V", lambda e, k=k: e.tensor_copy(lnw_cols[:, k:k + 1], pS2[:, 0:1]),
                    deps=[t1], inc=True)
        t2 = t_ms(lambda e, k=k: e.transpose(pS2[:, 1:2], lnbrow[0:1, 128 * k:128 * (k + 1)],
                                             onef[0:1, 0:1]), deps=[d_lnb, c1])
        col_last = sch.op("V", lambda e, k=k: e.tensor_copy(lnb_cols[:, k:k + 1], pS2[:, 1:2]),
                          deps=[t2], inc=True)
    v_cols = col_last

    d_xbf = []
    for g in range(NG):
        d_xbf.append(sch.op("G", lambda e, g=g: e.dma_start(
            xbf_h[GT * g:GT * (g + 1), :], x_h[GT * g:GT * (g + 1), :]), dma=True))

    wq_slot_free = [[], []]
    w_done = {}
    sbt_chain = [None]
    guard_pS = [None]


    w_staged = {}

    def emit_weight_oc(wname, oc, unit_idx):
        slot = unit_idx % 2
        d = sch.op("G", lambda e, wname=wname, oc=oc, slot=slot: e.dma_start(
            wq_bf[:, 1024 * slot:1024 * (slot + 1)],
            W_attn[wname][128 * oc:128 * (oc + 1), :]), dma=True,
            deps=wq_slot_free[slot])
        st = sch.op("G", lambda e, wname=wname, oc=oc, slot=slot: e.dma_start(
            wbf_h[wname][128 * oc:128 * (oc + 1), :],
            wq_bf[:, 1024 * slot:1024 * (slot + 1)]), dma=True, deps=[d])
        wq_slot_free[slot] = [st]
        w_staged.setdefault(wname, []).append(st)

    def emit_weight_loads(wname):
        WT = wT_map[wname]
        toks = []
        for k in range(DC):
            for oc in range(DC):
                q = "S"
                toks.append(sch.op(q, lambda e, WT=WT, k=k, oc=oc, wname=wname: e.dma_start(
                    WT[:, D * k + 128 * oc: D * k + 128 * oc + 128],
                    wbf_h[wname][128 * oc:128 * (oc + 1), 128 * k:128 * (k + 1)],
                    transpose=True), deps=w_staged[wname], dma=True))
        w_done[wname] = toks

    ui = 0
    for oc in range(DC):
        emit_weight_oc("Wk", oc, ui); ui += 1
        emit_weight_oc("Wv", oc, ui); ui += 1
    emit_weight_loads("Wk")
    emit_weight_loads("Wv")

    from collections import deque
    bg = deque()
    for oc in range(DC):
        bg.append(("wq", oc))
    bg.append(("wqload", 0))
    for oc in range(DC):
        bg.append(("wd", oc))
    bg.append(("wdload", 0))

    bgstate = dict(ui=ui, am_last=v_cols, qf_last=[[], []], qp_last=[[], []],
                   wfcq_done=None, wprq_done=None, am_fc_tok=None, am_pr_tok=None,
                   v_thr=None)

    def emit_absmean(i):
        slot = bgstate["ui"] % 2
        bgstate["ui"] += 1
        if i < RT:
            d = sch.op("G", lambda e, i=i, slot=slot: e.dma_start(
                wnat[:, 1024 * slot:1024 * (slot + 1)], Wfc_h[128 * i:128 * (i + 1), :]),
                dma=True, deps=[bgstate["am_last"]])
            v = sch.op("V", lambda e, i=i, slot=slot: e.reduce_sum(
                redq[:, i:i + 1], wnat[:, 1024 * slot:1024 * (slot + 1)], AX.X,
                apply_absolute_value=True), deps=[d], inc=True)
            bgstate["am_last"] = v
        else:
            ii = i - RT
            nps = HFF // 1024
            vs = []
            for pcs in range(nps):
                slot2 = bgstate["ui"] % 2
                bgstate["ui"] += 1
                d = sch.op("G", lambda e, ii=ii, pcs=pcs, slot2=slot2: e.dma_start(
                    wnat[:, 1024 * slot2:1024 * (slot2 + 1)],
                    Wpr_h[128 * ii:128 * (ii + 1), 1024 * pcs:1024 * (pcs + 1)]),
                    dma=True, deps=[bgstate["am_last"]])
                v = sch.op("V", lambda e, ii=ii, pcs=pcs, slot2=slot2: e.reduce_sum(
                    qs1[:, ii * nps + pcs:ii * nps + pcs + 1],
                    wnat[:, 1024 * slot2:1024 * (slot2 + 1)], AX.X,
                    apply_absolute_value=True), deps=[d], inc=True)
                bgstate["am_last"] = v
                vs.append(v)
            v = sch.op("V", lambda e, ii=ii, nps=nps: e.reduce_sum(
                redq[:, RT + ii:RT + ii + 1], qs1[:, ii * nps:(ii + 1) * nps], AX.X),
                deps=vs, inc=True)
            bgstate["am_last"] = v
        if i == RT - 1:
            bgstate["am_fc_tok"] = sch.op("V", lambda e: e.reduce_sum(
                accq[:, 0:1], redq[:, 0:RT], AX.X), deps=[bgstate["am_last"]], inc=True)
        if i == RT + DC - 1:
            bgstate["am_pr_tok"] = sch.op("V", lambda e: e.reduce_sum(
                accq[:, 1:2], redq[:, RT:RT + DC], AX.X), deps=[bgstate["am_last"]], inc=True)

    def emit_thr():
        tg1 = t_ms(lambda e: e.matmul(pS2[0:1, 4:5], accq[:, 0:1], onesf_col[:, 0:1],
                                      start=True, stop=True),
                   deps=[bgstate["am_fc_tok"], v_cols, guard_pS[0]])
        tg2 = t_ms(lambda e: e.matmul(pS2[0:1, 5:6], accq[:, 1:2], onesf_col[:, 0:1],
                                      start=True, stop=True), deps=[bgstate["am_pr_tok"]])
        v1 = sch.op("V", lambda e: e.tensor_scalar(
            smallf[0:1, 0:1], pS2[0:1, 4:5], 0.5 / (HFF * D), 0.5e-6, ALU.mult, ALU.add),
            deps=[tg1], inc=True)
        v2 = sch.op("V", lambda e: e.tensor_scalar(
            smallf[0:1, 1:2], pS2[0:1, 5:6], 0.5 / (HFF * D), 0.5e-6, ALU.mult, ALU.add),
            deps=[tg2], inc=True)
        tb = t_ms(lambda e: e.matmul(pS2[:, 8:10], onesf_row[0:1, :], smallf[0:1, 0:2],
                                     start=True, stop=True), deps=[v1, v2])
        v3 = sch.op("V", lambda e: e.tensor_copy(thr_cols[:, 0:2], pS2[:, 8:10]),
                    deps=[tb], inc=True)
        v4 = sch.op("V", lambda e: e.tensor_scalar(thr_cols[:, 2:4], thr_cols[:, 0:2],
                                                   -1.0, None, ALU.mult), inc=True)
        bgstate["v_thr"] = v4

    def emit_quant(kind, idx):
        slot = idx % 2
        if kind == "qf":
            W_src = Wfc_h[128 * idx:128 * (idx + 1), :]
            dst = wfcq_h[128 * idx:128 * (idx + 1), :]
            thr_i = 0
            lastlist = bgstate["qf_last"]
        else:
            ii, pcs = divmod(idx, HFF // 1024)
            W_src = Wpr_h[128 * ii:128 * (ii + 1), 1024 * pcs:1024 * (pcs + 1)]
            dst = wprq_h[128 * ii:128 * (ii + 1), 1024 * pcs:1024 * (pcs + 1)]
            thr_i = 1
            lastlist = bgstate["qp_last"]
        dep = [lastlist[slot], bgstate["am_last"]]
        d = sch.op("G", lambda e, W_src=W_src, slot=slot: e.dma_start(
            wnat[:, 1024 * slot:1024 * (slot + 1)], W_src), dma=True, deps=dep)
        bgstate.setdefault(kind + "_loads", []).append(d)
        sch.op("V", lambda e, slot=slot: e.tensor_scalar(
            qs1[:, 0:1024], wnat[:, 1024 * slot:1024 * (slot + 1)],
            thr_cols[:, thr_i:thr_i + 1], None, ALU.is_ge),
            deps=[d, bgstate["v_thr"]])
        sch.op("V", lambda e, slot=slot: e.tensor_scalar(
            qs2[:, 0:1024], wnat[:, 1024 * slot:1024 * (slot + 1)],
            thr_cols[:, 2 + thr_i:3 + thr_i], None, ALU.is_le))
        v3 = sch.op("V", lambda e, slot=slot: e.tensor_tensor(
            wq_bf[:, 1024 * slot:1024 * (slot + 1)], qs1[:, 0:1024], qs2[:, 0:1024],
            ALU.subtract), deps=[wq_slot_free[slot]], inc=True)
        st = sch.op("G", lambda e, dst=dst, slot=slot: e.dma_start(
            dst, wq_bf[:, 1024 * slot:1024 * (slot + 1)]), deps=[v3], dma=True)
        lastlist[slot] = [st]
        wq_slot_free[slot] = [st]
        bgstate.setdefault(kind + "_stores", []).append(st)
        return st

    nps_pr = HFF // 1024
    for i in range(RT + DC):
        bg.append(("am", i))
    bg.append(("thr", 0))
    for i in range(RT):
        bg.append(("qf", i))
    for i in range(DC * nps_pr):
        bg.append(("qp", i))

    def drain_bg(n):
        for _ in range(n):
            if not bg:
                return
            kind, i = bg.popleft()
            if kind == "wq":
                emit_weight_oc("Wq", i, bgstate["ui"]); bgstate["ui"] += 1
            elif kind == "wqload":
                emit_weight_loads("Wq")
            elif kind == "wd":
                emit_weight_oc("Wd", i, bgstate["ui"]); bgstate["ui"] += 1
            elif kind == "wdload":
                emit_weight_loads("Wd")
            elif kind == "am":
                emit_absmean(i)
            elif kind == "thr":
                emit_thr()
            elif kind == "qf":
                emit_quant("qf", i)
                if i == RT - 1:
                    bgstate["wfcq_done"] = bgstate["qf_stores"]
            elif kind == "qp":
                emit_quant("qp", i)
                if i == DC * nps_pr - 1:
                    bgstate["wprq_done"] = bgstate["qp_stores"]

    def drain_all_bg():
        while bg:
            drain_bg(1)

    def win_loads(dst, src_h, g, slot, deps):
        toks = []
        for tc in range(CPG):
            row0 = GT * g + 128 * tc
            for k in range(DC):
                q = "S"
                toks.append(sch.op(q, lambda e, dst=dst, src_h=src_h, row0=row0, k=k, tc=tc,
                                   slot=slot: e.dma_start(
                                       dst[:, slot * DC * GT + GT * k + 128 * tc:
                                           slot * DC * GT + GT * k + 128 * tc + 128],
                                       src_h[row0:row0 + 128, 128 * k:128 * (k + 1)],
                                       transpose=True), deps=deps, dma=True))
        return toks

    xfm_st = {}
    xfm_free = [None, None]
    k_ps_free = None
    v_ps_free = None
    elue_free = [None, None]
    kbf_free = [None, None]
    vaug_free = [None, None]
    kv_seq_tok = {}
    last_kv_mm = None
    kv_acc = None

    for g in range(NG):
        seq = g // GPS
        slot = g % 2
        wl = win_loads(xfm, xbf_h, g, slot,
                       [d_xbf[g], v_cols] + ([xfm_free[slot]] if xfm_free[slot] else []))
        st_xfm = sch.op("G", lambda e, g=g, slot=slot: e.dma_start(
            xfm_h2[:, 4096 * g:4096 * (g + 1)],
            xfm[:, slot * DC * GT:slot * DC * GT + 4096]), deps=wl, dma=True)
        xfm_st[g] = st_xfm
        if KDUMP and g == 0:
            dxf_t = sch.op("G", lambda e: e.dma_start(dxf_h[:, :], xfm[:, 0:4096]),
                           deps=wl, dma=True)
            wl = wl + [dxf_t]
        xoff = slot * DC * GT
        for tc in range(CPG):
            c = g * CPG + tc
            cs = c % CPS
            pp = c % 2
            mms = []
            for k in range(DC):
                for half, bank in ((0, pA), (1, pB)):
                    def mk(e, k=k, bank=bank, half=half, tc=tc, xoff=xoff):
                        return e.matmul(bank[:, 0:512],
                                        xfm[:, xoff + GT * k + 128 * tc: xoff + GT * k + 128 * tc + 128],
                                        WkT[:, D * k + 512 * half: D * k + 512 * (half + 1)],
                                        start=(k == 0), stop=False)
                    mms.append(mk)
            for half, bank in ((0, pA), (1, pB)):
                def mkb(e, bank=bank, half=half):
                    return e.matmul(bank[:, 0:512], ones_row[0:1, 0:128],
                                    brows[0:1, D + 512 * half: D + 512 * (half + 1)],
                                    start=False, stop=True)
                mms.append(mkb)
            tok_kmm = chain_mm(mms, deps=[wl, w_done["Wk"],
                                          d_b["bk"], v_ones, k_ps_free])
            mms = []
            for k in range(DC):
                for half, bank in ((0, pC), (1, pD)):
                    def mv_(e, k=k, bank=bank, half=half, tc=tc, xoff=xoff):
                        return e.matmul(bank[:, 0:512],
                                        xfm[:, xoff + GT * k + 128 * tc: xoff + GT * k + 128 * tc + 128],
                                        WvT[:, D * k + 512 * half: D * k + 512 * (half + 1)],
                                        start=(k == 0), stop=False)
                    mms.append(mv_)
            for half, bank in ((0, pC), (1, pD)):
                def mvb(e, bank=bank, half=half):
                    return e.matmul(bank[:, 0:512], ones_row[0:1, 0:128],
                                    brows[0:1, 2 * D + 512 * half: 2 * D + 512 * (half + 1)],
                                    start=False, stop=True)
                mms.append(mvb)
            tok_vmm = chain_mm(mms, deps=[w_done["Wv"], d_b["bv"],
                                          v_ps_free])

            a_u1 = sch.op("A", lambda e: e.activation(
                eluu[:, 0:512], pA[:, 0:512], AF.Relu, scale=-1.0), deps=[tok_kmm])
            a_u2 = sch.op("A", lambda e: e.activation(
                eluu[:, 512:1024], pB[:, 0:512], AF.Relu, scale=-1.0), inc=True)
            sch.op("A", lambda e, pp=pp: e.activation(
                elue[:, D * pp:D * pp + 512], eluu[:, 0:512], AF.Exp, scale=-1.0),
                deps=[elue_free[pp]])
            a_e2 = sch.op("A", lambda e, pp=pp: e.activation(
                elue[:, D * pp + 512:D * pp + 1024], eluu[:, 512:1024], AF.Exp, scale=-1.0),
                inc=True)
            p1_last_a = a_e2
            sch.op("V", lambda e, pp=pp: e.scalar_tensor_tensor(
                kbf[:, D * pp:D * pp + 512], pA[:, 0:512], 0.0,
                elue[:, D * pp:D * pp + 512], ALU.max, ALU.add),
                deps=[a_e2, kbf_free[pp]])
            v_k2 = sch.op("V", lambda e, pp=pp: e.scalar_tensor_tensor(
                kbf[:, D * pp + 512:D * pp + 1024], pB[:, 0:512], 0.0,
                elue[:, D * pp + 512:D * pp + 1024], ALU.max, ALU.add), inc=True)
            k_ps_free = v_k2
            elue_free[pp] = v_k2
            def vev1(e, pp=pp):
                dst = vaug[:, pp * NH * 65:(pp + 1) * NH * 65].rearrange(
                    "p (h c) -> p h c", h=NH)[:, 0:8, 0:64]
                return e.tensor_copy(dst, pC[:, 0:512].rearrange("p (h c) -> p h c", h=8))
            sch.op("V", vev1, deps=[tok_vmm, vaug_free[pp]])
            def vev2(e, pp=pp):
                dst = vaug[:, pp * NH * 65:(pp + 1) * NH * 65].rearrange(
                    "p (h c) -> p h c", h=NH)[:, 8:16, 0:64]
                return e.tensor_copy(dst, pD[:, 0:512].rearrange("p (h c) -> p h c", h=8))
            v_v2 = sch.op("V", vev2, inc=True)
            v_ps_free = v_v2
            if cs == 0:
                kv_acc = sch.op("V", lambda e: e.memset(raug_f[:, :], 0.0),
                                deps=[kv_seq_tok.get(seq - 1)])
            mms = []
            for hh in range(NH):
                j = hh // 2
                half = hh % 2
                bank = pR1 if hh < 8 else pR2
                off = 65 * (j % 4)
                def mkv(e, hh=hh, half=half, bank=bank, off=off, pp=pp):
                    kw = dict(tile_position=(0, 64)) if half == 1 else {}
                    return e.matmul(
                        bank[64 * half:64 * (half + 1), off:off + 65],
                        kbf[:, D * pp + 64 * hh: D * pp + 64 * (hh + 1)],
                        vaug[:, pp * NH * 65 + 65 * hh: pp * NH * 65 + 65 * (hh + 1)],
                        start=True, stop=True, **kw)
                mms.append(mkv)
            kv_deps = [v_k2, v_v2, kv_acc]
            tok_kv = chain_mm(mms, deps=kv_deps)
            sch.op("V", lambda e: e.tensor_tensor(
                raug_f[:, 0:260], raug_f[:, 0:260], pR1[:, 0:260], ALU.add),
                deps=[tok_kv])
            kv_acc = sch.op("V", lambda e: e.tensor_tensor(
                raug_f[:, 260:520], raug_f[:, 260:520], pR2[:, 0:260], ALU.add))
            kbf_free[pp] = tok_kv
            vaug_free[pp] = tok_kv
            last_kv_mm = kv_acc
            if KDUMP and g == 0 and tc == 0:
                dt1 = sch.op("G", lambda e: e.dma_start(dkb_h[:, :], kbf[:, :]),
                             deps=[v_k2], dma=True)
                dt2 = sch.op("G", lambda e: e.dma_start(dvg_h[:, :], vaug[:, :]),
                             deps=[v_v2], dma=True)
                kbf_free[pp] = [tok_kv, dt1]
                vaug_free[pp] = [tok_kv, dt2]
            if tc == CPG - 1:
                xfm_free[slot] = [tok_kv, xfm_st[g]]
        if (g + 1) % GPS == 0:
            def veps(e):
                ap = raug_f[:, :].rearrange("p (j c) -> p j c", j=NPAIR)[:, :, 64:65]
                return e.tensor_scalar(ap, ap, 1e-6, None, ALU.add)
            v_eps_r = sch.op("V", veps, inc=True)
            base = seq * NPAIR * 130
            for j in range(NPAIR):
                sch.op("V", lambda e, j=j, base=base: e.tensor_copy(
                    kvaug[0:64, base + 130 * j: base + 130 * j + 65],
                    raug_f[0:64, 65 * j:65 * (j + 1)]), deps=[v_eps_r])
                tok_cp = sch.op("V", lambda e, j=j, base=base: e.tensor_copy(
                    kvaug[64:128, base + 130 * j + 65: base + 130 * (j + 1)],
                    raug_f[64:128, 65 * j:65 * (j + 1)]), inc=(j == NPAIR - 1))
            kv_seq_tok[seq] = tok_cp
        drain_bg(9)

    if KDUMP:
        sch.op("G", lambda e: e.dma_start(dkv_h[:, :], kvaug[:, :]),
               deps=[kv_seq_tok[cfg.NSEQ - 1]], dma=True)
        sch.op("G", lambda e: e.dma_start(dwk_h[:, :], WkT[:, :]),
               deps=[w_done["Wk"]], dma=True)
    p1_kv_done = kv_seq_tok[cfg.NSEQ - 1]
    p1_T_done = last_kv_mm
    p1_A_done = p1_last_a
    if upto == "p1":
        finish([p1_kv_done, p1_T_done, p1_A_done])
        es.close()
        return nc
    if upto == "setup":
        finish([v_cols, v_gb, p1_kv_done])
        es.close()
        return nc

    xfm_free2 = dict(enumerate(xfm_free))
    qe_free = None
    qpsum_free = v_id
    ctx_ps_free = p1_kv_done
    ctx_ps_free_a = p1_A_done
    xf32_free = [[v_gb, v_cols], [v_gb, v_cols]]
    x1t_free = [None, None]
    nt_free = None
    v0_free = None
    v1_free = None
    ctxbf_free = [[], []]
    ctxfm_free = [None, None]
    ctx_pr = {0: [p1_kv_done], 1: [p1_kv_done]}
    ctx_pr_par = [0]
    rden_free = [None, None]
    mv_free = None
    qelu_free = [None, None]
    x1_store_toks = []

    p2a_stage = 99
    if isinstance(upto, tuple) and upto[0] == "p2a":
        p2a_stage = upto[1]
        upto = "p2a"
    stage_toks = []
    for g in range(NG):
        seq = g // GPS
        slot = g % 2
        wl = [sch.op("A", lambda e, g=g, slot=slot: e.dma_start(
            xfm[:, slot * DC * GT:slot * DC * GT + 4096],
            xfm_h2[:, 4096 * g:4096 * (g + 1)]),
            deps=[xfm_st[g]] + ([xfm_free2.get(slot)] if xfm_free2.get(slot) else []),
            dma=True)]
        xoff = slot * DC * GT
        q_v_toks = []
        for oc in range(DC):
            bank = pS1 if oc % 2 == 0 else pS2
            mms = []
            for k in range(DC):
                def mq(e, oc=oc, k=k, bank=bank, xoff=xoff):
                    return e.matmul(bank[:, 0:512],
                                    WqT[:, D * k + 128 * oc: D * k + 128 * oc + 128],
                                    xfm[:, xoff + GT * k: xoff + GT * (k + 1)],
                                    start=(k == 0), stop=False)
                mms.append(mq)
            def mqb(e, oc=oc, bank=bank):
                return e.matmul(bank[:, 0:512], brows[0:1, 128 * oc:128 * (oc + 1)],
                                ones_row[0:1, 0:512], start=False, stop=True)
            mms.append(mqb)
            qdeps = [wl, w_done["Wq"], d_b["bq"], v_ones,
                     v_gb, v_cols, qpsum_free]
            if len(q_v_toks) >= 2:
                qdeps.append(q_v_toks[-2])
            tok_q = chain_mm(mms, deps=qdeps)
            if p2a_stage < 2:
                stage_toks.append(tok_q)
                qpsum_free = tok_q
                continue
            op_ = oc % 2
            sch.op("A", lambda e, bank=bank, op_=op_: e.activation(
                eluu[:, 512 * op_:512 * (op_ + 1)], bank[:, 0:512], AF.Relu, scale=-1.0),
                deps=[tok_q, p1_kv_done, qelu_free[op_]])
            a2 = sch.op("A", lambda e, op_=op_: e.activation(
                elue[:, 512 * op_:512 * (op_ + 1)], eluu[:, 512 * op_:512 * (op_ + 1)],
                AF.Exp, scale=-1.0), deps=[qelu_free[op_]], inc=True)
            vq = sch.op("V", lambda e, oc=oc, bank=bank, op_=op_: e.scalar_tensor_tensor(
                Qe[:, GT * oc:GT * oc + 512], bank[:, 0:512], 0.0,
                elue[:, 512 * op_:512 * (op_ + 1)], ALU.max, ALU.add),
                deps=[a2, qe_free], inc=True)
            q_v_toks.append(vq)
            qelu_free[op_] = vq
        xfm_free2[slot] = tok_q
        if KDUMP and g == 0:
            dqe_t = sch.op("G", lambda e: e.dma_start(dqe_h[:, :], Qe[:, 0:4096]),
                           deps=q_v_toks, dma=True)
            qe_free = [dqe_t, qe_free]

        if p2a_stage < 3:
            continue
        for tc in range(CPG):
            c = g * CPG + tc
            pp = c % 2
            d_x32 = sch.op("A", lambda e, c=c, pp=pp: e.dma_start(
                xf32[:, D * pp:D * (pp + 1)], x_h[128 * c:128 * (c + 1), :]),
                dma=True, deps=[xf32_free[pp]])
            v_xb = sch.op("V", lambda e, pp=pp: e.tensor_tensor(
                xbeta[:, 0:D], xf32[:, D * pp:D * (pp + 1)],
                gb_bc[:, D:2 * D], ALU.add), deps=[d_x32, v_gb], inc=True)
            mms = []
            for j in range(NPAIR):
                bank = (pA, pB, pC)[j // 3]
                off = 130 * (j % 3)
                base = seq * NPAIR * 130
                def mc(e, j=j, bank=bank, off=off, base=base, tc=tc):
                    return e.matmul(bank[:, off:off + 130],
                                    Qe[:, GT * j + 128 * tc: GT * j + 128 * tc + 128],
                                    kvaug[:, base + 130 * j: base + 130 * (j + 1)],
                                    start=True, stop=True)
                mms.append(mc)
            cdeps = list(q_v_toks) + [p1_kv_done, ctx_ps_free, ctx_ps_free_a]
            tok_ctx = chain_mm(mms, deps=cdeps)
            if tc == CPG - 1:
                qe_free = tok_ctx
            if p2a_stage < 4:
                stage_toks.append(tok_ctx)
                ctx_ps_free = tok_ctx
                ctx_ps_free_a = None
                continue
            for j in range(NPAIR):
                bank = (pA, pB, pC)[j // 3]
                off = 130 * (j % 3)
                def vr(e, j=j, bank=bank, off=off, pp=pp):
                    src = bank[:, off:off + 130].rearrange(
                        "p (a b) -> p a b", a=2)[:, :, 64:65].squeeze(2)
                    return e.reciprocal(rden[:, pp * NH + 2 * j: pp * NH + 2 * j + 2], src)
                v_rc = sch.op("V", vr, deps=[tok_ctx, rden_free[pp]],
                              inc=(j == NPAIR - 1))
            if p2a_stage < 5:
                stage_toks.append(v_rc)
                ctx_ps_free = v_rc
                ctx_ps_free_a = None
                rden_free[pp] = v_rc
                continue
            for j in range(NPAIR):
                bank = (pA, pB, pC)[j // 3]
                off = 130 * (j % 3)
                for half in range(2):
                    hh = 2 * j + half
                    v_ctxe = sch.op("V", lambda e, bank=bank, off=off, half=half,
                                    hh=hh, pp=pp: e.tensor_scalar(
                                        ctxbf[:, D * pp + 64 * hh: D * pp + 64 * (hh + 1)],
                                        bank[:, off + 65 * half: off + 65 * half + 64],
                                        rden[:, pp * NH + hh: pp * NH + hh + 1],
                                        None, ALU.mult),
                                    deps=[v_rc] + ctxbf_free[pp], inc=True)
            a_ctxe = v_ctxe
            if KDUMP and c == 0:
                dcx_t = sch.op("G", lambda e: e.dma_start(dcx_h[:, :], ctxbf[:, 0:D]),
                               deps=[v_ctxe], dma=True)
                v_ctxe = [v_ctxe, dcx_t]
                a_ctxe = v_ctxe
            ctx_ps_free, ctx_ps_free_a = v_ctxe, a_ctxe
            rden_free[pp] = v_ctxe
            if p2a_stage < 6:
                stage_toks.append(v_ctxe)
                stage_toks.append(a_ctxe)
                ctxbf_free[pp] = [v_ctxe, a_ctxe]
                continue
            ct_toks = []
            for k in range(DC):
                p = ctx_pr_par[0]
                bank = pR1 if p == 0 else pR2
                t = sch.op("T", lambda e, k=k, pp=pp, bank=bank: e.matmul(
                    bank[:, 0:128], ctxbf[:, D * pp + 128 * k: D * pp + 128 * (k + 1)],
                    idmat[:, :], start=True, stop=True),
                    deps=[v_id, v_ctxe, a_ctxe] + ctx_pr[p], inc=True)
                v = sch.op("V", lambda e, k=k, pp=pp, bank=bank: e.tensor_copy(
                    ctxfm[:, D * pp + 128 * k: D * pp + 128 * (k + 1)], bank[:, 0:128]),
                    deps=[t, ctxfm_free[pp]], inc=True)
                ctx_pr[p] = [v]
                ctx_pr_par[0] ^= 1
                ct_toks.append(v)
            ctxbf_free[pp] = ct_toks
            if p2a_stage < 7:
                stage_toks.extend(ct_toks)
                continue
            mms = []
            for k in range(DC):
                for half, bank in ((0, pS1), (1, pS2)):
                    def mh(e, k=k, half=half, bank=bank, pp=pp):
                        return e.matmul(bank[:, 0:512],
                                        ctxfm[:, D * pp + 128 * k: D * pp + 128 * (k + 1)],
                                        WdT[:, D * k + 512 * half: D * k + 512 * (half + 1)],
                                        start=(k == 0), stop=False)
                    mms.append(mh)
            for half, bank in ((0, pS1), (1, pS2)):
                def mhb(e, half=half, bank=bank):
                    return e.matmul(bank[:, 0:512], ones_row[0:1, 0:128],
                                    brows[0:1, 3 * D + 512 * half: 3 * D + 512 * (half + 1)],
                                    start=False, stop=True)
                mms.append(mhb)
            hdeps = [ct_toks, w_done["Wd"], d_b["bd"],
                     qpsum_free] + q_v_toks
            tok_hs = chain_mm(mms, deps=hdeps)
            ctxfm_free[pp] = tok_hs
            if p2a_stage < 8:
                stage_toks.append(tok_hs)
                qpsum_free = tok_hs
                guard_pS[0] = tok_hs
                continue
            v_v0a = sch.op("V", lambda e, pp=pp: e.tensor_tensor(
                v0t[:, 0:512], pS1[:, 0:512], xf32[:, D * pp:D * pp + 512], ALU.add),
                deps=[tok_hs, d_x32, v0_free])
            v_v0b = sch.op("V", lambda e, pp=pp: e.tensor_tensor(
                v0t[:, 512:1024], pS2[:, 0:512], xf32[:, D * pp + 512:D * pp + 1024],
                ALU.add), inc=True)
            qpsum_free = v_v0b
            guard_pS[0] = v_v0b
            sch.op("V", lambda e: e.bn_stats(stat[:, 0:6], v0t[:, 0:512]), deps=[mv_free])
            sch.op("V", lambda e: e.bn_stats(stat[:, 6:12], v0t[:, 512:1024]))
            v_ag = sch.op("V", lambda e: e.bn_aggr(mv[:, 0:2], stat[:, 0:12]), inc=True)
            sch.op("A", lambda e: e.activation(mv[:, 2:3], mv[:, 1:2], AF.Ln, bias=epsc[:, 0:1]),
                   deps=[v_ag, v_eps])
            a_r = sch.op("A", lambda e: e.activation(mv[:, 3:4], mv[:, 2:3], AF.Exp,
                                                     scale=-0.5), inc=True)
            a_n = sch.op("V", lambda e: e.tensor_scalar(
                nt[:, 0:D], v0t[:, 0:D], mv[:, 0:1], mv[:, 3:4], ALU.subtract, ALU.mult),
                deps=[a_r, nt_free], inc=True)
            v0_free = a_n
            sch.op("V", lambda e: e.tensor_tensor(nt[:, 0:D], nt[:, 0:D], gb_bc[:, 0:D],
                                                  ALU.mult), deps=[a_n])
            v_v1b = sch.op("V", lambda e: e.tensor_tensor(
                v1t[:, 0:D], nt[:, 0:D], xbeta[:, 0:D], ALU.add),
                deps=[v_xb, v1_free], inc=True)
            nt_free = v_v1b
            xf32_free[pp] = v_v1b
            sch.op("V", lambda e: e.bn_stats(stat[:, 12:18], v1t[:, 0:512]))
            sch.op("V", lambda e: e.bn_stats(stat[:, 18:24], v1t[:, 512:1024]))
            v_ag1 = sch.op("V", lambda e: e.bn_aggr(mv[:, 6:8], stat[:, 12:24]), inc=True)
            sch.op("A", lambda e: e.activation(mv[:, 2:3], mv[:, 7:8], AF.Ln, bias=epsc[:, 1:2]),
                   deps=[v_ag1, v_eps])
            a_r1 = sch.op("A", lambda e: e.activation(mv[:, 3:4], mv[:, 2:3], AF.Exp,
                                                      scale=-0.5), inc=True)
            a_x1 = sch.op("V", lambda e, pp=pp: e.tensor_scalar(
                x1t[:, D * pp:D * (pp + 1)], v1t[:, 0:D], mv[:, 6:7], mv[:, 3:4],
                ALU.subtract, ALU.mult), deps=[a_r1, x1t_free[pp]], inc=True)
            mv_free = a_x1
            v1_free = a_x1
            st = sch.op("G", lambda e, c=c, pp=pp: e.dma_start(
                x1_h[128 * c:128 * (c + 1), :], x1t[:, D * pp:D * (pp + 1)]),
                deps=[a_x1], dma=True)
            x1t_free[pp] = st
            x1_store_toks.append(st)
        drain_bg(9)

    drain_all_bg()
    if p2a_stage < 99:
        p2a_T_done = p2a_V_done = p2a_A_done = None
    else:
        p2a_T_done = tok_hs
        p2a_V_done = a_x1
        p2a_A_done = a_r1
    if upto == "p2a":
        if p2a_stage < 99:
            finish([t for t in stage_toks[-8:] if t] +
                   [bgstate["wfcq_done"], bgstate["wprq_done"]])
            es.close()
            return nc
        finish([p2a_T_done, p2a_V_done, p2a_A_done, x1_store_toks[-1],
                bgstate["wfcq_done"], bgstate["wprq_done"]])
        es.close()
        return nc

    stg_free = [None, None, None, None]
    ps_tr = {0: [qpsum_free], 1: [qpsum_free]}
    ps_par = [0]

    def pe_transpose(dst_slice, src_slice, deps_mm, deps_evac):
        p = ps_par[0]
        bank = pS1 if p == 0 else pS2
        t = sch.op("T", lambda e, bank=bank, src_slice=src_slice: e.matmul(
            bank[:, 0:128], src_slice, idmat[:, :], start=True, stop=True),
            deps=[v_id] + deps_mm + ps_tr[p], inc=True)
        v = sch.op("V", lambda e, bank=bank, dst_slice=dst_slice: e.tensor_copy(
            dst_slice, bank[:, 0:128]), deps=[t] + deps_evac, inc=True)
        ps_tr[p] = [v]
        ps_par[0] ^= 1
        return t, v

    wfcT_by_hc = {}
    wfcT_loads = []
    for hc in range(HC):
        s = hc % 4
        ld = sch.op("A", lambda e, hc=hc, s=s: e.dma_start(
            wstg[:, 1024 * s:1024 * (s + 1)], wfcq_h[128 * hc:128 * (hc + 1), :]),
            deps=[bgstate["wfcq_done"], stg_free[s]], dma=True)
        toks = []
        t_last = None
        for k in range(DC):
            if k < 4:
                rdep = [p1_T_done]
            elif k < 6:
                rdep = [qpsum_free]
            else:
                rdep = [p2a_T_done]
            t_last, v = pe_transpose(
                WfcT[:, HFF * k + 128 * hc:HFF * k + 128 * hc + 128],
                wstg[:, 1024 * s + 128 * k:1024 * s + 128 * k + 128],
                [ld], rdep)
            toks.append(v)
        stg_free[s] = t_last
        wfcT_by_hc[hc] = toks
        wfcT_loads.extend(toks)

    if KDUMP:
        sch.op("G", lambda e: e.dma_start(dqh_h[:, :], wfcq_h[:, :]),
               deps=[bgstate["wfcq_done"]], dma=True)
        sch.op("G", lambda e: e.dma_start(dqp_h[:, :], wprq_h[:, :]),
               deps=[bgstate["wprq_done"]], dma=True)
        sch.op("G", lambda e: e.dma_start(dwf_h[:, :], WfcT[:, 0:DC * HFF]),
               deps=wfcT_loads, dma=True)

    x1fm_free = [None, None]
    fc_ps_free = [[ctx_ps_free, ctx_ps_free_a] for _ in range(4)]
    hqt_free = [None, None]
    stat_ps_free = ctx_pr[0] + ctx_pr[1]
    hgrp_free = None
    hst_toks = []
    lastrow = {}

    for g in range(NG):
        slot = g % 2
        wl = []
        for tc in range(CPG):
            c = g * CPG + tc
            s = tc
            ld = sch.op("A", lambda e, c=c, s=s: e.dma_start(
                wstg[:, 1024 * s:1024 * (s + 1)], x1_h[128 * c:128 * (c + 1), :]),
                deps=[x1_store_toks[c], stg_free[s]], dma=True)
            t_last = None
            for k in range(DC):
                edep = [x1fm_free[slot]] if x1fm_free[slot] else []
                t_last, v = pe_transpose(
                    x1fm[:, slot * DC * GT + GT * k + 128 * tc:
                         slot * DC * GT + GT * k + 128 * tc + 128],
                    wstg[:, 1024 * s + 128 * k:1024 * s + 128 * k + 128],
                    [ld], edep)
                wl.append(v)
            stg_free[s] = t_last
        xoff = slot * DC * GT
        lastst = None
        a_gelu_last = None
        first_extra = [hgrp_free] if hgrp_free else []
        for hc in range(HC):
            bank = (pA, pB, pC, pD)[hc % 4]
            hcp = hc % 4
            mms = []
            for k in range(DC):
                def mf(e, hc=hc, k=k, bank=bank, xoff=xoff):
                    return e.matmul(bank[:, 0:512],
                                    WfcT[:, HFF * k + 128 * hc: HFF * k + 128 * hc + 128],
                                    x1fm[:, xoff + GT * k: xoff + GT * (k + 1)],
                                    start=(k == 0), stop=(k == DC - 1))
                mms.append(mf)
            fdeps = [wl, wfcT_by_hc[hc], fc_ps_free[hcp]]
            tok_fc = chain_mm(mms, deps=fdeps)
            a_g = sch.op("A", lambda e, hc=hc, bank=bank: e.activation(
                hgrp[:, GT * hc:GT * (hc + 1)], bank[:, 0:512], AF.Gelu),
                deps=[tok_fc] + (first_extra if hc == 0 else []), inc=True)
            first_extra = []
            fc_ps_free[hcp] = [a_g]
            hq2 = hc % 2
            a_q = sch.op("A", lambda e, hc=hc, hq2=hq2: e.activation(
                hqt[:, 512 * hq2:512 * (hq2 + 1)], hgrp[:, GT * hc:GT * (hc + 1)],
                AF.Square), deps=[hqt_free[hq2]], inc=True)
            a_gelu_last = a_q
            def ms1(e, hc=hc):
                return e.matmul(pR1[0:1, 0:512], ones_col[:, 0:1],
                                hgrp[:, GT * hc:GT * (hc + 1)],
                                start=(hc == 0), stop=(hc == HC - 1))
            def ms2(e, hc=hc, hq2=hq2):
                return e.matmul(pR2[0:1, 0:512], ones_col[:, 0:1],
                                hqt[:, 512 * hq2:512 * (hq2 + 1)],
                                start=(hc == 0), stop=(hc == HC - 1))
            sdeps = [a_g, a_q, v_ones]
            if hc == 0:
                sdeps.append(stat_ps_free)
            sch.op("T", ms1, deps=sdeps)
            t2 = sch.op("T", ms2, inc=(hc == HC - 1))
            hqt_free[hq2] = t2
            if hc == HC - 1:
                lastst = t2
        sch.op("V", lambda e, g=g: e.tensor_copy(
            sumh_r[0:1, GT * g:GT * (g + 1)], pR1[0:1, 0:512]), deps=[lastst])
        v_sr2 = sch.op("V", lambda e, g=g: e.tensor_copy(
            sumq_r[0:1, GT * g:GT * (g + 1)], pR2[0:1, 0:512]), inc=True)
        lastrow[g] = v_sr2
        stat_ps_free = v_sr2
        sth = sch.op("G", lambda e, g=g: e.dma_start(
            h_sh[:, :, GT * g:GT * (g + 1)],
            hgrp[:, :].rearrange("p (h t) -> p h t", h=HC)),
            deps=[lastst, a_gelu_last], dma=True)
        hst_toks.append(sth)
        hgrp_free = sth
        x1fm_free[slot] = tok_fc

    if KDUMP:
        sch.op("G", lambda e: e.dma_start(
            dhh_h[:, :], h_sh[:, :, :].rearrange("p h t -> p (h t)")),
            deps=hst_toks, dma=True)

    if upto == "p2bi":
        finish([lastrow[NG - 1], hst_toks[-1]] + sum(fc_ps_free, []))
        es.close()
        return nc

    W2T = WfcT
    c0_last = None
    s2_last = None
    w2_scale_last = None
    w2dep = [bgstate["wprq_done"], hst_toks[-1]] + sum(fc_ps_free, [])
    w2_by_k = {k: [] for k in range(HC)}
    for o in range(DC):
        for qtr in range(HFF // 1024):
            s = (o * (HFF // 1024) + qtr) % 4
            ld = sch.op("A", lambda e, o=o, qtr=qtr, s=s: e.dma_start(
                wstg[:, 1024 * s:1024 * (s + 1)],
                wprq_h[128 * o:128 * (o + 1), 1024 * qtr:1024 * (qtr + 1)]),
                deps=[bgstate["wprq_done"], stg_free[s]], dma=True)
            t_last = None
            for kk in range(8):
                k = 8 * qtr + kk
                t_last, v = pe_transpose(
                    W2T[:, D * k + 128 * o:D * k + 128 * (o + 1)],
                    wstg[:, 1024 * s + 128 * kk:1024 * s + 128 * kk + 128],
                    [ld], w2dep)
                w2_by_k[k].append(v)
            stg_free[s] = t_last
    for k in range(HC):
        d = w2_by_k[k]
        sch.op("T", lambda e, k=k: e.matmul(
            pR1[0:1, 0:512], lnb_cols[:, k:k + 1], W2T[:, D * k:D * k + 512],
            start=(k == 0), stop=(k == HC - 1)),
            deps=[d, v_cols, lastrow[NG - 1]])
        c0_last = sch.op("T", lambda e, k=k: e.matmul(
            pR2[0:1, 0:512], lnb_cols[:, k:k + 1], W2T[:, D * k + 512:D * (k + 1)],
            start=(k == 0), stop=(k == HC - 1)), inc=True)
        vsc = sch.op("V", lambda e, k=k: e.tensor_scalar(
            W2T[:, D * k:D * (k + 1)], W2T[:, D * k:D * (k + 1)],
            lnw_cols[:, k:k + 1], None, ALU.mult), deps=[c0_last], inc=True)
        w2_scale_last = vsc
        sch.op("T", lambda e, k=k: e.matmul(
            pC[0:1, 0:512], ones_col[:, 0:1], W2T[:, D * k:D * k + 512],
            start=(k == 0), stop=(k == HC - 1)), deps=[vsc])
        s2_last = sch.op("T", lambda e, k=k: e.matmul(
            pD[0:1, 0:512], ones_col[:, 0:1], W2T[:, D * k + 512:D * (k + 1)],
            start=(k == 0), stop=(k == HC - 1)), inc=True)
    w2_ready = w2_scale_last

    v_c0a = sch.op("V", lambda e: e.tensor_copy(c0row[0:1, 0:512], pR1[0:1, 0:512]),
                   deps=[c0_last, s2_last])
    v_c0b = sch.op("V", lambda e: e.tensor_copy(c0row[0:1, 512:1024], pR2[0:1, 0:512]),
                   inc=True)
    tb1 = t_ms(lambda e: e.matmul(pS1[:, 0:512], onesf_row[0:1, :], c0row[0:1, 0:512],
                                  start=True, stop=True), deps=[v_c0b, qpsum_free])
    v_cb1 = sch.op("V", lambda e: e.tensor_copy(c0_bc[:, 0:512], pS1[:, 0:512]),
                   deps=[tb1], inc=True)
    tb2 = t_ms(lambda e: e.matmul(pS1[:, 0:512], onesf_row[0:1, :], c0row[0:1, 512:1024],
                                  start=True, stop=True), deps=[v_cb1])
    v_cb2 = sch.op("V", lambda e: e.tensor_copy(c0_bc[:, 512:1024], pS1[:, 0:512]),
                   deps=[tb2], inc=True)
    v_s2a = sch.op("V", lambda e: e.tensor_copy(s2_row[0:1, 0:512], pC[0:1, 0:512]),
                   deps=[s2_last])
    v_s2b = sch.op("V", lambda e: e.tensor_copy(s2_row[0:1, 512:1024], pD[0:1, 0:512]),
                   inc=True)

    row_v = []
    row_a = []
    for pcs in range((T + 1023) // 1024):
        lo, hi = 1024 * pcs, min(T, 1024 * (pcs + 1))
        sch.op("V", lambda e, lo=lo, hi=hi: e.tensor_scalar(
            sumh_r[0:1, lo:hi], sumh_r[0:1, lo:hi], 1.0 / HFF, None, ALU.mult),
            deps=[lastrow[NG - 1]])
        v2 = sch.op("V", lambda e, lo=lo, hi=hi: e.tensor_scalar(
            negmu_r[0:1, lo:hi], sumh_r[0:1, lo:hi], -1.0, None, ALU.mult), inc=True)
        sch.op("V", lambda e, lo=lo, hi=hi: e.tensor_scalar(
            sumq_r[0:1, lo:hi], sumq_r[0:1, lo:hi], 1.0 / HFF, None, ALU.mult))
        sch.op("V", lambda e, lo=lo, hi=hi: e.tensor_tensor(
            c0row[0:1, 0:hi - lo], sumh_r[0:1, lo:hi], sumh_r[0:1, lo:hi], ALU.mult),
            deps=[v_cb2])
        v4 = sch.op("V", lambda e, lo=lo, hi=hi: e.tensor_tensor(
            sumq_r[0:1, lo:hi], sumq_r[0:1, lo:hi], c0row[0:1, 0:hi - lo], ALU.subtract),
            inc=True)
        row_v.append((v2, v4))

    rcol_last = None
    rows_v_done = row_v[-1][1]
    for pcs in range((T + 1023) // 1024):
        lo, hi = 1024 * pcs, min(T, 1024 * (pcs + 1))
        a1 = sch.op("A", lambda e, lo=lo, hi=hi: e.activation(
            sumq_r[0:1, lo:hi], sumq_r[0:1, lo:hi], AF.Ln, bias=epsc[0:1, 1:2]), deps=[rows_v_done, v_eps])
        a2 = sch.op("A", lambda e, lo=lo, hi=hi: e.activation(
            sumq_r[0:1, lo:hi], sumq_r[0:1, lo:hi], AF.Exp, scale=-0.5), inc=True)
        row_a.append(a2)
    rows_a_done = row_a[-1]

    for ci in range(NC):
        dep = [rows_a_done] + ([rcol_last] if rcol_last else [])
        tkt = t_ms(lambda e, ci=ci: e.transpose(pS2[:, 2:3],
                                                sumq_r[0:1, 128 * ci:128 * (ci + 1)],
                                                onef[0:1, 0:1]), deps=dep)
        rcol_last = sch.op("V", lambda e, ci=ci: e.tensor_copy(
            rff_cols[:, ci:ci + 1], pS2[:, 2:3]), deps=[tkt], inc=True)
    v_rcols = rcol_last

    NHG = T // 256
    hld_free = [None, None]
    x1tb_free = [None, None]
    x1c_free = None
    v2_free = None
    out_free = [None, None]
    proj_ps_free = [s2_last, s2_last]
    mv2_free = None
    out_toks = []

    for hgi in range(NHG):
        slot = hgi % 2
        qh = "A" if hgi % 2 == 0 else "G"
        d_h = sch.op(qh, lambda e, hgi=hgi, slot=slot: e.dma_start(
            hld[:, slot * HC * 256:(slot + 1) * HC * 256].rearrange("p (h t) -> p h t", h=HC),
            h_sh[:, :, 256 * hgi:256 * (hgi + 1)]),
            deps=[hst_toks[hgi // 2], hld_free[slot]], dma=True)
        for t2_ in range(2):
            c = 2 * hgi + t2_
            pp = c % 2
            bank1, bank2 = (pA, pB) if pp == 0 else (pC, pD)
            d_x1 = sch.op("A", lambda e, c=c, pp=pp: e.dma_start(
                x1t_b[:, D * pp:D * (pp + 1)], x1_h[128 * c:128 * (c + 1), :]),
                deps=[x1_store_toks[c], x1tb_free[pp]], dma=True)
            v_x1c = sch.op("V", lambda e, pp=pp: e.tensor_tensor(
                x1c[:, 0:D], x1t_b[:, D * pp:D * (pp + 1)], c0_bc[:, :], ALU.add),
                deps=[d_x1, v_cb2, x1c_free], inc=True)
            x1tb_free[pp] = v_x1c
            mms = []
            for k in range(HC):
                for half, bank in ((0, bank1), (1, bank2)):
                    def mp(e, k=k, half=half, bank=bank, slot=slot, t2_=t2_):
                        return e.matmul(
                            bank[:, 0:512],
                            hld[:, slot * HC * 256 + 256 * k + 128 * t2_:
                                 slot * HC * 256 + 256 * k + 128 * t2_ + 128],
                            W2T[:, D * k + 512 * half:D * k + 512 * (half + 1)],
                            start=(k == 0), stop=False)
                    mms.append(mp)
            for half, bank in ((0, bank1), (1, bank2)):
                def mpf(e, half=half, bank=bank, c=c):
                    return e.matmul(bank[:, 0:512],
                                    negmu_r[0:1, 128 * c:128 * (c + 1)],
                                    s2_row[0:1, 512 * half:512 * (half + 1)],
                                    start=False, stop=True)
                mms.append(mpf)
            pdeps = [d_h, v_s2b, rows_v_done, w2_ready, proj_ps_free[pp]]
            tok_p = chain_mm(mms, deps=pdeps)
            if t2_ == 1:
                hld_free[slot] = tok_p
            v_ya = sch.op("V", lambda e, c=c, bank1=bank1: e.scalar_tensor_tensor(
                v2t[:, 0:512], bank1[:, 0:512], rff_cols[:, c:c + 1],
                x1c[:, 0:512], ALU.mult, ALU.add),
                deps=[tok_p, v_rcols, v_x1c, v2_free])
            v_yb = sch.op("V", lambda e, c=c, bank2=bank2: e.scalar_tensor_tensor(
                v2t[:, 512:1024], bank2[:, 0:512], rff_cols[:, c:c + 1],
                x1c[:, 512:1024], ALU.mult, ALU.add), inc=True)
            proj_ps_free[pp] = v_yb
            x1c_free = v_yb
            sch.op("V", lambda e: e.bn_stats(stat[:, 0:6], v2t[:, 0:512]), deps=[mv2_free])
            sch.op("V", lambda e: e.bn_stats(stat[:, 6:12], v2t[:, 512:1024]))
            v_ag2 = sch.op("V", lambda e: e.bn_aggr(mv[:, 0:2], stat[:, 0:12]), inc=True)
            sch.op("A", lambda e: e.activation(mv[:, 2:3], mv[:, 1:2], AF.Ln, bias=epsc[:, 1:2]),
                   deps=[v_ag2, v_eps])
            a_r2 = sch.op("A", lambda e: e.activation(mv[:, 3:4], mv[:, 2:3], AF.Exp,
                                                      scale=-0.5), inc=True)
            a_o = sch.op("V", lambda e, pp=pp: e.tensor_scalar(
                outt[:, D * pp:D * (pp + 1)], v2t[:, 0:D], mv[:, 0:1], mv[:, 3:4],
                ALU.subtract, ALU.mult), deps=[a_r2, out_free[pp]], inc=True)
            mv2_free = a_o
            v2_free = a_o
            sto = sch.op("G", lambda e, c=c, pp=pp: e.dma_start(
                out_h[128 * c:128 * (c + 1), :], outt[:, D * pp:D * (pp + 1)]),
                deps=[a_o], dma=True)
            out_free[pp] = sto
            out_toks.append(sto)

    sch.op("S", lambda e: e.nop(), deps=out_toks)
    sch.build(nc)
    es.close()
    return nc


_CACHE = {}


def _get_program(cfg=None):
    key = "full" if cfg is None else (cfg.S, cfg.NSEQ, cfg.HFF)
    if key not in _CACHE:
        _CACHE[key] = build_program(cfg or Cfg())
    return _CACHE[key]


def kernel(L0_attn_input_tensor, L0_attn_attention_mask, L0_attn_Wq, L0_attn_bq,
           L0_attn_Wk, L0_attn_bk, L0_attn_Wv, L0_attn_bv, L0_attn_Wd, L0_attn_bd,
           L0_attn_gamma, L0_attn_beta, L0_ffn_W_fc, L0_ffn_ln_w, L0_ffn_ln_b,
           L0_ffn_W_proj):
    x = np.ascontiguousarray(np.asarray(L0_attn_input_tensor, dtype=np.float32))
    B, S, D = x.shape
    nc = _get_program()
    w = dict(
        Wq=np.ascontiguousarray(L0_attn_Wq, dtype=np.float32),
        Wk=np.ascontiguousarray(L0_attn_Wk, dtype=np.float32),
        Wv=np.ascontiguousarray(L0_attn_Wv, dtype=np.float32),
        Wd=np.ascontiguousarray(L0_attn_Wd, dtype=np.float32),
        bq=np.ascontiguousarray(np.asarray(L0_attn_bq, dtype=np.float32).reshape(1, D)),
        bk=np.ascontiguousarray(np.asarray(L0_attn_bk, dtype=np.float32).reshape(1, D)),
        bv=np.ascontiguousarray(np.asarray(L0_attn_bv, dtype=np.float32).reshape(1, D)),
        bd=np.ascontiguousarray(np.asarray(L0_attn_bd, dtype=np.float32).reshape(1, D)),
        gamma=np.ascontiguousarray(np.asarray(L0_attn_gamma, dtype=np.float32).reshape(1, D)),
        beta=np.ascontiguousarray(np.asarray(L0_attn_beta, dtype=np.float32).reshape(1, D)),
        Wfc=np.ascontiguousarray(L0_ffn_W_fc, dtype=np.float32),
        lnw=np.ascontiguousarray(np.asarray(L0_ffn_ln_w, dtype=np.float32).reshape(1, -1)),
        lnb=np.ascontiguousarray(np.asarray(L0_ffn_ln_b, dtype=np.float32).reshape(1, -1)),
        Wproj=np.ascontiguousarray(L0_ffn_W_proj, dtype=np.float32),
    )
    per = B // N_CORES
    in_maps = []
    for c in range(N_CORES):
        m = dict(w)
        m["x"] = np.ascontiguousarray(x[per * c:per * (c + 1)].reshape(per * S, D))
        in_maps.append(m)
    res = run_bass_kernel_spmd(nc, in_maps, core_ids=list(range(N_CORES)))
    outs = [np.asarray(r["out"]).reshape(per, S, D) for r in res.results]
    return np.concatenate(outs, axis=0).astype(np.float32)
